# version 14
# speedup vs baseline: 1.7009x; 1.1372x over previous
import contextlib
import numpy as np
import ml_dtypes
import concourse.bass as bass
import concourse.mybir as mybir
from concourse.bass_utils import run_bass_kernel_spmd
import concourse.bass_utils as _bu

if False and not getattr(_bu, "_ldw_patched", False):
    _orig_run_command = _bu.run_command

    def _run_command_ldw(argv, **kwargs):
        if isinstance(argv, list):
            argv = ["--enable-ldw-opt=true" if a == "--enable-ldw-opt=false" else a
                    for a in argv]
        return _orig_run_command(argv, **kwargs)

    _bu.run_command = _run_command_ldw
    _bu._ldw_patched = True

F32 = mybir.dt.float32
BF16 = mybir.dt.bfloat16
AF = mybir.ActivationFunctionType
ALU = mybir.AluOpType
AX = mybir.AxisListType

D = 1024
T = 8192
B = 4
OWN = 4096
EPS = 1e-5
NC = 8
NKB = 64
NQS = 8


def _nc():
    return bass.Bass("TRN2", target_bir_lowering=False)


def build_p1a():
    nc = _nc()
    x_fm = nc.dram_tensor("x_fm", [D, T], F32, kind="ExternalInput")
    wqT = nc.dram_tensor("wqT", [D, 128], F32, kind="ExternalInput")
    wkT = nc.dram_tensor("wkT", [D, 128], F32, kind="ExternalInput")
    wvT = nc.dram_tensor("wvT", [D, D], F32, kind="ExternalInput")
    bq = nc.dram_tensor("bq", [128, 1], F32, kind="ExternalInput")
    bk = nc.dram_tensor("bk", [128, 1], F32, kind="ExternalInput")
    k_out = nc.dram_tensor("k_out", [128, T], BF16, kind="ExternalOutput")
    q_out = nc.dram_tensor("q_out", [128, T], BF16, kind="ExternalOutput")
    v_out = nc.dram_tensor("v_out", [128, 64 * 1032], BF16, kind="ExternalOutput")

    NT = 16

    with contextlib.ExitStack() as _st:
        _e = _st.enter_context
        wq_sb = _e(nc.sbuf_tensor("wq_sb", [128, 8 * 128], BF16))
        wk_sb = _e(nc.sbuf_tensor("wk_sb", [128, 8 * 128], BF16))
        wv_sb = _e(nc.sbuf_tensor("wv_sb", [128, 8 * 1024], BF16))
        bq_sb = _e(nc.sbuf_tensor("bq_sb", [128, 1], F32))
        bq4 = _e(nc.sbuf_tensor("bq4", [128, 1], F32))
        bk_sb = _e(nc.sbuf_tensor("bk_sb", [128, 1], F32))
        x0 = _e(nc.sbuf_tensor("x0", [128, 8 * 512], BF16))
        x1 = _e(nc.sbuf_tensor("x1", [128, 8 * 512], BF16))
        k_sb = _e(nc.sbuf_tensor("k_sb", [128, T], BF16))
        q_sb = _e(nc.sbuf_tensor("q_sb", [128, T], BF16))
        v_tm = _e(nc.sbuf_tensor("v_tm", [128, 64 * 1032], BF16))
        psK0 = _e(nc.psum_tensor("psK0", [128, 512], F32))
        psK1 = _e(nc.psum_tensor("psK1", [128, 512], F32))
        psV0 = _e(nc.psum_tensor("psV0", [128, 1024], F32))
        psV1 = _e(nc.psum_tensor("psV1", [128, 1024], F32))
        psQ = _e(nc.psum_tensor("psQ", [128, 512], F32))
        sX = _e(nc.semaphore("sX"))
        sK = _e(nc.semaphore("sK"))
        sKE = _e(nc.semaphore("sKE"))
        sV = _e(nc.semaphore("sV"))
        sVE = _e(nc.semaphore("sVE"))
        sQp = _e(nc.semaphore("sQp"))
        sQE = _e(nc.semaphore("sQE"))
        sF = _e(nc.semaphore("sF"))
        sB4 = _e(nc.semaphore("sB4"))
        sW = _e(nc.semaphore("sW"))
        sMs = _e(nc.semaphore("sMs"))
        block = _e(nc.Block())
        xb = [x0, x1]
        psK = [psK0, psK1]
        psV = [psV0, psV1]
        x_r = x_fm.ap().rearrange("(c p) t -> p c t", p=128)

        @block.gpsimd
        def _(g):
            g.dma_start(wq_sb.ap().rearrange("p (c m) -> p c m", m=128),
                        wqT.ap().rearrange("(c p) m -> p c m", p=128)).then_inc(sW, 16)
            g.dma_start(wk_sb.ap().rearrange("p (c m) -> p c m", m=128),
                        wkT.ap().rearrange("(c p) m -> p c m", p=128)).then_inc(sW, 16)
            g.dma_start(wv_sb.ap().rearrange("p (c m) -> p c m", m=1024),
                        wvT.ap().rearrange("(c p) m -> p c m", p=128)).then_inc(sW, 16)
            g.dma_start(bq_sb[:, :], bq.ap()).then_inc(sW, 16)
            g.dma_start(bk_sb[:, :], bk.ap()).then_inc(sW, 16)
            g.wait_ge(sW, 80)
            for ti in range(NT):
                if ti >= 2:
                    g.wait_ge(sV, 4 * (ti - 1))
                g.wait_ge(sX, 16 * ti)
                g.dma_start(
                    xb[ti % 2].ap().rearrange("p (c t) -> p c t", t=512),
                    x_r[:, :, ti * 512:(ti + 1) * 512],
                ).then_inc(sX, 16)

        @block.tensor
        def _(t):
            t.wait_ge(sW, 80)
            for ti in range(NT):
                t.wait_ge(sX, 16 * (ti + 1))
                x = xb[ti % 2]
                if ti >= 2:
                    t.wait_ge(sKE, ti - 1)
                for dc in range(8):
                    m = t.matmul(psK[ti % 2][0:128, 0:512],
                                 wk_sb[:, dc * 128:(dc + 1) * 128],
                                 x[:, dc * 512:(dc + 1) * 512],
                                 start=(dc == 0), stop=(dc == 7))
                    if dc == 7:
                        m.then_inc(sK, 1)
                qi = ti
                if qi >= 1:
                    t.wait_ge(sQE, qi)
                for dc in range(8):
                    m = t.matmul(psQ[0:128, 0:512],
                                 wq_sb[:, dc * 128:(dc + 1) * 128],
                                 x[:, dc * 512:(dc + 1) * 512],
                                 start=(dc == 0), stop=(dc == 7))
                    if dc == 7:
                        m.then_inc(sQp, 1)
                for ci in range(4):
                    vi = ti * 4 + ci
                    if vi >= 2:
                        t.wait_ge(sVE, vi - 1)
                    for dc in range(8):
                        for n in range(2):
                            m = t.matmul(
                                psV[vi % 2][0:128, n * 512:(n + 1) * 512],
                                x[:, dc * 512 + ci * 128: dc * 512 + ci * 128 + 128],
                                wv_sb[:, dc * 1024 + n * 512: dc * 1024 + (n + 1) * 512],
                                start=(dc == 0), stop=(dc == 7))
                            if dc == 7 and n == 1:
                                m.then_inc(sV, 1)

        @block.scalar
        def _(s):
            s.wait_ge(sW, 80)
            s.mul(bq4[:, :], bq_sb[:, :], 0.25).then_inc(sB4, 1)

        @block.vector
        def _(v):
            v.memset(v_tm[:, 0:33024], 1.0)
            v.memset(v_tm[:, 33024:66048], 1.0).then_inc(sMs, 1)
            v.wait_ge(sMs, 1)
            v.wait_ge(sB4, 1)
            for ti in range(NT):
                v.wait_ge(sK, ti + 1)
                v.tensor_scalar(k_sb[:, ti * 512:(ti + 1) * 512], psK[ti % 2][0:128, 0:512],
                                bk_sb[:, 0:1], None, ALU.add).then_inc(sKE, 1)
                qi = ti
                v.wait_ge(sQp, qi + 1)
                v.tensor_scalar(q_sb[:, qi * 512:(qi + 1) * 512], psQ[0:128, 0:512],
                                0.25, bq4[:, 0:1], ALU.mult, ALU.add).then_inc(sQE, 1)
                for ci in range(4):
                    vi = ti * 4 + ci
                    v.wait_ge(sV, vi + 1)
                    v.tensor_copy(
                        v_tm.ap().rearrange("p (k h j) -> p k h j", h=8, j=129)[:, vi, :, 1:129],
                        psV[vi % 2].ap().rearrange("p (h j) -> p h j", j=128),
                    ).then_inc(sVE, 1)

        @block.sync
        def _(sy):
            sy.wait_ge(sKE, NT)
            sy.dma_start(k_out.ap(), k_sb[:, :]).then_inc(sF, 16)
            sy.wait_ge(sQE, NT)
            sy.wait_ge(sF, 16)
            sy.dma_start(q_out.ap(), q_sb[:, :]).then_inc(sF, 16)
            sy.wait_ge(sVE, NT * 4)
            sy.wait_ge(sF, 32)
            sy.dma_start(v_out.ap(), v_tm[:, :]).then_inc(sF, 16)
            sy.wait_ge(sF, 48)

    return nc


def build_p1b():
    nc = _nc()
    k_scr = nc.dram_tensor("k_scr", [128, T], BF16, kind="ExternalInput")
    q_scr = nc.dram_tensor("q_scr", [128, OWN], BF16, kind="ExternalInput")
    v_scr = nc.dram_tensor("v_scr", [128, 64 * 1032], BF16, kind="ExternalInput")
    x_nat = nc.dram_tensor("x_nat", [OWN, D], F32, kind="ExternalInput")
    gam = nc.dram_tensor("gam", [128, 1], F32, kind="ExternalInput")
    bv_bc = nc.dram_tensor("bv_bc", [128, D], F32, kind="ExternalInput")
    ident = nc.dram_tensor("ident", [128, 128], F32, kind="ExternalInput")
    s1 = nc.dram_tensor("s1", [D, OWN], F32, kind="ExternalOutput")

    with contextlib.ExitStack() as _st:
        _e = _st.enter_context
        k_sb = _e(nc.sbuf_tensor("k_sb", [128, T], BF16))
        v_tm = _e(nc.sbuf_tensor("v_tm", [128, 64 * 1032], BF16))
        qpad = _e(nc.sbuf_tensor("qpad", [128, 8 * 512], BF16))
        x_tm = _e(nc.sbuf_tensor("x_tm", [128, 2 * 1024], F32))
        attn = _e(nc.sbuf_tensor("attn", [128, 4 * 1024], F32))
        s1tm = _e(nc.sbuf_tensor("s1tm", [128, 1024], F32))
        s1f0 = _e(nc.sbuf_tensor("s1f0", [128, 8 * 128], F32))
        s1f1 = _e(nc.sbuf_tensor("s1f1", [128, 8 * 128], F32))
        tb0 = _e(nc.sbuf_tensor("tb0", [128, 512], BF16))
        tb1 = _e(nc.sbuf_tensor("tb1", [128, 512], BF16))
        lk0 = _e(nc.sbuf_tensor("lk0", [128, 512], BF16))
        lk1 = _e(nc.sbuf_tensor("lk1", [128, 512], BF16))
        pt0 = _e(nc.sbuf_tensor("pt0", [128, 512], BF16))
        pt1 = _e(nc.sbuf_tensor("pt1", [128, 512], BF16))
        gam_sb = _e(nc.sbuf_tensor("gam_sb", [128, 1], F32))
        bv_sb = _e(nc.sbuf_tensor("bv_sb", [128, 1024], F32))
        id_sb = _e(nc.sbuf_tensor("id_sb", [128, 128], F32))
        rg = _e(nc.sbuf_tensor("rg", [128, 4], F32))
        stat = _e(nc.sbuf_tensor("stat", [128, 16], F32))
        rstd = _e(nc.sbuf_tensor("rstd", [128, 4], F32))
        nmr = _e(nc.sbuf_tensor("nmr", [128, 4], F32))
        y2 = _e(nc.sbuf_tensor("y2", [128, 1024], BF16))
        eps_sb = _e(nc.sbuf_tensor("eps_sb", [128, 1], F32))
        psS0 = _e(nc.psum_tensor("psS0", [128, 512], F32))
        psS1 = _e(nc.psum_tensor("psS1", [128, 512], F32))
        psP0 = _e(nc.psum_tensor("psP0", [128, 512], F32))
        psP1 = _e(nc.psum_tensor("psP1", [128, 512], F32))
        psP2 = _e(nc.psum_tensor("psP2", [128, 512], F32))
        psP3 = _e(nc.psum_tensor("psP3", [128, 512], F32))
        psT0 = _e(nc.psum_tensor("psT0", [128, 512], F32))
        psT1 = _e(nc.psum_tensor("psT1", [128, 512], F32))
        sLd = _e(nc.semaphore("sLd"))
        sQp = _e(nc.semaphore("sQp"))
        sQm = _e(nc.semaphore("sQm"))
        sXt = _e(nc.semaphore("sXt"))
        sS = _e(nc.semaphore("sS"))
        sT = _e(nc.semaphore("sT"))
        sL = _e(nc.semaphore("sL"))
        sP = _e(nc.semaphore("sP"))
        sV2 = _e(nc.semaphore("sV2"))
        sE = _e(nc.semaphore("sE"))
        sYa = _e(nc.semaphore("sYa"))
        sVr = _e(nc.semaphore("sVr"))
        sLn = _e(nc.semaphore("sLn"))
        sNm = _e(nc.semaphore("sNm"))
        sTr = _e(nc.semaphore("sTr"))
        sTe = _e(nc.semaphore("sTe"))
        sO = _e(nc.semaphore("sO"))
        sMs = _e(nc.semaphore("sMs"))
        sDv = _e(nc.semaphore("sDv"))
        sAc = _e(nc.semaphore("sAc"))
        block = _e(nc.Block())
        psS = [psS0, psS1]
        psPV = [psP0, psP1, psP2, psP3]
        psT = [psT0, psT1]
        tb = [tb0, tb1]
        lk = [lk0, lk1]
        pt = [pt0, pt1]
        s1f = [s1f0, s1f1]
        xr2 = x_nat.ap().rearrange("(g p) c -> g p c", p=128)

        def i_of(qs, h, kb):
            return (qs * 8 + h) * NKB + kb

        def pe_pv(t, i):
            qs, rem = divmod(i, 8 * NKB)
            h, kb = divmod(rem, NKB)
            t.wait_ge(sP, i + 1)
            if kb == 0:
                hg = qs * 8 + h
                if hg >= 1:
                    t.wait_ge(sE, 4 * hg)
            for u in range(4):
                m = t.matmul(psPV[u][0:128, 0:129],
                             pt[i % 2][:, u * 128:(u + 1) * 128],
                             v_tm[:, kb * 1032 + h * 129: kb * 1032 + (h + 1) * 129],
                             start=(kb == 0), stop=(kb == NKB - 1))
                if u == 3:
                    m.then_inc(sV2, 1)

        @block.sync
        def _(sy):
            sy.dma_start(k_sb[:, :], k_scr.ap()).then_inc(sLd, 16)
            sy.dma_start(v_tm[:, :], v_scr.ap()).then_inc(sLd, 16)
            sy.dma_start(id_sb[:, :], ident.ap()).then_inc(sLd, 16)
            sy.dma_start(gam_sb[:, :], gam.ap()).then_inc(sLd, 16)
            sy.dma_start(bv_sb[:, :], bv_bc.ap()).then_inc(sLd, 16)
            for qs in range(NQS):
                sy.wait_ge(sQm, qs + 1)
                sy.wait_ge(sQp, 128 * qs)
                for h in range(8):
                    sy.dma_start(qpad[16 * h:16 * h + 16, h * 512:(h + 1) * 512],
                                 q_scr[16 * h:16 * h + 16, qs * 512:(qs + 1) * 512]
                                 ).then_inc(sQp, 16)
                if qs >= 1:
                    for u in range(4):
                        g = (qs - 1) * 4 + u
                        sy.wait_ge(sTe, 8 * (g + 1))
                        sy.wait_ge(sO, 16 * g)
                        sy.dma_start(
                            s1.ap().rearrange("(c p) t -> p c t", p=128)[
                                :, :, (qs - 1) * 512 + u * 128: (qs - 1) * 512 + (u + 1) * 128],
                            s1f[g % 2].ap().rearrange("p (c t) -> p c t", t=128),
                        ).then_inc(sO, 16)
            for u in range(4):
                g = (NQS - 1) * 4 + u
                sy.wait_ge(sTe, 8 * (g + 1))
                sy.wait_ge(sO, 16 * g)
                sy.dma_start(
                    s1.ap().rearrange("(c p) t -> p c t", p=128)[
                        :, :, (NQS - 1) * 512 + u * 128: (NQS - 1) * 512 + (u + 1) * 128],
                    s1f[g % 2].ap().rearrange("p (c t) -> p c t", t=128),
                ).then_inc(sO, 16)
            sy.wait_ge(sO, 16 * 4 * NQS)

        @block.gpsimd
        def _(g):
            for gi in range(4 * NQS):
                if gi >= 2:
                    g.wait_ge(sYa, gi - 1)
                g.wait_ge(sXt, 16 * gi)
                g.dma_start(x_tm[:, (gi % 2) * 1024:(gi % 2 + 1) * 1024],
                            xr2[gi]).then_inc(sXt, 16)

        @block.tensor
        def _(t):
            t.wait_ge(sLd, 80)
            for qs in range(NQS):
                t.wait_ge(sQp, 128 * (qs + 1))
                for h in range(8):
                    for kb in range(NKB):
                        i = i_of(qs, h, kb)
                        if i >= 2:
                            t.wait_ge(sL, i - 1)
                        t.matmul(psS[i % 2][0:128, 0:512],
                                 k_sb[:, kb * 128:(kb + 1) * 128],
                                 qpad[:, h * 512:(h + 1) * 512],
                                 start=True, stop=True).then_inc(sS, 1)
                        if i >= 1 and (i % (8 * NKB)) != 0:
                            pe_pv(t, i - 1)
                pe_pv(t, i_of(qs, 7, NKB - 1))
                for u in range(4):
                    gg = qs * 4 + u
                    t.wait_ge(sNm, gg + 1)
                    for dc in range(8):
                        j = gg * 8 + dc
                        if j >= 2:
                            t.wait_ge(sTe, j - 1)
                        t.transpose(psT[j % 2][0:128, 0:128],
                                    s1tm[:, dc * 128:(dc + 1) * 128],
                                    id_sb[:, :]).then_inc(sTr, 1)

        @block.scalar
        def _(s):
            for qs in range(NQS):
                for h in range(8):
                    for kb in range(NKB):
                        i = i_of(qs, h, kb)
                        s.wait_ge(sS, i + 1)
                        if i >= 2:
                            s.wait_ge(sL, i - 1)
                        s.activation(tb[i % 2][:, :], psS[i % 2][0:128, 0:512],
                                     AF.Copy, scale=0.2).then_inc(sT, 1)
                        s.wait_ge(sL, i + 1)
                        if i >= 2:
                            s.wait_ge(sV2, i - 1)
                        s.activation(pt[i % 2][:, :], lk[i % 2][:, :],
                                     AF.Exp).then_inc(sP, 1)
                s.wait_ge(sVr, 2 * qs + 1)
                s.activation(stat[:, 12:16], stat[:, 8:12], AF.Ln,
                             bias=eps_sb[:, 0:1]).then_inc(sAc, 1)
                s.wait_ge(sAc, qs + 1)
                s.activation(rstd[:, :], stat[:, 12:16], AF.Exp, scale=-0.5).then_inc(sLn, 1)

        @block.vector
        def _(v):
            v.memset(eps_sb[:, :], EPS).then_inc(sMs, 1)
            v.wait_ge(sMs, 1)
            v.wait_ge(sLd, 80)
            v.tensor_scalar(bv_sb[:, :], bv_sb[:, :], gam_sb[:, 0:1], None,
                            ALU.mult).then_inc(sMs, 1)
            v.wait_ge(sMs, 2)
            dv_n = [0]
            v.memset(qpad[:, :], 0.0).then_inc(sQm, 1)
            for qs in range(NQS):
                for h in range(8):
                    for kb in range(NKB):
                        i = i_of(qs, h, kb)
                        v.wait_ge(sT, i + 1)
                        if i >= 2:
                            v.wait_ge(sP, i - 1)
                        v.tensor_tensor(lk[i % 2][:, :], psS[i % 2][0:128, 0:512],
                                        tb[i % 2][:, :], ALU.max).then_inc(sL, 1)
                    hg = qs * 8 + h
                    v.wait_ge(sV2, (hg + 1) * NKB)
                    if h == 0 and qs >= 1:
                        v.wait_ge(sNm, 4 * qs)
                    v.wait_ge(sE, 4 * hg)
                    nDv = dv_n[0]
                    for u in range(4):
                        v.reciprocal(rg[:, u:u + 1], psPV[u][0:128, 0:1]).then_inc(sDv, 1)
                        nDv += 1
                    v.wait_ge(sDv, nDv)
                    v.tensor_scalar(rg[:, :], rg[:, :], gam_sb[:, 0:1], None,
                                    ALU.mult).then_inc(sDv, 1)
                    nDv += 1
                    dv_n[0] = nDv
                    v.wait_ge(sDv, nDv)
                    for u in range(4):
                        v.scalar_tensor_tensor(
                            attn[:, u * 1024 + h * 128: u * 1024 + (h + 1) * 128],
                            psPV[u][0:128, 1:129], rg[:, u:u + 1],
                            bv_sb[:, h * 128:(h + 1) * 128],
                            ALU.mult, ALU.add).then_inc(sE, 1)
                if qs + 1 < NQS:
                    v.wait_ge(sS, 512 * (qs + 1))
                    v.memset(qpad[:, :], 0.0).then_inc(sQm, 1)
                v.wait_ge(sE, 32 * (qs + 1))
                v.wait_ge(sVr, 2 * qs)
                nDv = dv_n[0]
                for u in range(4):
                    gi = qs * 4 + u
                    v.wait_ge(sXt, 16 * (gi + 1))
                    v.scalar_tensor_tensor(attn[:, u * 1024:(u + 1) * 1024],
                                           x_tm[:, (gi % 2) * 1024:(gi % 2 + 1) * 1024],
                                           2.0,
                                           attn[:, u * 1024:(u + 1) * 1024],
                                           ALU.mult, ALU.add).then_inc(sYa, 1)
                    v.wait_ge(sYa, gi + 1)
                    v.reduce_sum(stat[:, u:u + 1], attn[:, u * 1024:(u + 1) * 1024],
                                 AX.X).then_inc(sDv, 1)
                    nDv += 1
                    v.tensor_tensor(y2[:, :], attn[:, u * 1024:(u + 1) * 1024],
                                    attn[:, u * 1024:(u + 1) * 1024],
                                    ALU.mult).then_inc(sDv, 1)
                    nDv += 1
                    v.wait_ge(sDv, nDv)
                    v.reduce_sum(stat[:, 4 + u:5 + u], y2[:, :], AX.X).then_inc(sDv, 1)
                    nDv += 1
                v.wait_ge(sDv, nDv)
                v.tensor_scalar(stat[:, 0:4], stat[:, 0:4], 1.0 / D, None,
                                ALU.mult).then_inc(sDv, 1)
                nDv += 1
                v.tensor_scalar(stat[:, 4:8], stat[:, 4:8], 1.0 / D, None,
                                ALU.mult).then_inc(sDv, 1)
                nDv += 1
                v.wait_ge(sDv, nDv)
                v.tensor_tensor(stat[:, 12:16], stat[:, 0:4], stat[:, 0:4],
                                ALU.mult).then_inc(sDv, 1)
                nDv += 1
                v.wait_ge(sDv, nDv)
                v.tensor_tensor(stat[:, 8:12], stat[:, 4:8], stat[:, 12:16],
                                ALU.subtract).then_inc(sVr, 1)
                dv_n[0] = nDv
                v.wait_ge(sLn, qs + 1)
                v.scalar_tensor_tensor(nmr[:, :], stat[:, 0:4], -1.0, rstd[:, :],
                                       ALU.mult, ALU.mult).then_inc(sVr, 1)
                v.wait_ge(sVr, 2 * qs + 2)
                for u in range(4):
                    gg = qs * 4 + u
                    if gg >= 1:
                        v.wait_ge(sTr, 8 * gg)
                    v.tensor_scalar(s1tm[:, :], attn[:, u * 1024:(u + 1) * 1024],
                                    rstd[:, u:u + 1], nmr[:, u:u + 1],
                                    ALU.mult, ALU.add).then_inc(sNm, 1)
                    for dc in range(8):
                        j = gg * 8 + dc
                        v.wait_ge(sTr, j + 1)
                        if dc == 0 and gg >= 2:
                            v.wait_ge(sO, 16 * (gg - 1))
                        v.tensor_copy(s1f[gg % 2][:, dc * 128:(dc + 1) * 128],
                                      psT[j % 2][0:128, 0:128]).then_inc(sTe, 1)

    return nc


def build_p2():
    nc = _nc()
    s1c = nc.dram_tensor("s1c", [D, OWN + 1], F32, kind="ExternalInput")
    wrT = nc.dram_tensor("wrT", [D, D], F32, kind="ExternalInput")
    wkT2 = nc.dram_tensor("wkT2", [D, D], F32, kind="ExternalInput")
    wvT2 = nc.dram_tensor("wvT2", [D, D], F32, kind="ExternalInput")
    mur = nc.dram_tensor("mur", [128, 8], F32, kind="ExternalInput")
    muk = nc.dram_tensor("muk", [128, 8], F32, kind="ExternalInput")
    ident = nc.dram_tensor("ident", [128, 128], F32, kind="ExternalInput")
    s2f = nc.dram_tensor("s2f", [D, OWN], F32, kind="ExternalOutput")
    s2t = nc.dram_tensor("s2t", [OWN, D], F32, kind="ExternalOutput")

    NTS = 8
    NGT = 60
    NB = 4

    marks = []
    for ts in range(NTS):
        for oc in range(8):
            marks.append(("sSg", 8 * ts + oc + 1))
        for oc in range(8):
            marks.append(("sRl", 8 * ts + oc + 1))
        for oc in range(8):
            marks.append(("sOu", 8 * ts + oc + 1))
        marks.append(("sMu", ts + 1))
        marks.append(("sVr", 2 * ts + 1))
        marks.append(("sNm", 8 * (ts + 1)))
        marks.append(("sNm", 8 * (ts + 1)))
        for j in range(32):
            marks.append(("sTe", 32 * ts + j + 1))

    def gidx(ts, g):
        return ts * NGT + g

    with contextlib.ExitStack() as _st:
        _e = _st.enter_context
        wr_sb = _e(nc.sbuf_tensor("wr_sb", [128, 8 * 1024], BF16))
        wk_sb = _e(nc.sbuf_tensor("wk_sb", [128, 8 * 1024], BF16))
        wv_sb = _e(nc.sbuf_tensor("wv_sb", [128, 8 * 1024], BF16))
        mur_sb = _e(nc.sbuf_tensor("mur_sb", [128, 8], F32))
        muk_sb = _e(nc.sbuf_tensor("muk_sb", [128, 8], F32))
        omr_sb = _e(nc.sbuf_tensor("omr_sb", [128, 8], F32))
        omk_sb = _e(nc.sbuf_tensor("omk_sb", [128, 8], F32))
        id_sb = _e(nc.sbuf_tensor("id_sb", [128, 128], F32))
        x0 = _e(nc.sbuf_tensor("x0", [128, 8 * 513], F32))
        x1 = _e(nc.sbuf_tensor("x1", [128, 8 * 513], F32))
        xr_bf = _e(nc.sbuf_tensor("xr_bf", [128, 8 * 512], BF16))
        xk_bf = _e(nc.sbuf_tensor("xk_bf", [128, 8 * 512], BF16))
        t1 = _e(nc.sbuf_tensor("t1", [128, 512], F32))
        r_bf = _e(nc.sbuf_tensor("r_bf", [128, 8 * 512], BF16))
        t2_bf = _e(nc.sbuf_tensor("t2_bf", [128, 512], BF16))
        ka_bf = _e(nc.sbuf_tensor("ka_bf", [128, 8 * 512], BF16))
        t3 = _e(nc.sbuf_tensor("t3", [128, 512], F32))
        y_sb = _e(nc.sbuf_tensor("y_sb", [128, 8 * 512], F32))
        y_bf = _e(nc.sbuf_tensor("y_bf", [128, 8 * 512], BF16))
        y2_bf = _e(nc.sbuf_tensor("y2_bf", [128, 8 * 512], BF16))
        ones_bf = _e(nc.sbuf_tensor("ones_bf", [128, 1], BF16))
        ones_f = _e(nc.sbuf_tensor("ones_f", [1, 128], F32))
        eps_sb = _e(nc.sbuf_tensor("eps_sb", [1, 1], F32))
        mu_row = _e(nc.sbuf_tensor("mu_row", [1, 512], F32))
        mu2_row = _e(nc.sbuf_tensor("mu2_row", [1, 512], F32))
        var_row = _e(nc.sbuf_tensor("var_row", [1, 512], F32))
        lnv_row = _e(nc.sbuf_tensor("lnv_row", [1, 512], F32))
        rstd_row = _e(nc.sbuf_tensor("rstd_row", [1, 512], F32))
        mr_row = _e(nc.sbuf_tensor("mr_row", [1, 512], F32))
        s2_sb = _e(nc.sbuf_tensor("s2_sb", [128, 8 * 512], F32))
        s2t_sb0 = _e(nc.sbuf_tensor("s2t_sb0", [128, 8 * 128], F32))
        s2t_sb1 = _e(nc.sbuf_tensor("s2t_sb1", [128, 8 * 128], F32))
        psX0 = _e(nc.psum_tensor("psX0", [128, 512], F32))
        psX1 = _e(nc.psum_tensor("psX1", [128, 512], F32))
        psX2 = _e(nc.psum_tensor("psX2", [128, 512], F32))
        psX3 = _e(nc.psum_tensor("psX3", [128, 512], F32))
        sW = _e(nc.semaphore("sW"))
        sIn = _e(nc.semaphore("sIn"))
        sMM = _e(nc.semaphore("sMM"))
        sMx = _e(nc.semaphore("sMx"))
        sDv = _e(nc.semaphore("sDv"))
        sAc = _e(nc.semaphore("sAc"))
        sSg = _e(nc.semaphore("sSg"))
        sRl = _e(nc.semaphore("sRl"))
        sOu = _e(nc.semaphore("sOu"))
        sMu = _e(nc.semaphore("sMu"))
        sSq2 = _e(nc.semaphore("sSq2"))
        sYc = _e(nc.semaphore("sYc"))
        sLn = _e(nc.semaphore("sLn"))
        sVr = _e(nc.semaphore("sVr"))
        sNm = _e(nc.semaphore("sNm"))
        sTe = _e(nc.semaphore("sTe"))
        sOF = _e(nc.semaphore("sOF"))
        sOT = _e(nc.semaphore("sOT"))
        sMs = _e(nc.semaphore("sMs"))
        block = _e(nc.Block())

        xb = [x0, x1]
        psX = [psX0, psX1, psX2, psX3]
        s2t_sb = [s2t_sb0, s2t_sb1]
        sems = dict(sSg=sSg, sRl=sRl, sOu=sOu, sMu=sMu, sVr=sVr, sNm=sNm, sTe=sTe)

        def bank_wait(t, g):
            if g >= NB:
                nm, val = marks[g - NB]
                t.wait_ge(sems[nm], val)

        @block.gpsimd
        def _(g):
            g.dma_start(wr_sb.ap().rearrange("p (c m) -> p c m", m=1024),
                        wrT.ap().rearrange("(c p) m -> p c m", p=128)).then_inc(sW, 16)
            g.dma_start(wk_sb.ap().rearrange("p (c m) -> p c m", m=1024),
                        wkT2.ap().rearrange("(c p) m -> p c m", p=128)).then_inc(sW, 16)
            g.dma_start(wv_sb.ap().rearrange("p (c m) -> p c m", m=1024),
                        wvT2.ap().rearrange("(c p) m -> p c m", p=128)).then_inc(sW, 16)
            g.dma_start(mur_sb[:, :], mur.ap()).then_inc(sW, 16)
            g.dma_start(muk_sb[:, :], muk.ap()).then_inc(sW, 16)
            g.dma_start(id_sb[:, :], ident.ap()).then_inc(sW, 16)
            for ts in range(NTS):
                if ts >= 2:
                    g.wait_ge(sMx, 16 * (ts - 1))
                    g.wait_ge(sYc, 9 * (ts - 1))
                g.wait_ge(sIn, 16 * ts)
                g.dma_start(
                    xb[ts % 2].ap().rearrange("p (c t) -> p c t", t=513),
                    s1c.ap().rearrange("(c p) t -> p c t", p=128)[:, :, ts * 512: ts * 512 + 513],
                ).then_inc(sIn, 16)

        @block.tensor
        def _(t):
            t.wait_ge(sW, 96)
            t.wait_ge(sMs, 1)
            for ts in range(NTS):
                for stage, (wsb, rhs) in enumerate([(wr_sb, xr_bf), (wk_sb, xk_bf)]):
                    t.wait_ge(sMx, 16 * ts + (stage + 1) * 8)
                    for oc in range(8):
                        g = gidx(ts, stage * 8 + oc)
                        bank_wait(t, g)
                        for ic in range(8):
                            m = t.matmul(psX[g % NB][0:128, 0:512],
                                         wsb[:, ic * 1024 + oc * 128: ic * 1024 + (oc + 1) * 128],
                                         rhs[:, ic * 512:(ic + 1) * 512],
                                         start=(ic == 0), stop=(ic == 7))
                            if ic == 7:
                                m.then_inc(sMM, 1)
                t.wait_ge(sSq2, 8 * (ts + 1))
                for oc in range(8):
                    g = gidx(ts, 16 + oc)
                    bank_wait(t, g)
                    for ic in range(8):
                        m = t.matmul(psX[g % NB][0:128, 0:512],
                                     wv_sb[:, ic * 1024 + oc * 128: ic * 1024 + (oc + 1) * 128],
                                     ka_bf[:, ic * 512:(ic + 1) * 512],
                                     start=(ic == 0), stop=(ic == 7))
                        if ic == 7:
                            m.then_inc(sMM, 1)
                g = gidx(ts, 24)
                t.wait_ge(sYc, 9 * (ts + 1))
                bank_wait(t, g)
                for dc in range(8):
                    m = t.matmul(psX[g % NB][0:1, 0:512], ones_bf[:, 0:1],
                                 y_bf[:, dc * 512:(dc + 1) * 512],
                                 start=(dc == 0), stop=(dc == 7))
                    if dc == 7:
                        m.then_inc(sMM, 1)
                g = gidx(ts, 25)
                bank_wait(t, g)
                for dc in range(8):
                    m = t.matmul(psX[g % NB][0:1, 0:512], ones_bf[:, 0:1],
                                 y2_bf[:, dc * 512:(dc + 1) * 512],
                                 start=(dc == 0), stop=(dc == 7))
                    if dc == 7:
                        m.then_inc(sMM, 1)
                g = gidx(ts, 26)
                t.wait_ge(sLn, ts + 1)
                bank_wait(t, g)
                t.matmul(psX[g % NB][0:128, 0:512], ones_f[0:1, :], rstd_row[0:1, :],
                         start=True, stop=True).then_inc(sMM, 1)
                g = gidx(ts, 27)
                t.wait_ge(sVr, 2 * ts + 2)
                bank_wait(t, g)
                t.matmul(psX[g % NB][0:128, 0:512], ones_f[0:1, :], mr_row[0:1, :],
                         start=True, stop=True).then_inc(sMM, 1)
                t.wait_ge(sNm, 8 * (ts + 1))
                for u in range(4):
                    for dc in range(8):
                        g = gidx(ts, 28 + u * 8 + dc)
                        bank_wait(t, g)
                        t.transpose(psX[g % NB][0:128, 0:128],
                                    s2_sb[:, dc * 512 + u * 128: dc * 512 + (u + 1) * 128],
                                    id_sb[:, :]).then_inc(sMM, 1)

        @block.scalar
        def _(s):
            for ts in range(NTS):
                for oc in range(8):
                    g = gidx(ts, oc)
                    s.wait_ge(sMM, g + 1)
                    s.activation(r_bf[:, oc * 512:(oc + 1) * 512], psX[g % NB][0:128, 0:512],
                                 AF.Sigmoid).then_inc(sSg, 1)
                for oc in range(8):
                    s.wait_ge(sRl, 8 * ts + oc + 1)
                    s.activation(ka_bf[:, oc * 512:(oc + 1) * 512], t2_bf[:, :],
                                 AF.Square).then_inc(sSq2, 1)
                s.wait_ge(sVr, 2 * ts + 1)
                s.activation(lnv_row[0:1, :], var_row[0:1, :], AF.Ln,
                             bias=eps_sb[0:1, 0:1]).then_inc(sAc, 1)
                s.wait_ge(sAc, ts + 1)
                s.activation(rstd_row[0:1, :], lnv_row[0:1, :], AF.Exp,
                             scale=-0.5).then_inc(sLn, 1)

        @block.vector
        def _(v):
            v.memset(ones_bf[:, :], 1.0 / D)
            v.memset(ones_f[:, :], 1.0)
            v.memset(eps_sb[:, :], EPS).then_inc(sMs, 1)
            v.wait_ge(sMs, 1)
            v.wait_ge(sW, 96)
            nDv = 0
            v.tensor_scalar(omr_sb[:, :], mur_sb[:, :], -1.0, 1.0, ALU.mult,
                            ALU.add).then_inc(sDv, 1)
            v.tensor_scalar(omk_sb[:, :], muk_sb[:, :], -1.0, 1.0, ALU.mult,
                            ALU.add).then_inc(sDv, 1)
            nDv += 2
            v.wait_ge(sDv, nDv)
            for ts in range(NTS):
                v.wait_ge(sIn, 16 * (ts + 1))
                if ts >= 1:
                    v.wait_ge(sMM, gidx(ts - 1, 16))
                for stage, (mu_t, om_t, dst) in enumerate(
                        [(mur_sb, omr_sb, xr_bf), (muk_sb, omk_sb, xk_bf)]):
                    for dc in range(8):
                        x = xb[ts % 2]
                        v.wait_ge(sMx, 16 * ts + stage * 8 + dc)
                        v.tensor_scalar(t1[:, :], x[:, dc * 513: dc * 513 + 512],
                                        om_t[:, dc:dc + 1], None, ALU.mult).then_inc(sDv, 1)
                        nDv += 1
                        v.wait_ge(sDv, nDv)
                        v.scalar_tensor_tensor(dst[:, dc * 512:(dc + 1) * 512],
                                               x[:, dc * 513 + 1: dc * 513 + 513],
                                               mu_t[:, dc:dc + 1],
                                               t1[:, :], ALU.mult,
                                               ALU.add).then_inc(sMx, 1)
                for oc in range(8):
                    g = gidx(ts, 8 + oc)
                    v.wait_ge(sMM, g + 1)
                    if oc >= 1:
                        v.wait_ge(sSq2, 8 * ts + oc)
                    v.tensor_scalar(t2_bf[:, :], psX[g % NB][0:128, 0:512],
                                    0.0, None, ALU.max).then_inc(sRl, 1)
                for oc in range(8):
                    g = gidx(ts, 16 + oc)
                    v.wait_ge(sMM, g + 1)
                    v.wait_ge(sSg, 8 * ts + oc + 1)
                    v.wait_ge(sDv, nDv)
                    v.tensor_tensor(t3[:, :], psX[g % NB][0:128, 0:512],
                                    r_bf[:, oc * 512:(oc + 1) * 512],
                                    ALU.mult).then_inc(sOu, 1)
                    v.wait_ge(sOu, 8 * ts + oc + 1)
                    if ts >= 1 and oc == 0:
                        v.wait_ge(sMM, gidx(ts - 1, 26))
                    v.tensor_tensor(y_sb[:, oc * 512:(oc + 1) * 512], t3[:, :],
                                    xb[ts % 2][:, oc * 513 + 1: oc * 513 + 513],
                                    ALU.add).then_inc(sDv, 1)
                    nDv += 1
                    v.wait_ge(sDv, nDv)
                    v.tensor_copy(y_bf[:, oc * 512:(oc + 1) * 512],
                                  y_sb[:, oc * 512:(oc + 1) * 512]).then_inc(sYc, 1)
                v.wait_ge(sYc, 9 * ts + 8)
                v.tensor_tensor(y2_bf[:, :], y_bf[:, :], y_bf[:, :],
                                ALU.mult).then_inc(sYc, 1)
                g = gidx(ts, 24)
                v.wait_ge(sMM, g + 1)
                v.tensor_copy(mu_row[0:1, :], psX[g % NB][0:1, 0:512]).then_inc(sMu, 1)
                g = gidx(ts, 25)
                v.wait_ge(sMM, g + 1)
                v.wait_ge(sMu, ts + 1)
                v.tensor_tensor(mu2_row[0:1, :], mu_row[0:1, :], mu_row[0:1, :],
                                ALU.mult).then_inc(sDv, 1)
                nDv += 1
                v.wait_ge(sDv, nDv)
                v.tensor_tensor(var_row[0:1, :], psX[g % NB][0:1, 0:512],
                                mu2_row[0:1, :], ALU.subtract).then_inc(sVr, 1)
                v.wait_ge(sLn, ts + 1)
                v.tensor_tensor(mr_row[0:1, :], mu_row[0:1, :], rstd_row[0:1, :],
                                ALU.mult).then_inc(sVr, 1)
                gR = gidx(ts, 26)
                gM = gidx(ts, 27)
                v.wait_ge(sMM, gM + 1)
                if ts >= 1:
                    v.wait_ge(sMM, gidx(ts, 0))
                    v.wait_ge(sOF, 16 * ts)
                for dc in range(8):
                    v.wait_ge(sNm, 8 * ts + dc)
                    v.tensor_tensor(t3[:, :], y_sb[:, dc * 512:(dc + 1) * 512],
                                    psX[gR % NB][0:128, 0:512], ALU.mult).then_inc(sDv, 1)
                    nDv += 1
                    v.wait_ge(sDv, nDv)
                    v.tensor_tensor(s2_sb[:, dc * 512:(dc + 1) * 512], t3[:, :],
                                    psX[gM % NB][0:128, 0:512],
                                    ALU.subtract).then_inc(sNm, 1)
                for u in range(4):
                    for dc in range(8):
                        g = gidx(ts, 28 + u * 8 + dc)
                        gu = ts * 4 + u
                        v.wait_ge(sMM, g + 1)
                        if dc == 0 and gu >= 2:
                            v.wait_ge(sOT, 16 * (gu - 1))
                        v.tensor_copy(s2t_sb[gu % 2][:, dc * 128:(dc + 1) * 128],
                                      psX[g % NB][0:128, 0:128]).then_inc(sTe, 1)

        @block.sync
        def _(sy):
            for ts in range(NTS):
                sy.wait_ge(sNm, 8 * (ts + 1))
                sy.wait_ge(sOF, 16 * ts)
                sy.dma_start(
                    s2f.ap().rearrange("(c p) t -> p c t", p=128)[:, :, ts * 512:(ts + 1) * 512],
                    s2_sb.ap().rearrange("p (c t) -> p c t", t=512),
                ).then_inc(sOF, 16)
                for u in range(4):
                    gu = ts * 4 + u
                    sy.wait_ge(sTe, 8 * (gu + 1))
                    sy.wait_ge(sOT, 16 * gu)
                    sy.dma_start(
                        s2t.ap().rearrange("(g p) c -> g p c", p=128)[gu],
                        s2t_sb[gu % 2][:, :],
                    ).then_inc(sOT, 16)
            sy.wait_ge(sOF, 16 * NTS)
            sy.wait_ge(sOT, 16 * 4 * NTS)

    return nc


def build_p4a(R):
    nc = _nc()
    NKR = R // 128
    NTR = R // 512
    s2full = nc.dram_tensor("s2full", [D, R], F32, kind="ExternalInput")
    wq1 = nc.dram_tensor("wq1", [D, D], F32, kind="ExternalInput")
    wk1 = nc.dram_tensor("wk1", [D, D], F32, kind="ExternalInput")
    wv1 = nc.dram_tensor("wv1", [D, D], F32, kind="ExternalInput")
    q1_out = nc.dram_tensor("q1_out", [D, OWN], BF16, kind="ExternalOutput")
    k1_out = nc.dram_tensor("k1_out", [D, R], BF16, kind="ExternalOutput")
    v1_out = nc.dram_tensor("v1_out", [128, NKR * 1025], BF16, kind="ExternalOutput")

    own_lo = (R - OWN) // 512
    own_lo = NTR - 8

    with contextlib.ExitStack() as _st:
        _e = _st.enter_context
        wq_sb = _e(nc.sbuf_tensor("wq_sb", [128, 8 * 1024], BF16))
        wk_sb = _e(nc.sbuf_tensor("wk_sb", [128, 8 * 1024], BF16))
        wv_sb = _e(nc.sbuf_tensor("wv_sb", [128, 8 * 1024], BF16))
        x0 = _e(nc.sbuf_tensor("x0", [128, 8 * 512], BF16))
        x1 = _e(nc.sbuf_tensor("x1", [128, 8 * 512], BF16))
        q_st = _e(nc.sbuf_tensor("q_st", [128, 2 * 8 * 512], BF16))
        k_st = _e(nc.sbuf_tensor("k_st", [128, 2 * 8 * 512], BF16))
        v_st = _e(nc.sbuf_tensor("v_st", [128, 2 * 1025], BF16))
        psK0 = _e(nc.psum_tensor("psK0", [128, 512], F32))
        psK1 = _e(nc.psum_tensor("psK1", [128, 512], F32))
        psQ0 = _e(nc.psum_tensor("psQ0", [128, 512], F32))
        psQ1 = _e(nc.psum_tensor("psQ1", [128, 512], F32))
        psV0 = _e(nc.psum_tensor("psV0", [128, 1024], F32))
        psV1 = _e(nc.psum_tensor("psV1", [128, 1024], F32))
        sW = _e(nc.semaphore("sW"))
        sX = _e(nc.semaphore("sX"))
        sK = _e(nc.semaphore("sK"))
        sKE = _e(nc.semaphore("sKE"))
        sQg = _e(nc.semaphore("sQg"))
        sQE = _e(nc.semaphore("sQE"))
        sV = _e(nc.semaphore("sV"))
        sVE = _e(nc.semaphore("sVE"))
        sOK = _e(nc.semaphore("sOK"))
        sOQ = _e(nc.semaphore("sOQ"))
        sOV = _e(nc.semaphore("sOV"))
        sMs = _e(nc.semaphore("sMs"))
        block = _e(nc.Block())

        xb = [x0, x1]
        psK = [psK0, psK1]
        psQ = [psQ0, psQ1]
        psV = [psV0, psV1]

        @block.gpsimd
        def _(g):
            g.dma_start(wq_sb.ap().rearrange("p (c m) -> p c m", m=1024),
                        wq1.ap().rearrange("(c p) m -> p c m", p=128)).then_inc(sW, 16)
            g.dma_start(wk_sb.ap().rearrange("p (c m) -> p c m", m=1024),
                        wk1.ap().rearrange("(c p) m -> p c m", p=128)).then_inc(sW, 16)
            g.dma_start(wv_sb.ap().rearrange("p (c m) -> p c m", m=1024),
                        wv1.ap().rearrange("(c p) m -> p c m", p=128)).then_inc(sW, 16)
            g.wait_ge(sW, 48)
            for ti in range(NTR):
                if ti >= 2:
                    g.wait_ge(sV, 4 * (ti - 1))
                g.wait_ge(sX, 16 * ti)
                g.dma_start(
                    xb[ti % 2].ap().rearrange("p (c t) -> p c t", t=512),
                    s2full.ap().rearrange("(c p) t -> p c t", p=128)[:, :, ti * 512:(ti + 1) * 512],
                ).then_inc(sX, 16)

        @block.tensor
        def _(t):
            t.wait_ge(sW, 48)
            for ti in range(NTR):
                t.wait_ge(sX, 16 * (ti + 1))
                x = xb[ti % 2]
                for oc in range(8):
                    kg = ti * 8 + oc
                    if kg >= 2:
                        t.wait_ge(sKE, kg - 1)
                    for ic in range(8):
                        m = t.matmul(psK[kg % 2][0:128, 0:512],
                                     wk_sb[:, ic * 1024 + oc * 128: ic * 1024 + (oc + 1) * 128],
                                     x[:, ic * 512:(ic + 1) * 512],
                                     start=(ic == 0), stop=(ic == 7))
                        if ic == 7:
                            m.then_inc(sK, 1)
                if ti >= own_lo:
                    for oc in range(8):
                        qg = (ti - own_lo) * 8 + oc
                        if qg >= 2:
                            t.wait_ge(sQE, qg - 1)
                        for ic in range(8):
                            m = t.matmul(psQ[qg % 2][0:128, 0:512],
                                         wq_sb[:, ic * 1024 + oc * 128: ic * 1024 + (oc + 1) * 128],
                                         x[:, ic * 512:(ic + 1) * 512],
                                         start=(ic == 0), stop=(ic == 7))
                            if ic == 7:
                                m.then_inc(sQg, 1)
                for ci in range(4):
                    vi = ti * 4 + ci
                    if vi >= 2:
                        t.wait_ge(sVE, vi - 1)
                    for ic in range(8):
                        for n in range(2):
                            m = t.matmul(
                                psV[vi % 2][0:128, n * 512:(n + 1) * 512],
                                x[:, ic * 512 + ci * 128: ic * 512 + ci * 128 + 128],
                                wv_sb[:, ic * 1024 + n * 512: ic * 1024 + (n + 1) * 512],
                                start=(ic == 0), stop=(ic == 7))
                            if ic == 7 and n == 1:
                                m.then_inc(sV, 1)

        @block.scalar
        def _(s):
            for ti in range(NTR):
                for oc in range(8):
                    kg = ti * 8 + oc
                    s.wait_ge(sK, kg + 1)
                    if kg >= 2 * 8:
                        s.wait_ge(sOK, 16 * (ti - 1))
                    s.activation(k_st[:, (ti % 2) * 4096 + oc * 512: (ti % 2) * 4096 + (oc + 1) * 512],
                                 psK[kg % 2][0:128, 0:512], AF.Copy).then_inc(sKE, 1)
                if ti >= own_lo:
                    for oc in range(8):
                        qg = (ti - own_lo) * 8 + oc
                        s.wait_ge(sQg, qg + 1)
                        if qg >= 2 * 8:
                            s.wait_ge(sOQ, 16 * (ti - own_lo - 1))
                        s.activation(q_st[:, (ti % 2) * 4096 + oc * 512: (ti % 2) * 4096 + (oc + 1) * 512],
                                     psQ[qg % 2][0:128, 0:512], AF.Copy,
                                     scale=float(1.0 / 32.0)).then_inc(sQE, 1)

        @block.vector
        def _(v):
            v.memset(v_st[:, 0:1], 1.0)
            v.memset(v_st[:, 1025:1026], 1.0).then_inc(sMs, 1)
            v.wait_ge(sMs, 1)
            for vi in range(NKR):
                v.wait_ge(sV, vi + 1)
                if vi >= 2:
                    v.wait_ge(sOV, 16 * (vi - 1))
                v.tensor_copy(v_st[:, (vi % 2) * 1025 + 1: (vi % 2) * 1025 + 1025],
                              psV[vi % 2][0:128, 0:1024]).then_inc(sVE, 1)

        @block.sync
        def _(sy):
            for ti in range(NTR):
                sy.wait_ge(sKE, 8 * (ti + 1))
                sy.wait_ge(sOK, 16 * ti)
                sy.dma_start(
                    k1_out.ap().rearrange("(c p) t -> p c t", p=128)[:, :, ti * 512:(ti + 1) * 512],
                    k_st[:, (ti % 2) * 4096:(ti % 2) * 4096 + 4096].rearrange("p (c t) -> p c t", t=512),
                ).then_inc(sOK, 16)
                if ti >= own_lo:
                    tq = ti - own_lo
                    sy.wait_ge(sQE, 8 * (tq + 1))
                    sy.wait_ge(sOQ, 16 * tq)
                    sy.dma_start(
                        q1_out.ap().rearrange("(c p) t -> p c t", p=128)[:, :, tq * 512:(tq + 1) * 512],
                        q_st[:, (ti % 2) * 4096:(ti % 2) * 4096 + 4096].rearrange("p (c t) -> p c t", t=512),
                    ).then_inc(sOQ, 16)
                for ci in range(4):
                    vi = ti * 4 + ci
                    sy.wait_ge(sVE, vi + 1)
                    sy.wait_ge(sOV, 16 * vi)
                    sy.dma_start(v1_out[:, vi * 1025:(vi + 1) * 1025],
                                 v_st[:, (vi % 2) * 1025:(vi % 2) * 1025 + 1025]).then_inc(sOV, 16)
            sy.wait_ge(sOK, 16 * NTR)
            sy.wait_ge(sOV, 16 * NKR)
            sy.wait_ge(sOQ, 16 * 8)

    return nc


def build_p4b(R, qoff):
    nc = _nc()
    NKR = R // 128
    NQT = 16
    q1 = nc.dram_tensor("q1", [D, OWN], BF16, kind="ExternalInput")
    k1 = nc.dram_tensor("k1", [D, R], BF16, kind="ExternalInput")
    v1 = nc.dram_tensor("v1", [128, NKR * 1025], BF16, kind="ExternalInput")
    s2town = nc.dram_tensor("s2town", [OWN, D], F32, kind="ExternalInput")
    g1r = nc.dram_tensor("g1r", [32, 4 * 32], F32, kind="ExternalInput")
    g1i = nc.dram_tensor("g1i", [32, 4 * 32], F32, kind="ExternalInput")
    g2r = nc.dram_tensor("g2r", [16, 4 * 16], F32, kind="ExternalInput")
    g2i = nc.dram_tensor("g2i", [16, 4 * 16], F32, kind="ExternalInput")
    e1m = nc.dram_tensor("e1m", [32, 512], F32, kind="ExternalInput")
    e2m = nc.dram_tensor("e2m", [32, 4 * 128], F32, kind="ExternalInput")
    e3m = nc.dram_tensor("e3m", [16, 512], F32, kind="ExternalInput")
    e4m = nc.dram_tensor("e4m", [16, 4 * 128], F32, kind="ExternalInput")
    brc = nc.dram_tensor("brc", [128, 4], F32, kind="ExternalInput")
    bic = nc.dram_tensor("bic", [128, 4], F32, kind="ExternalInput")
    m0 = nc.dram_tensor("m0", [128, 256], BF16, kind="ExternalInput")
    m1 = nc.dram_tensor("m1", [128, 256], BF16, kind="ExternalInput")
    ident = nc.dram_tensor("ident", [128, 128], F32, kind="ExternalInput")
    o_nat = nc.dram_tensor("o_nat", [OWN, D], F32, kind="ExternalOutput")

    KBH = [qoff // 128 + 2 * qt + 2 for qt in range(NQT)]
    CKB = [0]
    for qt in range(NQT):
        CKB.append(CKB[-1] + KBH[qt])
    TKB = CKB[-1]

    with contextlib.ExitStack() as _st:
        _e = _st.enter_context
        v_tm = _e(nc.sbuf_tensor("v_tm", [128, NKR * 1025], BF16))
        q_sb = _e(nc.sbuf_tensor("q_sb", [128, 8 * 256], BF16))
        k0 = _e(nc.sbuf_tensor("k0", [128, 8 * 128], BF16))
        k1b = _e(nc.sbuf_tensor("k1b", [128, 8 * 128], BF16))
        pt0 = _e(nc.sbuf_tensor("pt0", [128, 256], BF16))
        pt1 = _e(nc.sbuf_tensor("pt1", [128, 256], BF16))
        pt2a = _e(nc.sbuf_tensor("pt2a", [128, 256], BF16))
        pt2b = _e(nc.sbuf_tensor("pt2b", [128, 256], BF16))
        m0_sb = _e(nc.sbuf_tensor("m0_sb", [128, 256], BF16))
        m1_sb = _e(nc.sbuf_tensor("m1_sb", [128, 256], BF16))
        attn_q = _e(nc.sbuf_tensor("attn_q", [128, 1024], F32))
        s2_t0 = _e(nc.sbuf_tensor("s2_t0", [128, 1024], F32))
        s2_t1 = _e(nc.sbuf_tensor("s2_t1", [128, 1024], F32))
        s3tm = _e(nc.sbuf_tensor("s3tm", [128, 1024], F32))
        s3f = _e(nc.sbuf_tensor("s3f", [128, 1024], F32))
        s3f_bf = _e(nc.sbuf_tensor("s3f_bf", [128, 1024], BF16))
        y4 = _e(nc.sbuf_tensor("y4", [128, 1024], F32))
        y4tm = _e(nc.sbuf_tensor("y4tm", [128, 1024], F32))
        o_t0 = _e(nc.sbuf_tensor("o_t0", [128, 1024], F32))
        o_t1 = _e(nc.sbuf_tensor("o_t1", [128, 1024], F32))
        tb4 = _e(nc.sbuf_tensor("tb4", [128, 128], F32))
        y2b = _e(nc.sbuf_tensor("y2b", [128, 1024], BF16))
        stat = _e(nc.sbuf_tensor("stat", [128, 8], F32))
        rstd = _e(nc.sbuf_tensor("rstd", [128, 2], F32))
        nmr = _e(nc.sbuf_tensor("nmr", [128, 2], F32))
        rg = _e(nc.sbuf_tensor("rg", [128, 1], F32))
        eps_sb = _e(nc.sbuf_tensor("eps_sb", [128, 1], F32))
        id_sb = _e(nc.sbuf_tensor("id_sb", [128, 128], F32))
        g1_sb = _e(nc.sbuf_tensor("g1_sb", [32, 2 * 4 * 32], BF16))
        g2_sb = _e(nc.sbuf_tensor("g2_sb", [16, 2 * 4 * 16], BF16))
        e1_sb = _e(nc.sbuf_tensor("e1_sb", [32, 512], BF16))
        e2_sb = _e(nc.sbuf_tensor("e2_sb", [32, 4 * 128], BF16))
        e3_sb = _e(nc.sbuf_tensor("e3_sb", [16, 512], BF16))
        e4_sb = _e(nc.sbuf_tensor("e4_sb", [16, 4 * 128], BF16))
        r1_sb = _e(nc.sbuf_tensor("r1_sb", [32, 512], BF16))
        s1_sb = _e(nc.sbuf_tensor("s1_sb", [16, 512], BF16))
        wrt_bf = _e(nc.sbuf_tensor("wrt_bf", [128, 4 * 512], BF16))
        wit_bf = _e(nc.sbuf_tensor("wit_bf", [128, 4 * 512], BF16))
        witn_bf = _e(nc.sbuf_tensor("witn_bf", [128, 4 * 512], BF16))
        br_sb = _e(nc.sbuf_tensor("br_sb", [128, 4], F32))
        bi_sb = _e(nc.sbuf_tensor("bi_sb", [128, 4], F32))
        psS0 = _e(nc.psum_tensor("psS0", [128, 512], F32))
        psS1 = _e(nc.psum_tensor("psS1", [128, 512], F32))
        psP0 = _e(nc.psum_tensor("psP0", [128, 1536], F32))
        psP1 = _e(nc.psum_tensor("psP1", [128, 1536], F32))
        sLd = _e(nc.semaphore("sLd"))
        sQt = _e(nc.semaphore("sQt"))
        sKt = _e(nc.semaphore("sKt"))
        sS2r = _e(nc.semaphore("sS2r"))
        sS = _e(nc.semaphore("sS"))
        sP = _e(nc.semaphore("sP"))
        sMk = _e(nc.semaphore("sMk"))
        sV2 = _e(nc.semaphore("sV2"))
        sE = _e(nc.semaphore("sE"))
        sY3 = _e(nc.semaphore("sY3"))
        sVr = _e(nc.semaphore("sVr"))
        sLn = _e(nc.semaphore("sLn"))
        sNm = _e(nc.semaphore("sNm"))
        sTr = _e(nc.semaphore("sTr"))
        sTe = _e(nc.semaphore("sTe"))
        sCb = _e(nc.semaphore("sCb"))
        sFF = _e(nc.semaphore("sFF"))
        sFE = _e(nc.semaphore("sFE"))
        sY4 = _e(nc.semaphore("sY4"))
        sO = _e(nc.semaphore("sO"))
        sDv = _e(nc.semaphore("sDv"))
        sAc = _e(nc.semaphore("sAc"))
        sMs = _e(nc.semaphore("sMs"))
        sTT = _e(nc.semaphore("sTT"))
        sTD = _e(nc.semaphore("sTD"))
        sLg = _e(nc.semaphore("sLg"))
        block = _e(nc.Block())

        psS = [psS0, psS1]
        psPV = [psP0, psP1]
        kb_sb = [k0, k1b]
        aex_sb = y2b[:, 0:512]
        bex_sb = y2b[:, 512:1024]
        prod_sb = o_t0[:, 0:512]
        wacc_j = [attn_q[:, 0:512], attn_q[:, 512:1024], s3f[:, 0:512], s3f[:, 512:1024]]
        ptb = [pt0, pt1]
        pt2 = [pt2a, pt2b]
        s2_t = [s2_t0, s2_t1]
        o_t = [o_t0, o_t1]
        msk = [m0_sb, m1_sb]


        @block.sync
        def _(sy):
            sy.dma_start(v_tm[:, :], v1.ap()).then_inc(sLd, 16)
            sy.dma_start(id_sb[:, :], ident.ap()).then_inc(sLd, 16)
            sy.dma_start(m0_sb[:, :], m0.ap()).then_inc(sLd, 16)
            sy.dma_start(m1_sb[:, :], m1.ap()).then_inc(sLd, 16)
            sy.dma_start(br_sb[:, :], brc.ap()).then_inc(sLd, 16)
            sy.dma_start(bi_sb[:, :], bic.ap()).then_inc(sLd, 16)
            for qt in range(NQT):
                sy.wait_ge(sQt, 16 * qt)
                if qt >= 1:
                    sy.wait_ge(sS, CKB[qt])
                sy.dma_start(
                    q_sb.ap().rearrange("p (c t) -> p c t", t=256),
                    q1.ap().rearrange("(c p) t -> p c t", p=128)[:, :, qt * 256:(qt + 1) * 256],
                ).then_inc(sQt, 16)
                if qt >= 1:
                    for qsub in range(2):
                        gi = (qt - 1) * 2 + qsub
                        sy.wait_ge(sNm, 2 * gi + 2)
                        sy.wait_ge(sO, 16 * gi)
                        sy.dma_start(
                            o_nat.ap().rearrange("(g p) c -> g p c", p=128)[gi],
                            o_t[gi % 2][:, :]).then_inc(sO, 16)
            for qsub in range(2):
                gi = (NQT - 1) * 2 + qsub
                sy.wait_ge(sNm, 2 * gi + 2)
                sy.wait_ge(sO, 16 * gi)
                sy.dma_start(
                    o_nat.ap().rearrange("(g p) c -> g p c", p=128)[gi],
                    o_t[gi % 2][:, :]).then_inc(sO, 16)
            sy.wait_ge(sO, 16 * 2 * NQT)

        @block.gpsimd
        def _(g):
            g.dma_start(g1_sb[:, 0:128], g1r.ap()).then_inc(sLg, 16)
            g.dma_start(g1_sb[:, 128:256], g1i.ap()).then_inc(sLg, 16)
            g.dma_start(g2_sb[:, 0:64], g2r.ap()).then_inc(sLg, 16)
            g.dma_start(g2_sb[:, 64:128], g2i.ap()).then_inc(sLg, 16)
            g.dma_start(e1_sb[:, :], e1m.ap()).then_inc(sLg, 16)
            g.dma_start(e2_sb[:, :], e2m.ap()).then_inc(sLg, 16)
            g.dma_start(e3_sb[:, :], e3m.ap()).then_inc(sLg, 16)
            g.dma_start(e4_sb[:, :], e4m.ap()).then_inc(sLg, 16)
            nk = 0
            for qt in range(NQT):
                for kb in range(KBH[qt]):
                    e = CKB[qt] + kb
                    g.wait_ge(sKt, 16 * e)
                    if e >= 2:
                        g.wait_ge(sS, e - 1)
                    g.dma_start(
                        kb_sb[e % 2].ap().rearrange("p (c t) -> p c t", t=128),
                        k1.ap().rearrange("(c p) t -> p c t", p=128)[:, :, kb * 128:(kb + 1) * 128],
                    ).then_inc(sKt, 16)
                for qsub in range(2):
                    gi = qt * 2 + qsub
                    g.wait_ge(sS2r, 16 * gi)
                    if gi >= 2:
                        g.wait_ge(sY3, gi - 1)
                    g.dma_start(s2_t[gi % 2][:, :],
                                s2town.ap().rearrange("(g p) c -> g p c", p=128)[gi]
                                ).then_inc(sS2r, 16)

        @block.tensor
        def _(t):
            t.wait_ge(sLd, 96)
            t.wait_ge(sLg, 128)
            t.wait_ge(sMs, 1)
            ntt = 0
            for part in range(2):
                for c in range(4):
                    if ntt >= 1:
                        t.wait_ge(sTD, 2 * ntt)
                    t.matmul(psS0[0:32, 0:512], g1_sb[:, part * 128 + c * 32: part * 128 + (c + 1) * 32],
                             e1_sb[:, :], start=True, stop=True).then_inc(sTT, 1)
                    t.matmul(psS1[0:16, 0:512], g2_sb[:, part * 64 + c * 16: part * 64 + (c + 1) * 16],
                             e3_sb[:, :], start=True, stop=True).then_inc(sTT, 1)
                    t.wait_ge(sTD, 2 * ntt + 2)
                    for j in range(4):
                        t.matmul(psS0[0:128, 0:512], e2_sb[:, j * 128:(j + 1) * 128],
                                 r1_sb[:, :], start=True, stop=True).then_inc(sTT, 1)
                        t.matmul(psS1[0:128, 0:512], e4_sb[:, j * 128:(j + 1) * 128],
                                 s1_sb[:, :], start=True, stop=True).then_inc(sTT, 1)
                        t.wait_ge(sTD, 2 * ntt + 2 + 2 * (j + 1))
                    ntt += 5
            for qt in range(NQT):
                t.wait_ge(sQt, 16 * (qt + 1))
                if qt >= 1:
                    t.wait_ge(sTe, 32 * qt)
                for kb in range(KBH[qt]):
                    e = CKB[qt] + kb
                    t.wait_ge(sKt, 16 * (e + 1))
                    if e >= 2:
                        t.wait_ge(sP, e - 1)
                    for ic in range(8):
                        m = t.matmul(psS[e % 2][0:128, 0:256],
                                     kb_sb[e % 2][:, ic * 128:(ic + 1) * 128],
                                     q_sb[:, ic * 256:(ic + 1) * 256],
                                     start=(ic == 0), stop=(ic == 7))
                        if ic == 7:
                            m.then_inc(sS, 1)
                    crossing = kb >= KBH[qt] - 2
                    if crossing:
                        cidx = 2 * qt + (kb - (KBH[qt] - 2))
                        t.wait_ge(sMk, cidx + 1)
                        src = pt2[cidx % 2]
                    else:
                        t.wait_ge(sP, e + 1)
                        src = ptb[e % 2]
                    for qsub in range(2):
                        if kb == 0:
                            gi = qt * 2 + qsub
                            if gi >= 2:
                                t.wait_ge(sFE, 8 * (gi - 1))
                        for n in range(2):
                            t.matmul(psPV[qsub][0:128, n * 512:(n + 1) * 512],
                                     src[:, qsub * 128:(qsub + 1) * 128],
                                     v_tm[:, kb * 1025 + 1 + n * 512: kb * 1025 + 1 + (n + 1) * 512],
                                     start=(kb == 0), stop=(kb == KBH[qt] - 1))
                        m = t.matmul(psPV[qsub][0:128, 1024:1025],
                                     src[:, qsub * 128:(qsub + 1) * 128],
                                     v_tm[:, kb * 1025: kb * 1025 + 1],
                                     start=(kb == 0), stop=(kb == KBH[qt] - 1))
                        if qsub == 1:
                            m.then_inc(sV2, 1)
                for qsub in range(2):
                    gi = qt * 2 + qsub
                    t.wait_ge(sNm, 2 * gi + 1)
                    for dc in range(8):
                        j = 32 * qt + qsub * 16 + dc
                        if j >= 2:
                            t.wait_ge(sTe, j - 1)
                        t.transpose(psS[j % 2][0:128, 0:128],
                                    s3tm[:, dc * 128:(dc + 1) * 128],
                                    id_sb[:, :]).then_inc(sTr, 1)
                    t.wait_ge(sCb, gi + 1)
                    for og in range(8):
                        if og >= 2:
                            t.wait_ge(sFE, 8 * gi + og - 1)
                        reg = psPV[qsub][0:128, (og % 2) * 512:(og % 2) * 512 + 128]
                        for ic in range(8):
                            if og < 4:
                                w = wrt_bf if ic < 4 else witn_bf
                                oc = og
                            else:
                                w = wit_bf if ic < 4 else wrt_bf
                                oc = og - 4
                            m = t.matmul(reg, w[:, (ic % 4) * 512 + oc * 128: (ic % 4) * 512 + (oc + 1) * 128],
                                         s3f_bf[:, ic * 128:(ic + 1) * 128],
                                         start=(ic == 0), stop=(ic == 7))
                            if ic == 7:
                                m.then_inc(sFF, 1)
                    t.wait_ge(sY4, 8 * (gi + 1))
                    for dc in range(8):
                        j = 32 * qt + qsub * 16 + 8 + dc
                        if j >= 2:
                            t.wait_ge(sTe, j - 1)
                        t.transpose(psS[j % 2][0:128, 0:128],
                                    y4[:, dc * 128:(dc + 1) * 128],
                                    id_sb[:, :]).then_inc(sTr, 1)

        @block.scalar
        def _(s):
            for qt in range(NQT):
                for kb in range(KBH[qt]):
                    e = CKB[qt] + kb
                    s.wait_ge(sS, e + 1)
                    if e >= 2:
                        s.wait_ge(sV2, e - 1)
                    s.activation(ptb[e % 2][:, :], psS[e % 2][0:128, 0:256],
                                 AF.Exp).then_inc(sP, 1)
                for qsub in range(2):
                    gi = qt * 2 + qsub
                    s.wait_ge(sVr, 4 * gi + 1)
                    s.activation(stat[:, 6:7], stat[:, 4:5], AF.Ln,
                                 bias=eps_sb[:, 0:1]).then_inc(sAc, 1)
                    s.wait_ge(sAc, 2 * gi + 1)
                    s.activation(rstd[:, 0:1], stat[:, 6:7], AF.Exp, scale=-0.5).then_inc(sLn, 1)
                    s.wait_ge(sVr, 4 * gi + 3)
                    s.activation(stat[:, 7:8], stat[:, 5:6], AF.Ln,
                                 bias=eps_sb[:, 0:1]).then_inc(sAc, 1)
                    s.wait_ge(sAc, 2 * gi + 2)
                    s.activation(rstd[:, 1:2], stat[:, 7:8], AF.Exp, scale=-0.5).then_inc(sLn, 1)

        @block.vector
        def _(v):
            v.memset(eps_sb[:, :], EPS).then_inc(sMs, 1)
            v.wait_ge(sMs, 1)
            v.wait_ge(sLd, 96)
            v.wait_ge(sLg, 128)
            nDv = 0

            def dv(inst):
                nonlocal nDv
                inst.then_inc(sDv, 1)
                nDv += 1
                v.wait_ge(sDv, nDv)

            ntt = 0
            for part in range(2):
                wdst = [wrt_bf, wit_bf][part]
                for c in range(4):
                    v.wait_ge(sTT, 2 * ntt + 2)
                    v.tensor_copy(r1_sb[:, :], psS0[0:32, 0:512]).then_inc(sTD, 1)
                    v.tensor_copy(s1_sb[:, :], psS1[0:16, 0:512]).then_inc(sTD, 1)
                    for j in range(4):
                        v.wait_ge(sTT, 2 * ntt + 2 + 2 * (j + 1))
                        dv(v.tensor_copy(aex_sb, psS0[0:128, 0:512]))
                        dv(v.tensor_copy(bex_sb, psS1[0:128, 0:512]))
                        v.sem_inc(sTD, 2)
                        dv(v.tensor_tensor(prod_sb, aex_sb, bex_sb, ALU.mult))
                        if c == 0:
                            dv(v.tensor_copy(wacc_j[j], prod_sb))
                        else:
                            dv(v.tensor_tensor(wacc_j[j], wacc_j[j], prod_sb, ALU.add))
                        if c == 3:
                            dv(v.tensor_copy(wdst[:, j * 512:(j + 1) * 512], wacc_j[j]))
                    ntt += 5
            dv(v.tensor_scalar(witn_bf[:, :], wit_bf[:, :], -1.0, None, ALU.mult))

            for qt in range(NQT):
                for ci in range(2):
                    kb = KBH[qt] - 2 + ci
                    e = CKB[qt] + kb
                    cidx = 2 * qt + ci
                    v.wait_ge(sP, e + 1)
                    if cidx >= 2:
                        v.wait_ge(sV2, e - 1)
                    v.tensor_tensor(pt2[cidx % 2][:, :], ptb[e % 2][:, :],
                                    msk[ci][:, :], ALU.mult).then_inc(sMk, 1)
                for qsub in range(2):
                    gi = qt * 2 + qsub
                    v.wait_ge(sV2, CKB[qt + 1])
                    dv(v.reciprocal(rg[:, 0:1], psPV[qsub][0:128, 1024:1025]))
                    if gi >= 1:
                        v.wait_ge(sNm, 2 * gi - 1)
                    v.tensor_scalar(attn_q[:, :], psPV[qsub][0:128, 0:1024],
                                    rg[:, 0:1], None, ALU.mult).then_inc(sE, 1)
                    v.wait_ge(sE, gi + 1)
                    v.wait_ge(sS2r, 16 * (gi + 1))
                    v.tensor_tensor(attn_q[:, :], attn_q[:, :], s2_t[gi % 2][:, :],
                                    ALU.add).then_inc(sY3, 1)
                    v.wait_ge(sY3, gi + 1)
                    dv(v.reduce_sum(stat[:, 0:1], attn_q[:, :], AX.X))
                    dv(v.tensor_tensor(y2b[:, :], attn_q[:, :], attn_q[:, :], ALU.mult))
                    dv(v.reduce_sum(stat[:, 1:2], y2b[:, :], AX.X))
                    dv(v.tensor_scalar(stat[:, 2:3], stat[:, 0:1], 1.0 / D, None, ALU.mult))
                    dv(v.tensor_tensor(stat[:, 3:4], stat[:, 2:3], stat[:, 2:3], ALU.mult))
                    dv(v.tensor_scalar(stat[:, 4:5], stat[:, 1:2], 1.0 / D, None, ALU.mult))
                    v.tensor_tensor(stat[:, 4:5], stat[:, 4:5], stat[:, 3:4],
                                    ALU.subtract).then_inc(sVr, 1)
                    v.wait_ge(sLn, 2 * gi + 1)
                    v.scalar_tensor_tensor(nmr[:, 0:1], stat[:, 2:3], -1.0, rstd[:, 0:1],
                                           ALU.mult, ALU.mult).then_inc(sVr, 1)
                    v.wait_ge(sVr, 4 * gi + 2)
                    if gi >= 1:
                        v.wait_ge(sTr, 16 * gi - 8)
                    v.tensor_scalar(s3tm[:, :], attn_q[:, :], rstd[:, 0:1], nmr[:, 0:1],
                                    ALU.mult, ALU.add).then_inc(sNm, 1)
                    for dc in range(8):
                        j = 32 * qt + qsub * 16 + dc
                        v.wait_ge(sTr, j + 1)
                        v.tensor_copy(s3f[:, dc * 128:(dc + 1) * 128],
                                      psS[j % 2][0:128, 0:128]).then_inc(sTe, 1)
                        v.wait_ge(sTe, j + 1)
                    v.tensor_copy(s3f_bf[:, :], s3f[:, :]).then_inc(sCb, 1)
                    for og in range(8):
                        v.wait_ge(sFF, 8 * gi + og + 1)
                        b_t = br_sb if og < 4 else bi_sb
                        oc = og % 4
                        ocg = og
                        dv(v.tensor_scalar(tb4[:, :],
                                           psPV[qsub][0:128, (og % 2) * 512:(og % 2) * 512 + 128],
                                           b_t[:, oc:oc + 1], None, ALU.add))
                        m = v.tensor_tensor(y4[:, ocg * 128:(ocg + 1) * 128], tb4[:, :],
                                            s3f[:, ocg * 128:(ocg + 1) * 128], ALU.add)
                        m.then_inc(sFE, 1)
                        v.wait_ge(sFE, 8 * gi + og + 1)
                        v.sem_inc(sY4, 1)
                    for dc in range(8):
                        j = 32 * qt + qsub * 16 + 8 + dc
                        v.wait_ge(sTr, j + 1)
                        v.tensor_copy(y4tm[:, dc * 128:(dc + 1) * 128],
                                      psS[j % 2][0:128, 0:128]).then_inc(sTe, 1)
                        v.wait_ge(sTe, j + 1)
                    dv(v.reduce_sum(stat[:, 0:1], y4tm[:, :], AX.X))
                    dv(v.tensor_tensor(y2b[:, :], y4tm[:, :], y4tm[:, :], ALU.mult))
                    dv(v.reduce_sum(stat[:, 1:2], y2b[:, :], AX.X))
                    dv(v.tensor_scalar(stat[:, 2:3], stat[:, 0:1], 1.0 / D, None, ALU.mult))
                    dv(v.tensor_tensor(stat[:, 3:4], stat[:, 2:3], stat[:, 2:3], ALU.mult))
                    dv(v.tensor_scalar(stat[:, 5:6], stat[:, 1:2], 1.0 / D, None, ALU.mult))
                    v.tensor_tensor(stat[:, 5:6], stat[:, 5:6], stat[:, 3:4],
                                    ALU.subtract).then_inc(sVr, 1)
                    v.wait_ge(sLn, 2 * gi + 2)
                    v.scalar_tensor_tensor(nmr[:, 1:2], stat[:, 2:3], -1.0, rstd[:, 1:2],
                                           ALU.mult, ALU.mult).then_inc(sVr, 1)
                    v.wait_ge(sVr, 4 * gi + 4)
                    if gi >= 2:
                        v.wait_ge(sO, 16 * (gi - 1))
                    v.tensor_scalar(o_t[gi % 2][:, :], y4tm[:, :], rstd[:, 1:2], nmr[:, 1:2],
                                    ALU.mult, ALU.add).then_inc(sNm, 1)

    return nc


def _tt_consts():
    E1 = np.zeros((32, 512), np.float32)
    for b in range(32):
        E1[b, b * 16:(b + 1) * 16] = 1.0
    E2 = np.zeros((32, 4 * 128), np.float32)
    for j in range(4):
        for idx in range(128):
            E2[8 * j + idx // 16, j * 128 + idx] = 1.0
    E3 = np.zeros((16, 512), np.float32)
    for b in range(32):
        for e in range(16):
            E3[e, b * 16 + e] = 1.0
    E4 = np.zeros((16, 4 * 128), np.float32)
    for j in range(4):
        for idx in range(128):
            E4[idx % 16, j * 128 + idx] = 1.0
    M0 = np.zeros((128, 256), np.float32)
    M1 = np.zeros((128, 256), np.float32)
    for kk in range(128):
        M0[kk, kk:] = 1.0
        M1[kk, kk + 128:] = 1.0
    return E1, E2, E3, E4, M0.astype(ml_dtypes.bfloat16), M1.astype(ml_dtypes.bfloat16)


def kernel_unfused(L0_attn_x, L0_attn_Wq, L0_attn_bq, L0_attn_Wk, L0_attn_bk, L0_attn_Wv,
           L0_attn_bv, L0_attn_gamma, L0_ffn_W_r, L0_ffn_W_k, L0_ffn_W_v,
           L0_ffn_mu_r, L0_ffn_mu_k, L1_attn_w_q, L1_attn_w_k, L1_attn_w_v,
           L1_ffn_G1_R, L1_ffn_G2_R, L1_ffn_G1_I, L1_ffn_G2_I,
           L1_ffn_bias_real, L1_ffn_bias_imag):
    f32 = np.float32
    asc = np.ascontiguousarray
    x = np.asarray(L0_attn_x, f32)
    ident = np.eye(128, dtype=f32)
    gam = np.full((128, 1), np.asarray(L0_attn_gamma, f32).reshape(-1)[0], f32)
    bv_bc = np.tile(np.asarray(L0_attn_bv, f32)[None, :], (128, 1))
    wqT = asc(np.asarray(L0_attn_Wq, f32).T)
    wkT = asc(np.asarray(L0_attn_Wk, f32).T)
    wvT = asc(np.asarray(L0_attn_Wv, f32).T)
    bq = asc(np.asarray(L0_attn_bq, f32)[:, None])
    bk = asc(np.asarray(L0_attn_bk, f32)[:, None])
    wrT = asc(np.asarray(L0_ffn_W_r, f32).T)
    wkT2 = asc(np.asarray(L0_ffn_W_k, f32).T)
    wvT2 = asc(np.asarray(L0_ffn_W_v, f32).T)
    mur = asc(np.asarray(L0_ffn_mu_r, f32).reshape(8, 128).T)
    muk = asc(np.asarray(L0_ffn_mu_k, f32).reshape(8, 128).T)
    wq1 = asc(np.asarray(L1_attn_w_q, f32))
    wk1 = asc(np.asarray(L1_attn_w_k, f32))
    wv1 = asc(np.asarray(L1_attn_w_v, f32))
    g1r = asc(np.asarray(L1_ffn_G1_R, f32)[0].reshape(32, 128))
    g1i = asc(np.asarray(L1_ffn_G1_I, f32)[0].reshape(32, 128))
    g2r = asc(np.asarray(L1_ffn_G2_R, f32)[:, :, 0, :].transpose(1, 0, 2).reshape(16, 64))
    g2i = asc(np.asarray(L1_ffn_G2_I, f32)[:, :, 0, :].transpose(1, 0, 2).reshape(16, 64))
    brc = asc(np.asarray(L1_ffn_bias_real, f32).reshape(4, 128).T)
    bic = asc(np.asarray(L1_ffn_bias_imag, f32).reshape(4, 128).T)
    E1, E2, E3, E4, M0, M1 = _tt_consts()

    cores = list(range(NC))
    bs = [c // 2 for c in cores]
    hs = [c % 2 for c in cores]
    xT = [asc(x[b].T) for b in range(B)]

    nc = build_p1a()
    r1 = run_bass_kernel_spmd(nc, [dict(x_fm=xT[bs[c]], wqT=wqT, wkT=wkT, wvT=wvT,
                                        bq=bq, bk=bk) for c in cores],
                              core_ids=cores)
    k8 = [r1.results[c]["k_out"] for c in cores]
    q8 = [r1.results[c]["q_out"] for c in cores]
    v8 = [r1.results[c]["v_out"] for c in cores]

    nc = build_p1b()
    r2 = run_bass_kernel_spmd(nc, [dict(
        k_scr=k8[c], q_scr=asc(q8[c][:, hs[c] * OWN:(hs[c] + 1) * OWN]), v_scr=v8[c],
        x_nat=asc(x[bs[c], hs[c] * OWN:(hs[c] + 1) * OWN]),
        gam=gam, bv_bc=bv_bc, ident=ident) for c in cores], core_ids=cores)
    s1 = [r2.results[c]["s1"] for c in cores]

    nc = build_p2()
    in3 = []
    for c in cores:
        prev = (np.zeros((D, 1), f32) if hs[c] == 0
                else asc(s1[c - 1][:, -1:]))
        in3.append(dict(s1c=asc(np.concatenate([prev, s1[c]], axis=1)),
                        wrT=wrT, wkT2=wkT2, wvT2=wvT2, mur=mur, muk=muk, ident=ident))
    r3 = run_bass_kernel_spmd(nc, in3, core_ids=cores)
    s2f = [r3.results[c]["s2f"] for c in cores]
    s2t = [r3.results[c]["s2t"] for c in cores]

    nc = build_p4a(T)
    in4 = []
    for c in cores:
        if hs[c] == 0:
            s2full = np.concatenate([np.zeros((D, OWN), f32), s2f[c]], axis=1)
        else:
            s2full = np.concatenate([s2f[c - 1], s2f[c]], axis=1)
        in4.append(dict(s2full=asc(s2full), wq1=wq1, wk1=wk1, wv1=wv1))
    r4 = run_bass_kernel_spmd(nc, in4, core_ids=cores)
    q1o = [r4.results[c]["q1_out"] for c in cores]
    k1o = [r4.results[c]["k1_out"] for c in cores]
    v1o = [r4.results[c]["v1_out"] for c in cores]

    out = np.empty((B, T, D), f32)
    for h in range(2):
        R = 4096 if h == 0 else 8192
        qoff = 0 if h == 0 else 4096
        sub = [c for c in cores if hs[c] == h]
        nc = build_p4b(R, qoff)
        inb = []
        for c in sub:
            k1c = asc(k1o[c][:, OWN:]) if h == 0 else k1o[c]
            v1c = asc(v1o[c][:, 32 * 1025:]) if h == 0 else v1o[c]
            inb.append(dict(q1=q1o[c], k1=k1c, v1=v1c, s2town=s2t[c],
                            g1r=g1r, g1i=g1i, g2r=g2r, g2i=g2i,
                            e1m=E1, e2m=E2, e3m=E3, e4m=E4,
                            brc=brc, bic=bic, m0=M0, m1=M1, ident=ident))
        rb = run_bass_kernel_spmd(nc, inb, core_ids=list(range(len(sub))))
        for i, c in enumerate(sub):
            out[bs[c], hs[c] * OWN:(hs[c] + 1) * OWN, :] = rb.results[i]["o_nat"]
    return out


def build_fused():
    nc = _nc()
    R = T
    NKR = R // 128
    NQT = 16
    x_fm = nc.dram_tensor("x_fm", [D, T], F32, kind="ExternalInput")
    wqT = nc.dram_tensor("wqT", [D, 128], F32, kind="ExternalInput")
    wkT = nc.dram_tensor("wkT", [D, 128], F32, kind="ExternalInput")
    wvT = nc.dram_tensor("wvT", [D, D], F32, kind="ExternalInput")
    bq = nc.dram_tensor("bq", [128, 1], F32, kind="ExternalInput")
    bk = nc.dram_tensor("bk", [128, 1], F32, kind="ExternalInput")
    gam = nc.dram_tensor("gam", [128, 1], F32, kind="ExternalInput")
    bv_bc = nc.dram_tensor("bv_bc", [128, 8], F32, kind="ExternalInput")
    ident = nc.dram_tensor("ident", [128, 128], F32, kind="ExternalInput")
    wrT = nc.dram_tensor("wrT", [D, D], F32, kind="ExternalInput")
    wkT2 = nc.dram_tensor("wkT2", [D, D], F32, kind="ExternalInput")
    wvT2 = nc.dram_tensor("wvT2", [D, D], F32, kind="ExternalInput")
    mur = nc.dram_tensor("mur", [128, 8], F32, kind="ExternalInput")
    muk = nc.dram_tensor("muk", [128, 8], F32, kind="ExternalInput")
    wq1 = nc.dram_tensor("wq1", [D, D], F32, kind="ExternalInput")
    wk1 = nc.dram_tensor("wk1", [D, D], F32, kind="ExternalInput")
    wv1 = nc.dram_tensor("wv1", [D, D], F32, kind="ExternalInput")
    g1r = nc.dram_tensor("g1r", [32, 4 * 32], F32, kind="ExternalInput")
    g1i = nc.dram_tensor("g1i", [32, 4 * 32], F32, kind="ExternalInput")
    g2r = nc.dram_tensor("g2r", [16, 4 * 16], F32, kind="ExternalInput")
    g2i = nc.dram_tensor("g2i", [16, 4 * 16], F32, kind="ExternalInput")
    e1m = nc.dram_tensor("e1m", [32, 512], F32, kind="ExternalInput")
    e2m = nc.dram_tensor("e2m", [32, 4 * 128], F32, kind="ExternalInput")
    e3m = nc.dram_tensor("e3m", [16, 512], F32, kind="ExternalInput")
    e4m = nc.dram_tensor("e4m", [16, 4 * 128], F32, kind="ExternalInput")
    brc = nc.dram_tensor("brc", [128, 4], F32, kind="ExternalInput")
    bic = nc.dram_tensor("bic", [128, 4], F32, kind="ExternalInput")
    m0 = nc.dram_tensor("m0", [128, 256], BF16, kind="ExternalInput")
    m1 = nc.dram_tensor("m1", [128, 256], BF16, kind="ExternalInput")
    gate64 = nc.dram_tensor("gate64", [128, 64], F32, kind="ExternalInput")
    gprev = nc.dram_tensor("gprev", [128, 1], F32, kind="ExternalInput")
    hv = nc.dram_tensor("hv", [1, 1], mybir.dt.int32, kind="ExternalInput")
    o_nat = nc.dram_tensor("o_nat", [OWN, D], F32, kind="ExternalOutput")
    s1d = nc.dram_tensor("s1d", [D, 128 + OWN], F32, kind="Internal")
    s2own = nc.dram_tensor("s2own", [D, OWN], F32, kind="Internal")
    gjunk = nc.dram_tensor("gjunk", [D, OWN], F32, kind="Internal")
    prevsh = nc.dram_tensor("prevsh", [2 * D, 1], F32, kind="Internal", addr_space="Shared")
    s2sh = nc.dram_tensor("s2sh", [2 * D, OWN], F32, kind="Internal", addr_space="Shared")
    flg1 = nc.dram_tensor("flg1", [2, 8], mybir.dt.int32, kind="Internal", addr_space="Shared")
    flg2 = nc.dram_tensor("flg2", [2, 8], mybir.dt.int32, kind="Internal", addr_space="Shared")
    s2t_d = nc.dram_tensor("s2t_d", [OWN, D], F32, kind="Internal")
    q1_d = nc.dram_tensor("q1_d", [D, OWN], BF16, kind="Internal")
    q0_d = nc.dram_tensor("q0_d", [128, OWN], BF16, kind="Internal")
    k1_d = nc.dram_tensor("k1_d", [D, R], BF16, kind="Internal")

    KBH = [32 + 2 * qt + 2 for qt in range(NQT)]
    CKB = [0]
    for qt in range(NQT):
        CKB.append(CKB[-1] + KBH[qt])

    with contextlib.ExitStack() as outer:
        _e = outer.enter_context
        S = {}
        for name in ("aW aX aK aKE aV aVE aQp aQE aB4 aMs "
                     "bQp bQm bXt bS bT bL bP bV2 bE bYa bVr bLn bNm bTr bTe bO bDv bAc bY2 bSt bPL bYs "
                     "cP1 cPrv cPg "
                     "cW cIn cMM cMx cDv cAc cSg cRl cOu cMu cSq cYc cLn cVr cNm cTe cOF cOT cMs "
                     "cCC "
                     "dX dK dKE dQg dQE dV dVE dOK dOQ "
                     "eQt eKt eS2r eS eP eMk eV2 eE eY3 eVr eLn eNm eTr eTe eCb eFF eFE eY4 eO eDv eAc eLg eTT eTD eLd eMs zF x1 p1 x2 p2"
                     ).split():
            S[name] = _e(nc.semaphore("s_" + name))

        st_ab = outer.enter_context(contextlib.ExitStack())
        hv_sb = outer.enter_context(nc.sbuf_tensor("hv_sb", [1, 1], mybir.dt.int32))
        zi_sb = outer.enter_context(nc.sbuf_tensor("zi_sb", [1, 8], mybir.dt.int32))
        one_i = outer.enter_context(nc.sbuf_tensor("one_i", [1, 8], mybir.dt.int32))
        fl_sb = outer.enter_context(nc.sbuf_tensor("fl_sb", [1, 8], mybir.dt.int32))
        k_sb = st_ab.enter_context(nc.sbuf_tensor("k_sb", [128, T], BF16))
        v_tm = st_ab.enter_context(nc.sbuf_tensor("v_tm", [128, 64 * 1032], BF16))
        stQ = outer.enter_context(contextlib.ExitStack())
        q_sb = stQ.enter_context(nc.sbuf_tensor("q_sb", [128, OWN], BF16))

        with contextlib.ExitStack() as stA:
            _a = stA.enter_context
            wq_sb = _a(nc.sbuf_tensor("wq_sb", [128, 8 * 128], BF16))
            wk_sb = _a(nc.sbuf_tensor("wk_sb", [128, 8 * 128], BF16))
            wv_sb = _a(nc.sbuf_tensor("wv_sb", [128, 8 * 1024], BF16))
            bq_sb = _a(nc.sbuf_tensor("bq_sb", [128, 1], F32))
            bq4 = _a(nc.sbuf_tensor("bq4", [128, 1], F32))
            bk_sb = _a(nc.sbuf_tensor("bk_sb", [128, 1], F32))
            x0 = _a(nc.sbuf_tensor("x0a", [128, 8 * 512], BF16))
            x1 = _a(nc.sbuf_tensor("x1a", [128, 8 * 512], BF16))
            psK0 = _a(nc.psum_tensor("psK0a", [128, 512], F32))
            psK1 = _a(nc.psum_tensor("psK1a", [128, 512], F32))
            psV0 = _a(nc.psum_tensor("psV0a", [128, 1024], F32))
            psV1 = _a(nc.psum_tensor("psV1a", [128, 1024], F32))
            psQ = _a(nc.psum_tensor("psQa", [128, 512], F32))
            blockA = stA.enter_context(nc.Block())
            xb = [x0, x1]
            psK = [psK0, psK1]
            psV = [psV0, psV1]
            x_r = x_fm.ap().rearrange("(c p) t -> p c t", p=128)
            NT = 16

            @blockA.gpsimd
            def _(g):
                g.memset(zi_sb[:, :], 0)
                g.memset(one_i[:, :], 1)
                g.memset(fl_sb[:, :], 0).then_inc(S["zF"], 1)
                g.wait_ge(S["zF"], 1)
                g.dma_start(hv_sb[:, :], hv.ap()).then_inc(S["zF"], 16)
                g.wait_ge(S["zF"], 17)
                g.dma_start(flg1.ap()[0:1, :], zi_sb[:, :]).then_inc(S["zF"], 16)
                g.wait_ge(S["zF"], 33)
                g.dma_start(flg1.ap()[1:2, :], zi_sb[:, :]).then_inc(S["zF"], 16)
                g.wait_ge(S["zF"], 49)
                g.dma_start(flg2.ap()[0:1, :], zi_sb[:, :]).then_inc(S["zF"], 16)
                g.wait_ge(S["zF"], 65)
                g.dma_start(flg2.ap()[1:2, :], zi_sb[:, :]).then_inc(S["zF"], 16)
                g.wait_ge(S["zF"], 81)
                g.dma_start(wq_sb.ap().rearrange("p (c m) -> p c m", m=128),
                            wqT.ap().rearrange("(c p) m -> p c m", p=128)).then_inc(S["aW"], 16)
                g.dma_start(wk_sb.ap().rearrange("p (c m) -> p c m", m=128),
                            wkT.ap().rearrange("(c p) m -> p c m", p=128)).then_inc(S["aW"], 16)
                g.dma_start(wv_sb.ap().rearrange("p (c m) -> p c m", m=1024),
                            wvT.ap().rearrange("(c p) m -> p c m", p=128)).then_inc(S["aW"], 16)
                g.dma_start(bq_sb[:, :], bq.ap()).then_inc(S["aW"], 16)
                g.dma_start(bk_sb[:, :], bk.ap()).then_inc(S["aW"], 16)
                g.wait_ge(S["aW"], 80)
                for ti in range(NT):
                    if ti >= 2:
                        g.wait_ge(S["aV"], 4 * (ti - 1))
                    g.wait_ge(S["aX"], 16 * ti)
                    g.dma_start(
                        xb[ti % 2].ap().rearrange("p (c t) -> p c t", t=512),
                        x_r[:, :, ti * 512:(ti + 1) * 512],
                    ).then_inc(S["aX"], 16)

            @blockA.tensor
            def _(t):
                t.wait_ge(S["aW"], 80)
                for ti in range(NT):
                    t.wait_ge(S["aX"], 16 * (ti + 1))
                    x = xb[ti % 2]
                    if ti >= 2:
                        t.wait_ge(S["aKE"], ti - 1)
                    for dc in range(8):
                        m = t.matmul(psK[ti % 2][0:128, 0:512],
                                     wk_sb[:, dc * 128:(dc + 1) * 128],
                                     x[:, dc * 512:(dc + 1) * 512],
                                     start=(dc == 0), stop=(dc == 7))
                        if dc == 7:
                            m.then_inc(S["aK"], 1)
                    if ti >= 8:
                        qi = ti - 8
                        if qi >= 1:
                            t.wait_ge(S["aQE"], qi)
                        for dc in range(8):
                            m = t.matmul(psQ[0:128, 0:512],
                                         wq_sb[:, dc * 128:(dc + 1) * 128],
                                         x[:, dc * 512:(dc + 1) * 512],
                                         start=(dc == 0), stop=(dc == 7))
                            if dc == 7:
                                m.then_inc(S["aQp"], 1)
                    for ci in range(4):
                        vi = ti * 4 + ci
                        if vi >= 2:
                            t.wait_ge(S["aVE"], vi - 1)
                        for dc in range(8):
                            for n in range(2):
                                m = t.matmul(
                                    psV[vi % 2][0:128, n * 512:(n + 1) * 512],
                                    x[:, dc * 512 + ci * 128: dc * 512 + ci * 128 + 128],
                                    wv_sb[:, dc * 1024 + n * 512: dc * 1024 + (n + 1) * 512],
                                    start=(dc == 0), stop=(dc == 7))
                                if dc == 7 and n == 1:
                                    m.then_inc(S["aV"], 1)

            @blockA.scalar
            def _(s):
                s.wait_ge(S["aW"], 80)
                s.mul(bq4[:, :], bq_sb[:, :], 0.25).then_inc(S["aB4"], 1)

            @blockA.sync
            def _(sy):
                sy.wait_ge(S["aQE"], 8)
                sy.dma_start(q0_d.ap(), q_sb[:, :]).then_inc(S["aW"], 16)

            @blockA.vector
            def _(v):
                v.memset(v_tm[:, 0:33024], 1.0)
                v.memset(v_tm[:, 33024:66048], 1.0).then_inc(S["aMs"], 1)
                v.wait_ge(S["aMs"], 1)
                v.wait_ge(S["aB4"], 1)
                for ti in range(NT):
                    v.wait_ge(S["aK"], ti + 1)
                    v.tensor_scalar(k_sb[:, ti * 512:(ti + 1) * 512], psK[ti % 2][0:128, 0:512],
                                    bk_sb[:, 0:1], None, ALU.add).then_inc(S["aKE"], 1)
                    if ti >= 8:
                        qi = ti - 8
                        v.wait_ge(S["aQp"], qi + 1)
                        v.tensor_scalar(q_sb[:, qi * 512:(qi + 1) * 512], psQ[0:128, 0:512],
                                        0.25, bq4[:, 0:1], ALU.mult, ALU.add).then_inc(S["aQE"], 1)
                    for ci in range(4):
                        vi = ti * 4 + ci
                        v.wait_ge(S["aV"], vi + 1)
                        v.tensor_copy(
                            v_tm.ap().rearrange("p (k h j) -> p k h j", h=8, j=129)[:, vi, :, 1:129],
                            psV[vi % 2].ap().rearrange("p (h j) -> p h j", j=128),
                        ).then_inc(S["aVE"], 1)

        stQ.close()

        with contextlib.ExitStack() as stB:
            _b = stB.enter_context
            qp0 = _b(nc.sbuf_tensor("qp0", [128, 512], BF16))
            qp1 = _b(nc.sbuf_tensor("qp1", [128, 512], BF16))
            attn = _b(nc.sbuf_tensor("attn", [128, 9 * 512], F32))
            tb0 = _b(nc.sbuf_tensor("tb0", [128, 512], BF16))
            tb1 = _b(nc.sbuf_tensor("tb1", [128, 512], BF16))
            tb2 = _b(nc.sbuf_tensor("tb2", [128, 512], BF16))
            pt0 = _b(nc.sbuf_tensor("pt0", [128, 512], BF16))
            pt1 = _b(nc.sbuf_tensor("pt1", [128, 512], BF16))
            pt2 = _b(nc.sbuf_tensor("pt2", [128, 512], BF16))
            pt3 = _b(nc.sbuf_tensor("pt3", [128, 512], BF16))
            bc_sb = _b(nc.sbuf_tensor("bc_sb", [128, 512], F32))
            rstd_sb = _b(nc.sbuf_tensor("rstd_sb", [128, 512], F32))
            x_c = _b(nc.sbuf_tensor("x_c", [128, 512], F32))
            ybf = _b(nc.sbuf_tensor("ybf", [128, 512], BF16))
            y2c = _b(nc.sbuf_tensor("y2c", [128, 512], BF16))
            s1c0 = _b(nc.sbuf_tensor("s1c0", [128, 512], F32))
            s1c1 = _b(nc.sbuf_tensor("s1c1", [128, 512], F32))
            rg_row = _b(nc.sbuf_tensor("rg_row", [1, 512], F32))
            mu_row = _b(nc.sbuf_tensor("mu_row", [1, 512], F32))
            w_row = _b(nc.sbuf_tensor("w_row", [1, 512], F32))
            lv_row = _b(nc.sbuf_tensor("lv_row", [1, 512], F32))
            gam_sb = _b(nc.sbuf_tensor("gam_sb", [128, 1], F32))
            gbv8 = _b(nc.sbuf_tensor("gbv8", [128, 8], F32))
            onesb = _b(nc.sbuf_tensor("onesb", [128, 1], BF16))
            onesf = _b(nc.sbuf_tensor("onesf", [1, 128], F32))
            eps_sb = _b(nc.sbuf_tensor("eps_sbB", [1, 1], F32))
            prevt = _b(nc.sbuf_tensor("prevt", [128, 8], F32))
            gprev_sb = _b(nc.sbuf_tensor("gprev_sb", [128, 1], F32))
            psS0 = _b(nc.psum_tensor("psS0b", [128, 512], F32))
            psS1 = _b(nc.psum_tensor("psS1b", [128, 512], F32))
            psS2 = _b(nc.psum_tensor("psS2b", [128, 512], F32))
            psA0 = _b(nc.psum_tensor("psA0b", [128, 512], F32))
            psA1 = _b(nc.psum_tensor("psA1b", [128, 512], F32))
            psBm = _b(nc.psum_tensor("psBmb", [128, 512], F32))
            psC0 = _b(nc.psum_tensor("psC0b", [128, 512], F32))
            psC1 = _b(nc.psum_tensor("psC1b", [128, 512], F32))
            blockB = stB.enter_context(nc.Block())
            psS = [psS0, psS1, psS2]
            psA = [psA0, psA1]
            tb = [tb0, tb1, tb2]
            pt = [pt0, pt1, pt2, pt3]
            qp = [qp0, qp1]
            s1c = [s1c0, s1c1]
            sLd, sQf, sQm, sXt = S["cP1"], S["bQp"], S["bQm"], S["bXt"]
            sS, sT, sL, sP, sPV = S["bS"], S["bT"], S["bL"], S["bP"], S["bV2"]
            sRg, sBcP, sBcE, sE = S["bTr"], S["bTe"], S["bYa"], S["bE"]
            sYa, sYp, sSt, sPL = S["bYs"], S["bY2"], S["bSt"], S["bPL"]
            sVr, sLn, sNm, sO = S["bVr"], S["bLn"], S["bNm"], S["bO"]
            sDv, sAc = S["bDv"], S["bAc"]
            xr = x_fm.ap().rearrange("(c p) t -> p c t", p=128)

            def a_sl(ch):
                s = ch % 9
                return slice(s * 512, (s + 1) * 512)

            @blockB.sync
            def _(sy):
                sy.dma_start(gam_sb[:, :], gam.ap()).then_inc(sLd, 16)
                sy.dma_start(gbv8[:, :], bv_bc.ap()).then_inc(sLd, 16)
                sy.dma_start(gprev_sb[:, :], gprev.ap()).then_inc(sLd, 16)
                items = []
                for f in range(64):
                    items.append(((max(f - 1, 0)) * 64 + 6, 0, f))
                for g in range(64):
                    items.append(((8 * (g // 8 + 1) + 2) * 64 + 15 + 4 * (g % 8), 1, g))
                items.sort()
                nQf = 0
                nO = 0
                for _key, kind, idx in items:
                    if kind == 0:
                        f = idx
                        qs, h = f // 8, f % 8
                        sy.wait_ge(S["aW"], 96)
                        sy.wait_ge(sQm, f + 1)
                        sy.wait_ge(sQf, 16 * nQf)
                        sy.dma_start(qp[f % 2][16 * h:16 * h + 16, :],
                                     q0_d.ap()[16 * h:16 * h + 16, qs * 512:(qs + 1) * 512]
                                     ).then_inc(sQf, 16)
                        nQf += 1
                    else:
                        g = idx
                        qs, dc = g // 8, g % 8
                        sy.wait_ge(sNm, g + 1)
                        sy.wait_ge(sO, 16 * nO)
                        sy.dma_start(
                            s1d.ap().rearrange("(c p) t -> p c t", p=128)[
                                :, dc:dc + 1, 128 + qs * 512: 128 + (qs + 1) * 512],
                            s1c[g % 2].ap().rearrange("p (c t) -> p c t", c=1),
                        ).then_inc(sO, 16)
                        nO += 1

            @blockB.gpsimd
            def _(g):
                for ch in range(64):
                    qs, dc = ch // 8, ch % 8
                    if ch >= 1:
                        g.wait_ge(sYa, ch)
                    g.wait_ge(sXt, 16 * ch)
                    g.dma_start(x_c.ap().rearrange("p (c t) -> p c t", c=1),
                                xr[:, dc:dc + 1, OWN + qs * 512: OWN + (qs + 1) * 512]
                                ).then_inc(sXt, 16)
                g.wait_ge(sXt, 16 * 64)
                g.wait_ge(sO, 16 * 64)
                g.wait_ge(sQf, 16 * 64)
                g.wait_ge(sLd, 48)
                hv1 = nc.values_load(hv_sb[0:1, 0:1], engines=[mybir.EngineType.Pool],
                                     min_val=0, max_val=1, skip_runtime_bounds_check=True)
                with g.register("fv1") as fv1, g.register("pc1") as pc1:
                    for bi in g.Switch(hv1, 2):
                        g.br(f"x1_{bi}")
                    for bi in range(2):
                        other = 1 - bi
                        with nc.bb(f"x1_{bi}"):
                            g.dma_start(
                                prevsh.ap().rearrange("(s p) r -> s p r", p=D)[bi],
                                s1d.ap()[:, 128 + OWN - 1:128 + OWN],
                                allow_slow_non_contiguous=True).then_inc(S["x1"], 16)
                            g.wait_ge(S["x1"], 16)
                            g.dma_start(flg1.ap()[bi:bi + 1, :], one_i[:, :]).then_inc(S["x1"], 16)
                            g.wait_ge(S["x1"], 32)
                            g.reg_mov(pc1, 0)
                            g.br(f"x1p_{bi}")
                        with nc.bb(f"x1p_{bi}"):
                            g.reg_add(pc1, pc1, 16)
                            g.dma_start(fl_sb[:, :], flg1.ap()[other:other + 1, :]).then_inc(S["p1"], 16)
                            g.wait_ge(S["p1"], pc1)
                            g.reg_load(fv1, fl_sb[0:1, 0:1])
                            g.br_lt(fv1, 1, f"x1p_{bi}", f"x1g_{bi}")
                        with nc.bb(f"x1g_{bi}"):
                            g.dma_start(
                                prevt.ap().rearrange("p (c r) -> p c r", r=1),
                                prevsh.ap().rearrange("(s p) r -> s p r", p=D)[other]
                                .rearrange("(c p) r -> p c r", p=128),
                                allow_slow_non_contiguous=True).then_inc(S["cPrv"], 16)
                            g.br("x1_done")
                    with nc.bb("x1_done"):
                        g.wait_ge(S["cPrv"], 16)
                        g.wait_ge(S["cPg"], 1)
                        g.dma_start(s1d.ap().rearrange("(c p) t -> p c t", p=128)[:, :, 127:128],
                                    prevt.ap().rearrange("p (c r) -> p c r", r=1),
                                    allow_slow_non_contiguous=True).then_inc(S["cPrv"], 16)

            @blockB.tensor
            def _(t):
                t.wait_ge(S["aVE"], 64)
                t.wait_ge(S["aKE"], 16)
                t.wait_ge(sDv, 2)
                for hg in range(67):
                    q8, r8 = divmod(hg, 8)
                    for kb in range(64):
                        i = hg * 64 + kb
                        if hg <= 63:
                            if kb == 0:
                                t.wait_ge(sQf, 16 * (hg + 1))
                            if i >= 3:
                                t.wait_ge(sL, i - 2)
                            t.matmul(psS[i % 3][0:128, 0:512],
                                     k_sb[:, kb * 128:(kb + 1) * 128],
                                     qp[hg % 2][:, :],
                                     start=True, stop=True).then_inc(sS, 1)
                        ip = i - 3
                        if 0 <= ip <= 4095 and (hg <= 63 or (hg == 64 and kb <= 2)):
                            hp, kbp = divmod(ip, 64)
                            hh = hp % 8
                            t.wait_ge(sP, ip + 1)
                            if kbp == 0 and hp >= 2:
                                t.wait_ge(sE, hp - 1)
                            t.matmul(psA[hp % 2][0:128, 0:512],
                                     v_tm[:, kbp * 1032 + hh * 129 + 1: kbp * 1032 + hh * 129 + 129],
                                     pt[ip % 4][:, :],
                                     start=(kbp == 0), stop=(kbp == 63))
                            if kbp == 0 and hp >= 2:
                                t.wait_ge(sRg, hp - 1)
                            rb = (hp % 2) * 64
                            t.matmul(psBm[rb:rb + 1, 0:512],
                                     v_tm[:, kbp * 1032 + hh * 129: kbp * 1032 + hh * 129 + 1],
                                     pt[ip % 4][:, :],
                                     start=(kbp == 0), stop=(kbp == 63)).then_inc(sPV, 1)
                        if kb == 8 and 1 <= hg <= 64:
                            ch = hg - 1
                            t.wait_ge(sRg, ch + 1)
                            if ch >= 1:
                                t.wait_ge(sBcE, ch)
                            t.matmul(psC0[0:128, 0:512], onesf[0:1, :], rg_row[0:1, :],
                                     start=True, stop=True).then_inc(sBcP, 1)
                        if kb == 8 and r8 == 2 and hg >= 10:
                            b = q8 - 1
                            t.wait_ge(sVr, 2 * b + 2)
                            t.wait_ge(sPL, 3 * b + 2)
                            t.matmul(psC1[0:128, 0:512], onesf[0:1, :], mu_row[0:1, :],
                                     start=True, stop=True).then_inc(sPL, 1)
                        if kb == 48 and 2 <= hg <= 65:
                            ch = hg - 2
                            t.wait_ge(sYp, 2 * ch + 1)
                            if ch % 8 == 0 and ch >= 8:
                                t.wait_ge(sNm, ch)
                            t.matmul(psC1[0:1, 0:512], onesb[:, 0:1], ybf[:, :],
                                     start=(ch % 8 == 0), stop=(ch % 8 == 7))
                        if kb == 50 and 2 <= hg <= 65:
                            ch = hg - 2
                            t.wait_ge(sYp, 2 * ch + 2)
                            t.matmul(psC1[64:65, 0:512], onesb[:, 0:1], y2c[:, :],
                                     start=(ch % 8 == 0), stop=(ch % 8 == 7)).then_inc(sSt, 1)
                        if kb == 62 and r8 == 1 and hg >= 9:
                            b = q8 - 1
                            t.wait_ge(sLn, b + 1)
                            t.wait_ge(sVr, 2 * b + 1)
                            t.wait_ge(sPL, 3 * b)
                            t.matmul(psC1[0:128, 0:512], onesf[0:1, :], w_row[0:1, :],
                                     start=True, stop=True).then_inc(sPL, 1)

            @blockB.scalar
            def _(s):
                s.wait_ge(sDv, 3)
                for hg in range(66):
                    for kb in range(64):
                        i = hg * 64 + kb
                        if i <= 4095 and i % 3 != 2:
                            s.wait_ge(sS, i + 1)
                            if i >= 1:
                                s.wait_ge(sT, i)
                            if i >= 3:
                                s.wait_ge(sP, i - 2)
                            s.activation(tb[i % 3][:, :], psS[i % 3][0:128, 0:512],
                                         AF.Copy, scale=0.2).then_inc(sT, 1)
                        if 1 <= i <= 4096:
                            j = i - 1
                            s.wait_ge(sL, j + 1)
                            if j >= 4:
                                s.wait_ge(sPV, j - 3)
                            s.activation(pt[j % 4][:, :], tb[j % 3][:, :],
                                         AF.Exp).then_inc(sP, 1)
                        if kb == 56 and hg % 8 == 1 and hg >= 9:
                            b = hg // 8 - 1
                            s.wait_ge(sVr, 2 * b + 1)
                            s.activation(lv_row[0:1, :], w_row[0:1, :], AF.Ln,
                                         bias=eps_sb[0:1, 0:1]).then_inc(sAc, 1)
                        if kb == 58 and hg % 8 == 1 and hg >= 9:
                            b = hg // 8 - 1
                            s.wait_ge(sAc, b + 1)
                            s.activation(w_row[0:1, :], lv_row[0:1, :], AF.Exp,
                                         scale=-0.5).then_inc(sLn, 1)

            @blockB.vector
            def _(v):
                last = [None, 0]
                cnt = {}

                def dv(op, sem):
                    pass

                def w(vv):
                    if last[0] is not None:
                        vv.wait_ge(last[0], last[1])

                def fin(instr, sem):
                    cnt[sem] = cnt.get(sem, 0) + 1
                    instr.then_inc(sem, 1)
                    last[0], last[1] = sem, cnt[sem]

                w(v); fin(v.memset(onesb[:, :], 1.0 / D), sDv)
                w(v); fin(v.memset(onesf[:, :], 1.0), sDv)
                w(v); fin(v.memset(eps_sb[:, :], EPS), sDv)
                w(v); fin(v.memset(qp0[:, :], 0.0), sQm)
                w(v); fin(v.memset(qp1[:, :], 0.0), sQm)
                v.wait_ge(sLd, 48)
                w(v); fin(v.tensor_scalar(gbv8[:, :], gbv8[:, :], gam_sb[:, 0:1], None,
                                          ALU.mult), sDv)
                for hg in range(67):
                    q8, r8 = divmod(hg, 8)
                    for kb in range(64):
                        i = hg * 64 + kb
                        if kb == 2 and 1 <= hg <= 64:
                            ch = hg - 1
                            w(v); v.wait_ge(sPV, 64 * hg)
                            rbv = (ch % 2) * 64
                            fin(v.reciprocal(rg_row[0:1, :], psBm[rbv:rbv + 1, 0:512]), sDv)
                            w(v)
                            fin(v.tensor_scalar(rg_row[0:1, :], rg_row[0:1, :],
                                                gam_sb[0:1, 0:1], None, ALU.mult), sRg)
                        if kb == 4 and r8 == 2 and hg >= 10:
                            b = q8 - 1
                            w(v); v.wait_ge(sPL, 3 * b + 1)
                            fin(v.tensor_copy(rstd_sb[:, :], psC1[0:128, 0:512]), sPL)
                        if kb == 4 and 1 <= hg <= 62:
                            f = hg + 1
                            w(v); v.wait_ge(sS, 64 * hg)
                            fin(v.memset(qp[f % 2][:, :], 0.0), sQm)
                        if r8 == 2 and hg >= 10 and 12 <= kb <= 40 and (kb - 12) % 4 == 0:
                            b = q8 - 1
                            dcn = (kb - 12) // 4
                            gch = 8 * b + dcn
                            w(v); v.wait_ge(sPL, 3 * b + 2)
                            fin(v.tensor_tensor(attn[:, a_sl(gch)], attn[:, a_sl(gch)],
                                                rstd_sb[:, :], ALU.mult), sDv)
                        if r8 == 2 and hg >= 10 and 14 <= kb <= 42 and (kb - 14) % 4 == 0:
                            b = q8 - 1
                            dcn = (kb - 14) // 4
                            gch = 8 * b + dcn
                            w(v); v.wait_ge(sPL, 3 * b + 3)
                            if gch >= 2:
                                v.wait_ge(sO, 16 * (gch - 1))
                            fin(v.tensor_tensor(s1c[gch % 2][:, :], attn[:, a_sl(gch)],
                                                psC1[0:128, 0:512], ALU.subtract), sNm)
                        if kb == 20 and 1 <= hg <= 64:
                            ch = hg - 1
                            w(v); v.wait_ge(sBcP, ch + 1)
                            fin(v.tensor_copy(bc_sb[:, :], psC0[0:128, 0:512]), sBcE)
                        if kb == 48 and 1 <= hg <= 64:
                            ch = hg - 1
                            w(v)
                            if ch >= 9:
                                v.wait_ge(sNm, ch - 8)
                            fin(v.tensor_tensor(attn[:, a_sl(ch)], psA[ch % 2][0:128, 0:512],
                                                bc_sb[:, :], ALU.mult), sDv)
                        if kb == 50 and 1 <= hg <= 64:
                            ch = hg - 1
                            w(v)
                            fin(v.tensor_scalar(attn[:, a_sl(ch)], attn[:, a_sl(ch)],
                                                gbv8[:, (ch % 8):(ch % 8) + 1], None,
                                                ALU.add), sE)
                        if kb == 52 and 1 <= hg <= 64:
                            ch = hg - 1
                            w(v); v.wait_ge(sXt, 16 * (ch + 1))
                            fin(v.scalar_tensor_tensor(attn[:, a_sl(ch)], x_c[:, :], 2.0,
                                                       attn[:, a_sl(ch)], ALU.mult,
                                                       ALU.add), sYa)
                        if kb == 52 and r8 == 1 and hg >= 9:
                            b = q8 - 1
                            w(v); v.wait_ge(sSt, 8 * (b + 1))
                            fin(v.tensor_copy(mu_row[0:1, :], psC1[0:1, 0:512]), sDv)
                        if kb == 54 and 1 <= hg <= 64:
                            ch = hg - 1
                            w(v)
                            if ch >= 1:
                                v.wait_ge(sSt, ch)
                            fin(v.tensor_copy(ybf[:, :], attn[:, a_sl(ch)]), sYp)
                        if kb == 54 and r8 == 1 and hg >= 9:
                            w(v)
                            fin(v.tensor_tensor(w_row[0:1, :], mu_row[0:1, :],
                                                mu_row[0:1, :], ALU.mult), sDv)
                            w(v)
                            fin(v.tensor_tensor(w_row[0:1, :], psC1[64:65, 0:512],
                                                w_row[0:1, :], ALU.subtract), sVr)
                        if kb == 56 and 1 <= hg <= 64:
                            w(v)
                            fin(v.tensor_tensor(y2c[:, :], ybf[:, :], ybf[:, :],
                                                ALU.mult), sYp)
                        if kb == 62 and r8 == 1 and hg >= 9:
                            b = q8 - 1
                            w(v); v.wait_ge(sLn, b + 1)
                            fin(v.tensor_tensor(mu_row[0:1, :], mu_row[0:1, :],
                                                w_row[0:1, :], ALU.mult), sVr)
                        if i <= 4095 and i % 3 == 2:
                            w(v); v.wait_ge(sS, i + 1)
                            v.wait_ge(sT, i)
                            if i >= 3:
                                v.wait_ge(sP, i - 2)
                            fin(v.tensor_scalar(tb[i % 3][:, :], psS[i % 3][0:128, 0:512],
                                                0.2, None, ALU.mult), sT)
                        if i <= 4095:
                            w(v); v.wait_ge(sT, i + 1)
                            fin(v.tensor_tensor(tb[i % 3][:, :], psS[i % 3][0:128, 0:512],
                                                tb[i % 3][:, :], ALU.max), sL)
                v.wait_ge(S["cPrv"], 16)
                v.tensor_scalar(prevt[:, :], prevt[:, :], gprev_sb[:, 0:1], None,
                                ALU.mult).then_inc(S["cPg"], 1)
        st_ab.close()

        NTS = 8
        NGT = 60
        NBx = 4
        marks = []
        for ts in range(NTS):
            for oc in range(8):
                marks.append(("cSg", 8 * ts + oc + 1))
            for oc in range(8):
                marks.append(("cRl", 8 * ts + oc + 1))
            for oc in range(8):
                marks.append(("cOu", 8 * ts + oc + 1))
            marks.append(("cMu", ts + 1))
            marks.append(("cVr", 2 * ts + 1))
            marks.append(("cNm", 8 * (ts + 1)))
            marks.append(("cNm", 8 * (ts + 1)))
            for j in range(32):
                marks.append(("cTe", 32 * ts + j + 1))

        def gidx(ts, g):
            return ts * NGT + g

        with contextlib.ExitStack() as stC:
            _c = stC.enter_context
            wr_sb = _c(nc.sbuf_tensor("wr_sb", [128, 8 * 1024], BF16))
            wk2_sb = _c(nc.sbuf_tensor("wk2_sb", [128, 8 * 1024], BF16))
            wv2_sb = _c(nc.sbuf_tensor("wv2_sb", [128, 8 * 1024], BF16))
            mur_sb = _c(nc.sbuf_tensor("mur_sb", [128, 8], F32))
            muk_sb = _c(nc.sbuf_tensor("muk_sb", [128, 8], F32))
            omr_sb = _c(nc.sbuf_tensor("omr_sb", [128, 8], F32))
            omk_sb = _c(nc.sbuf_tensor("omk_sb", [128, 8], F32))
            idC_sb = _c(nc.sbuf_tensor("idC_sb", [128, 128], F32))
            xc0 = _c(nc.sbuf_tensor("xc0", [128, 8 * 513], F32))
            xc1 = _c(nc.sbuf_tensor("xc1", [128, 8 * 513], F32))
            xr_bf = _c(nc.sbuf_tensor("xr_bf", [128, 8 * 512], BF16))
            xk_bf = _c(nc.sbuf_tensor("xk_bf", [128, 8 * 512], BF16))
            t1 = _c(nc.sbuf_tensor("t1C", [128, 512], F32))
            r_bf = _c(nc.sbuf_tensor("r_bf", [128, 8 * 512], BF16))
            t2_bf = _c(nc.sbuf_tensor("t2_bf", [128, 512], BF16))
            ka_bf = _c(nc.sbuf_tensor("ka_bf", [128, 8 * 512], BF16))
            t3 = _c(nc.sbuf_tensor("t3C", [128, 512], F32))
            y_sb = _c(nc.sbuf_tensor("y_sbC", [128, 8 * 512], F32))
            y_bf = _c(nc.sbuf_tensor("y_bfC", [128, 8 * 512], BF16))
            y2_bf = _c(nc.sbuf_tensor("y2_bfC", [128, 8 * 512], BF16))
            ones_bf = _c(nc.sbuf_tensor("ones_bfC", [128, 1], BF16))
            ones_f = _c(nc.sbuf_tensor("ones_fC", [1, 128], F32))
            epsC = _c(nc.sbuf_tensor("epsC", [1, 1], F32))
            mu_row = _c(nc.sbuf_tensor("mu_rowC", [1, 512], F32))
            mu2_row = _c(nc.sbuf_tensor("mu2_rowC", [1, 512], F32))
            var_row = _c(nc.sbuf_tensor("var_rowC", [1, 512], F32))
            lnv_row = _c(nc.sbuf_tensor("lnv_rowC", [1, 512], F32))
            rstd_row = _c(nc.sbuf_tensor("rstd_rowC", [1, 512], F32))
            mr_row = _c(nc.sbuf_tensor("mr_rowC", [1, 512], F32))
            s2_sb = _c(nc.sbuf_tensor("s2_sbC", [128, 8 * 512], F32))
            s2t_sb0 = _c(nc.sbuf_tensor("s2t_sb0C", [128, 8 * 128], F32))
            s2t_sb1 = _c(nc.sbuf_tensor("s2t_sb1C", [128, 8 * 128], F32))
            psX0 = _c(nc.psum_tensor("psX0C", [128, 512], F32))
            psX1 = _c(nc.psum_tensor("psX1C", [128, 512], F32))
            psX2 = _c(nc.psum_tensor("psX2C", [128, 512], F32))
            psX3 = _c(nc.psum_tensor("psX3C", [128, 512], F32))
            blockC = stC.enter_context(nc.Block())
            xb = [xc0, xc1]
            psX = [psX0, psX1, psX2, psX3]
            s2t_sb = [s2t_sb0, s2t_sb1]
            semsC = dict(cSg=S["cSg"], cRl=S["cRl"], cOu=S["cOu"], cMu=S["cMu"],
                         cVr=S["cVr"], cNm=S["cNm"], cTe=S["cTe"])
            sW, sIn, sMM, sMx = S["cW"], S["cIn"], S["cMM"], S["cMx"]
            sDv, sAc, sSg, sRl = S["cDv"], S["cAc"], S["cSg"], S["cRl"]
            sOu, sMu, sSq2, sYc = S["cOu"], S["cMu"], S["cSq"], S["cYc"]
            sLn, sVr, sNm, sTe = S["cLn"], S["cVr"], S["cNm"], S["cTe"]
            sOF, sOT, sMs = S["cOF"], S["cOT"], S["cMs"]

            def bank_wait(t, g):
                if g >= NBx:
                    nm, val = marks[g - NBx]
                    t.wait_ge(semsC[nm], val)

            @blockC.gpsimd
            def _(g):
                g.wait_ge(S["bO"], 16 * 4 * NQS)
                g.dma_start(wr_sb.ap().rearrange("p (c m) -> p c m", m=1024),
                            wrT.ap().rearrange("(c p) m -> p c m", p=128)).then_inc(sW, 16)
                g.dma_start(wk2_sb.ap().rearrange("p (c m) -> p c m", m=1024),
                            wkT2.ap().rearrange("(c p) m -> p c m", p=128)).then_inc(sW, 16)
                g.dma_start(wv2_sb.ap().rearrange("p (c m) -> p c m", m=1024),
                            wvT2.ap().rearrange("(c p) m -> p c m", p=128)).then_inc(sW, 16)
                g.dma_start(mur_sb[:, :], mur.ap()).then_inc(sW, 16)
                g.dma_start(muk_sb[:, :], muk.ap()).then_inc(sW, 16)
                g.dma_start(idC_sb[:, :], ident.ap()).then_inc(sW, 16)
                g.wait_ge(S["cPrv"], 32)
                for ts in range(NTS):
                    if ts >= 2:
                        g.wait_ge(sMx, 16 * (ts - 1))
                        g.wait_ge(sYc, 9 * (ts - 1))
                    g.wait_ge(sIn, 16 * ts)
                    g.dma_start(
                        xb[ts % 2].ap().rearrange("p (c t) -> p c t", t=513),
                        s1d.ap().rearrange("(c p) t -> p c t", p=128)[:, :, 127 + ts * 512: 127 + ts * 512 + 513],
                    ).then_inc(sIn, 16)
                g.wait_ge(sOF, 16 * NTS)
                hv2 = nc.values_load(hv_sb[0:1, 0:1], engines=[mybir.EngineType.Pool],
                                     min_val=0, max_val=1, skip_runtime_bounds_check=True)
                with g.register("fv2") as fv2, g.register("pc2") as pc2:
                    for bi in g.Switch(hv2, 2):
                        g.br(f"x2_{bi}")
                    for bi in range(2):
                        other = 1 - bi
                        with nc.bb(f"x2_{bi}"):
                            g.dma_start(
                                s2sh.ap().rearrange("(s p) t -> s p t", p=D)[bi],
                                s2own.ap()).then_inc(S["x2"], 16)
                            g.wait_ge(S["x2"], 16)
                            g.dma_start(flg2.ap()[bi:bi + 1, :], one_i[:, :]).then_inc(S["x2"], 16)
                            g.wait_ge(S["x2"], 32)
                            g.reg_mov(pc2, 0)
                            g.br(f"x2p_{bi}")
                        with nc.bb(f"x2p_{bi}"):
                            g.reg_add(pc2, pc2, 16)
                            g.dma_start(fl_sb[:, :], flg2.ap()[other:other + 1, :]).then_inc(S["p2"], 16)
                            g.wait_ge(S["p2"], pc2)
                            g.reg_load(fv2, fl_sb[0:1, 0:1])
                            g.br_lt(fv2, 1, f"x2p_{bi}", f"x2g_{bi}")
                        with nc.bb(f"x2g_{bi}"):
                            g.dma_start(
                                gjunk.ap(),
                                s2sh.ap().rearrange("(s p) t -> s p t", p=D)[other]
                                ).then_inc(S["cCC"], 16)
                            g.br("x2_done")
                    with nc.bb("x2_done"):
                        g.wait_ge(S["cCC"], 16)

            @blockC.tensor
            def _(t):
                t.wait_ge(sW, 96)
                t.wait_ge(sMs, 1)
                t.wait_ge(S["bO"], 16 * 4 * NQS)
                for ts in range(NTS):
                    for stage, (wsb, rhs) in enumerate([(wr_sb, xr_bf), (wk2_sb, xk_bf)]):
                        t.wait_ge(sMx, 16 * ts + (stage + 1) * 8)
                        for oc in range(8):
                            g = gidx(ts, stage * 8 + oc)
                            bank_wait(t, g)
                            for ic in range(8):
                                m = t.matmul(psX[g % NBx][0:128, 0:512],
                                             wsb[:, ic * 1024 + oc * 128: ic * 1024 + (oc + 1) * 128],
                                             rhs[:, ic * 512:(ic + 1) * 512],
                                             start=(ic == 0), stop=(ic == 7))
                                if ic == 7:
                                    m.then_inc(sMM, 1)
                    t.wait_ge(sSq2, 8 * (ts + 1))
                    for oc in range(8):
                        g = gidx(ts, 16 + oc)
                        bank_wait(t, g)
                        for ic in range(8):
                            m = t.matmul(psX[g % NBx][0:128, 0:512],
                                         wv2_sb[:, ic * 1024 + oc * 128: ic * 1024 + (oc + 1) * 128],
                                         ka_bf[:, ic * 512:(ic + 1) * 512],
                                         start=(ic == 0), stop=(ic == 7))
                            if ic == 7:
                                m.then_inc(sMM, 1)
                    g = gidx(ts, 24)
                    t.wait_ge(sYc, 9 * (ts + 1))
                    bank_wait(t, g)
                    for dc in range(8):
                        m = t.matmul(psX[g % NBx][0:1, 0:512], ones_bf[:, 0:1],
                                     y_bf[:, dc * 512:(dc + 1) * 512],
                                     start=(dc == 0), stop=(dc == 7))
                        if dc == 7:
                            m.then_inc(sMM, 1)
                    g = gidx(ts, 25)
                    bank_wait(t, g)
                    for dc in range(8):
                        m = t.matmul(psX[g % NBx][0:1, 0:512], ones_bf[:, 0:1],
                                     y2_bf[:, dc * 512:(dc + 1) * 512],
                                     start=(dc == 0), stop=(dc == 7))
                        if dc == 7:
                            m.then_inc(sMM, 1)
                    g = gidx(ts, 26)
                    t.wait_ge(sLn, ts + 1)
                    bank_wait(t, g)
                    t.matmul(psX[g % NBx][0:128, 0:512], ones_f[0:1, :], rstd_row[0:1, :],
                             start=True, stop=True).then_inc(sMM, 1)
                    g = gidx(ts, 27)
                    t.wait_ge(sVr, 2 * ts + 2)
                    bank_wait(t, g)
                    t.matmul(psX[g % NBx][0:128, 0:512], ones_f[0:1, :], mr_row[0:1, :],
                             start=True, stop=True).then_inc(sMM, 1)
                    t.wait_ge(sNm, 8 * (ts + 1))
                    for u in range(4):
                        for dc in range(8):
                            g = gidx(ts, 28 + u * 8 + dc)
                            bank_wait(t, g)
                            t.transpose(psX[g % NBx][0:128, 0:128],
                                        s2_sb[:, dc * 512 + u * 128: dc * 512 + (u + 1) * 128],
                                        idC_sb[:, :]).then_inc(sMM, 1)

            @blockC.scalar
            def _(s):
                s.wait_ge(S["bO"], 16 * 4 * NQS)
                for ts in range(NTS):
                    for oc in range(8):
                        g = gidx(ts, oc)
                        s.wait_ge(sMM, g + 1)
                        s.activation(r_bf[:, oc * 512:(oc + 1) * 512], psX[g % NBx][0:128, 0:512],
                                     AF.Sigmoid).then_inc(sSg, 1)
                    for oc in range(8):
                        s.wait_ge(sRl, 8 * ts + oc + 1)
                        s.activation(ka_bf[:, oc * 512:(oc + 1) * 512], t2_bf[:, :],
                                     AF.Square).then_inc(sSq2, 1)
                    s.wait_ge(sVr, 2 * ts + 1)
                    s.activation(lnv_row[0:1, :], var_row[0:1, :], AF.Ln,
                                 bias=epsC[0:1, 0:1]).then_inc(sAc, 1)
                    s.wait_ge(sAc, ts + 1)
                    s.activation(rstd_row[0:1, :], lnv_row[0:1, :], AF.Exp,
                                 scale=-0.5).then_inc(sLn, 1)

            @blockC.vector
            def _(v):
                v.wait_ge(S["bO"], 16 * 4 * NQS)
                v.memset(ones_bf[:, :], 1.0 / D)
                v.memset(ones_f[:, :], 1.0)
                v.memset(epsC[:, :], EPS).then_inc(sMs, 1)
                v.wait_ge(sMs, 1)
                v.wait_ge(sW, 96)
                nDv = 0
                v.tensor_scalar(omr_sb[:, :], mur_sb[:, :], -1.0, 1.0, ALU.mult,
                                ALU.add).then_inc(sDv, 1)
                v.tensor_scalar(omk_sb[:, :], muk_sb[:, :], -1.0, 1.0, ALU.mult,
                                ALU.add).then_inc(sDv, 1)
                nDv += 2
                v.wait_ge(sDv, nDv)
                for ts in range(NTS):
                    v.wait_ge(sIn, 16 * (ts + 1))
                    if ts >= 1:
                        v.wait_ge(sMM, gidx(ts - 1, 16))
                    for stage, (mu_t, om_t, dst) in enumerate(
                            [(mur_sb, omr_sb, xr_bf), (muk_sb, omk_sb, xk_bf)]):
                        for dc in range(8):
                            x = xb[ts % 2]
                            v.wait_ge(sMx, 16 * ts + stage * 8 + dc)
                            v.tensor_scalar(t1[:, :], x[:, dc * 513: dc * 513 + 512],
                                            om_t[:, dc:dc + 1], None, ALU.mult).then_inc(sDv, 1)
                            nDv += 1
                            v.wait_ge(sDv, nDv)
                            v.scalar_tensor_tensor(dst[:, dc * 512:(dc + 1) * 512],
                                                   x[:, dc * 513 + 1: dc * 513 + 513],
                                                   mu_t[:, dc:dc + 1],
                                                   t1[:, :], ALU.mult,
                                                   ALU.add).then_inc(sMx, 1)
                    for oc in range(8):
                        g = gidx(ts, 8 + oc)
                        v.wait_ge(sMM, g + 1)
                        if oc >= 1:
                            v.wait_ge(sSq2, 8 * ts + oc)
                        v.tensor_scalar(t2_bf[:, :], psX[g % NBx][0:128, 0:512],
                                        0.0, None, ALU.max).then_inc(sRl, 1)
                    for oc in range(8):
                        g = gidx(ts, 16 + oc)
                        v.wait_ge(sMM, g + 1)
                        v.wait_ge(sSg, 8 * ts + oc + 1)
                        v.wait_ge(sDv, nDv)
                        v.tensor_tensor(t3[:, :], psX[g % NBx][0:128, 0:512],
                                        r_bf[:, oc * 512:(oc + 1) * 512],
                                        ALU.mult).then_inc(sOu, 1)
                        v.wait_ge(sOu, 8 * ts + oc + 1)
                        if ts >= 1 and oc == 0:
                            v.wait_ge(sMM, gidx(ts - 1, 26))
                        v.tensor_tensor(y_sb[:, oc * 512:(oc + 1) * 512], t3[:, :],
                                        xb[ts % 2][:, oc * 513 + 1: oc * 513 + 513],
                                        ALU.add).then_inc(sDv, 1)
                        nDv += 1
                        v.wait_ge(sDv, nDv)
                        v.tensor_copy(y_bf[:, oc * 512:(oc + 1) * 512],
                                      y_sb[:, oc * 512:(oc + 1) * 512]).then_inc(sYc, 1)
                    v.wait_ge(sYc, 9 * ts + 8)
                    v.tensor_tensor(y2_bf[:, :], y_bf[:, :], y_bf[:, :],
                                    ALU.mult).then_inc(sYc, 1)
                    g = gidx(ts, 24)
                    v.wait_ge(sMM, g + 1)
                    v.tensor_copy(mu_row[0:1, :], psX[g % NBx][0:1, 0:512]).then_inc(sMu, 1)
                    g = gidx(ts, 25)
                    v.wait_ge(sMM, g + 1)
                    v.wait_ge(sMu, ts + 1)
                    v.tensor_tensor(mu2_row[0:1, :], mu_row[0:1, :], mu_row[0:1, :],
                                    ALU.mult).then_inc(sDv, 1)
                    nDv += 1
                    v.wait_ge(sDv, nDv)
                    v.tensor_tensor(var_row[0:1, :], psX[g % NBx][0:1, 0:512],
                                    mu2_row[0:1, :], ALU.subtract).then_inc(sVr, 1)
                    v.wait_ge(sLn, ts + 1)
                    v.tensor_tensor(mr_row[0:1, :], mu_row[0:1, :], rstd_row[0:1, :],
                                    ALU.mult).then_inc(sVr, 1)
                    gR = gidx(ts, 26)
                    gM = gidx(ts, 27)
                    v.wait_ge(sMM, gM + 1)
                    if ts >= 1:
                        v.wait_ge(sMM, gidx(ts, 0))
                        v.wait_ge(sOF, 16 * ts)
                    for dc in range(8):
                        v.wait_ge(sNm, 8 * ts + dc)
                        v.tensor_tensor(t3[:, :], y_sb[:, dc * 512:(dc + 1) * 512],
                                        psX[gR % NBx][0:128, 0:512], ALU.mult).then_inc(sDv, 1)
                        nDv += 1
                        v.wait_ge(sDv, nDv)
                        v.tensor_tensor(s2_sb[:, dc * 512:(dc + 1) * 512], t3[:, :],
                                        psX[gM % NBx][0:128, 0:512],
                                        ALU.subtract).then_inc(sNm, 1)
                    for u in range(4):
                        for dc in range(8):
                            g = gidx(ts, 28 + u * 8 + dc)
                            gu = ts * 4 + u
                            v.wait_ge(sMM, g + 1)
                            if dc == 0 and gu >= 2:
                                v.wait_ge(sOT, 16 * (gu - 1))
                            v.tensor_copy(s2t_sb[gu % 2][:, dc * 128:(dc + 1) * 128],
                                          psX[g % NBx][0:128, 0:128]).then_inc(sTe, 1)

            @blockC.sync
            def _(sy):
                sy.wait_ge(S["bO"], 16 * 4 * NQS)
                for ts in range(NTS):
                    sy.wait_ge(sNm, 8 * (ts + 1))
                    sy.wait_ge(sOF, 16 * ts)
                    sy.dma_start(
                        s2own.ap().rearrange("(c p) t -> p c t", p=128)[:, :, ts * 512:(ts + 1) * 512],
                        s2_sb.ap().rearrange("p (c t) -> p c t", t=512),
                    ).then_inc(sOF, 16)
                    for u in range(4):
                        gu = ts * 4 + u
                        sy.wait_ge(sTe, 8 * (gu + 1))
                        sy.wait_ge(sOT, 16 * gu)
                        sy.dma_start(
                            s2t_d.ap().rearrange("(g p) c -> g p c", p=128)[gu],
                            s2t_sb[gu % 2][:, :],
                        ).then_inc(sOT, 16)

        with contextlib.ExitStack() as stDE:
            v1_tm = stDE.enter_context(nc.sbuf_tensor("v1_tm", [128, NKR * 1025], BF16))

            with contextlib.ExitStack() as stD:
                _d = stD.enter_context
                wqa = _d(nc.sbuf_tensor("wqa", [128, 8 * 1024], BF16))
                wka = _d(nc.sbuf_tensor("wka", [128, 8 * 1024], BF16))
                wva = _d(nc.sbuf_tensor("wva", [128, 8 * 1024], BF16))
                xd = _d(nc.sbuf_tensor("xd", [128, 8 * 512], BF16))
                k_st = _d(nc.sbuf_tensor("k_stD", [128, 8 * 512], BF16))
                q_st = _d(nc.sbuf_tensor("q_stD", [128, 8 * 512], BF16))
                gate_f = _d(nc.sbuf_tensor("gate_f", [128, 64], F32))
                gate_bf = _d(nc.sbuf_tensor("gate_bf", [128, 64], BF16))
                psKd0 = _d(nc.psum_tensor("psKd0", [128, 512], F32))
                psKd1 = _d(nc.psum_tensor("psKd1", [128, 512], F32))
                psQd0 = _d(nc.psum_tensor("psQd0", [128, 512], F32))
                psQd1 = _d(nc.psum_tensor("psQd1", [128, 512], F32))
                psVd0 = _d(nc.psum_tensor("psVd0", [128, 1024], F32))
                psVd1 = _d(nc.psum_tensor("psVd1", [128, 1024], F32))
                blockD = stD.enter_context(nc.Block())
                psKD = [psKd0, psKd1]
                psQD = [psQd0, psQd1]
                psVD = [psVd0, psVd1]
                NTR = 16
                sXd, sKd, sKEd, sQgd = S["dX"], S["dK"], S["dKE"], S["dQg"]
                sQEd, sVd, sVEd = S["dQE"], S["dV"], S["dVE"]
                sOKd, sOQd = S["dOK"], S["dOQ"]

                @blockD.gpsimd
                def _(g):
                    g.wait_ge(S["cCC"], 16)
                    g.wait_ge(S["cOT"], 16 * 32)
                    g.dma_start(wqa.ap().rearrange("p (c m) -> p c m", m=1024),
                                wq1.ap().rearrange("(c p) m -> p c m", p=128)).then_inc(S["dX"], 16)
                    g.dma_start(wka.ap().rearrange("p (c m) -> p c m", m=1024),
                                wk1.ap().rearrange("(c p) m -> p c m", p=128)).then_inc(S["dX"], 16)
                    g.dma_start(wva.ap().rearrange("p (c m) -> p c m", m=1024),
                                wv1.ap().rearrange("(c p) m -> p c m", p=128)).then_inc(S["dX"], 16)
                    g.dma_start(gate_f[:, :], gate64.ap()).then_inc(S["dX"], 16)
                    g.dma_start(gate_bf[:, :], gate64.ap()).then_inc(S["dX"], 16)
                    for ti in range(NTR):
                        if ti >= 1:
                            g.wait_ge(sVd, 4 * ti)
                        g.wait_ge(S["dX"], 80 + 16 * ti)
                        if ti < 8:
                            src = gjunk.ap().rearrange("(c p) t -> p c t", p=128)[
                                :, :, ti * 512:(ti + 1) * 512]
                        else:
                            src = s2own.ap().rearrange("(c p) t -> p c t", p=128)[
                                :, :, (ti - 8) * 512:(ti - 8) * 512 + 512]
                        g.dma_start(xd.ap().rearrange("p (c t) -> p c t", t=512),
                                    src).then_inc(S["dX"], 16)

                @blockD.tensor
                def _(t):
                    t.wait_ge(S["cCC"], 16)
                    t.wait_ge(S["cOF"], 16 * 8)
                    for ti in range(NTR):
                        t.wait_ge(S["dX"], 80 + 16 * (ti + 1))
                        for oc in range(8):
                            kg = ti * 8 + oc
                            if kg >= 2:
                                t.wait_ge(sKEd, kg - 1)
                            for ic in range(8):
                                m = t.matmul(psKD[kg % 2][0:128, 0:512],
                                             wka[:, ic * 1024 + oc * 128: ic * 1024 + (oc + 1) * 128],
                                             xd[:, ic * 512:(ic + 1) * 512],
                                             start=(ic == 0), stop=(ic == 7))
                                if ic == 7:
                                    m.then_inc(sKd, 1)
                        if ti >= 8:
                            for oc in range(8):
                                qg = (ti - 8) * 8 + oc
                                if qg >= 2:
                                    t.wait_ge(sQEd, qg - 1)
                                for ic in range(8):
                                    m = t.matmul(psQD[qg % 2][0:128, 0:512],
                                                 wqa[:, ic * 1024 + oc * 128: ic * 1024 + (oc + 1) * 128],
                                                 xd[:, ic * 512:(ic + 1) * 512],
                                                 start=(ic == 0), stop=(ic == 7))
                                    if ic == 7:
                                        m.then_inc(sQgd, 1)
                        for ci in range(4):
                            vi = ti * 4 + ci
                            if vi >= 2:
                                t.wait_ge(sVEd, 2 * (vi - 1))
                            for ic in range(8):
                                for n in range(2):
                                    m = t.matmul(
                                        psVD[vi % 2][0:128, n * 512:(n + 1) * 512],
                                        xd[:, ic * 512 + ci * 128: ic * 512 + ci * 128 + 128],
                                        wva[:, ic * 1024 + n * 512: ic * 1024 + (n + 1) * 512],
                                        start=(ic == 0), stop=(ic == 7))
                                    if ic == 7 and n == 1:
                                        m.then_inc(sVd, 1)

                @blockD.scalar
                def _(s):
                    s.wait_ge(S["cCC"], 16)
                    for ti in range(8, NTR):
                        for oc in range(8):
                            qg = (ti - 8) * 8 + oc
                            s.wait_ge(sQgd, qg + 1)
                            if qg >= 8:
                                s.wait_ge(sOQd, 16 * (ti - 8))
                            s.activation(q_st[:, oc * 512:(oc + 1) * 512],
                                         psQD[qg % 2][0:128, 0:512], AF.Copy,
                                         scale=float(1.0 / 32.0)).then_inc(sQEd, 1)

                @blockD.vector
                def _(v):
                    v.wait_ge(S["cCC"], 16)
                    v.wait_ge(S["dX"], 80)
                    for ti in range(NTR):
                        for oc in range(8):
                            kg = ti * 8 + oc
                            v.wait_ge(sKd, kg + 1)
                            if kg >= 8:
                                v.wait_ge(sOKd, 16 * ti)
                            v.tensor_copy(k_st[:, oc * 512:(oc + 1) * 512],
                                          psKD[kg % 2][0:128, 0:512]).then_inc(sKEd, 1)
                        for ci in range(4):
                            vi = ti * 4 + ci
                            v.wait_ge(sVd, vi + 1)
                            v.tensor_scalar(v1_tm[:, vi * 1025 + 1:(vi + 1) * 1025],
                                            psVD[vi % 2][0:128, 0:1024],
                                            gate_f[:, vi:vi + 1], None,
                                            ALU.mult).then_inc(sVEd, 1)
                            v.tensor_copy(v1_tm[:, vi * 1025: vi * 1025 + 1],
                                          gate_bf[:, vi:vi + 1]).then_inc(sVEd, 1)

                @blockD.sync
                def _(sy):
                    sy.wait_ge(S["cCC"], 16)
                    for ti in range(NTR):
                        sy.wait_ge(sKEd, 8 * (ti + 1))
                        sy.wait_ge(sOKd, 16 * ti)
                        sy.dma_start(
                            k1_d.ap().rearrange("(c p) t -> p c t", p=128)[:, :, ti * 512:(ti + 1) * 512],
                            k_st.ap().rearrange("p (c t) -> p c t", t=512),
                        ).then_inc(sOKd, 16)
                        if ti >= 8:
                            tq = ti - 8
                            sy.wait_ge(sQEd, 8 * (tq + 1))
                            sy.wait_ge(sOQd, 16 * tq)
                            sy.dma_start(
                                q1_d.ap().rearrange("(c p) t -> p c t", p=128)[:, :, tq * 512:(tq + 1) * 512],
                                q_st.ap().rearrange("p (c t) -> p c t", t=512),
                            ).then_inc(sOQd, 16)

            with contextlib.ExitStack() as stE:
                _f = stE.enter_context
                q_sbE = _f(nc.sbuf_tensor("q_sbE", [128, 8 * 256], BF16))
                q_sbE2 = _f(nc.sbuf_tensor("q_sbE2", [128, 8 * 256], BF16))
                k0 = _f(nc.sbuf_tensor("k0E", [128, 8 * 128], BF16))
                k1b = _f(nc.sbuf_tensor("k1E", [128, 8 * 128], BF16))
                k2b = _f(nc.sbuf_tensor("k2E", [128, 8 * 128], BF16))
                k3b = _f(nc.sbuf_tensor("k3E", [128, 8 * 128], BF16))
                pt0 = _f(nc.sbuf_tensor("pt0E", [128, 256], BF16))
                pt1 = _f(nc.sbuf_tensor("pt1E", [128, 256], BF16))
                ptc = _f(nc.sbuf_tensor("ptcE", [128, 256], BF16))
                pt2a = _f(nc.sbuf_tensor("pt2aE", [128, 256], BF16))
                pt2b = _f(nc.sbuf_tensor("pt2bE", [128, 256], BF16))
                m0_sb = _f(nc.sbuf_tensor("m0_sbE", [128, 256], BF16))
                m1_sb = _f(nc.sbuf_tensor("m1_sbE", [128, 256], BF16))
                attn_q = _f(nc.sbuf_tensor("attn_qE", [128, 1024], F32))
                s2_t0 = _f(nc.sbuf_tensor("s2_t0E", [128, 1024], F32))
                s2_t1 = _f(nc.sbuf_tensor("s2_t1E", [128, 1024], F32))
                s3tm = _f(nc.sbuf_tensor("s3tmE", [128, 1024], F32))
                s3f = _f(nc.sbuf_tensor("s3fE", [128, 1024], F32))
                s3f_bf = _f(nc.sbuf_tensor("s3f_bfE", [128, 1024], BF16))
                y4 = _f(nc.sbuf_tensor("y4E", [128, 1024], F32))
                y4tm = _f(nc.sbuf_tensor("y4tmE", [128, 1024], F32))
                o_t0 = _f(nc.sbuf_tensor("o_t0E", [128, 1024], F32))
                o_t1 = _f(nc.sbuf_tensor("o_t1E", [128, 1024], F32))
                tb4 = _f(nc.sbuf_tensor("tb4E", [128, 128], F32))
                y2b = _f(nc.sbuf_tensor("y2bE", [128, 1024], BF16))
                statE = _f(nc.sbuf_tensor("statE", [128, 8], F32))
                rstdE = _f(nc.sbuf_tensor("rstdE", [128, 2], F32))
                nmrE = _f(nc.sbuf_tensor("nmrE", [128, 2], F32))
                rgE = _f(nc.sbuf_tensor("rgE", [128, 1], F32))
                epsE = _f(nc.sbuf_tensor("epsE", [128, 1], F32))
                idE = _f(nc.sbuf_tensor("idE", [128, 128], F32))
                g1_sb = _f(nc.sbuf_tensor("g1_sbE", [32, 2 * 4 * 32], BF16))
                g2_sb = _f(nc.sbuf_tensor("g2_sbE", [16, 2 * 4 * 16], BF16))
                e1_sb = _f(nc.sbuf_tensor("e1_sbE", [32, 512], BF16))
                e2_sb = _f(nc.sbuf_tensor("e2_sbE", [32, 4 * 128], BF16))
                e3_sb = _f(nc.sbuf_tensor("e3_sbE", [16, 512], BF16))
                e4_sb = _f(nc.sbuf_tensor("e4_sbE", [16, 4 * 128], BF16))
                r1_sb = _f(nc.sbuf_tensor("r1_sbE", [32, 512], BF16))
                s1_sb = _f(nc.sbuf_tensor("s1_sbE", [16, 512], BF16))
                wrt_bf = _f(nc.sbuf_tensor("wrt_bfE", [128, 4 * 512], BF16))
                wit_bf = _f(nc.sbuf_tensor("wit_bfE", [128, 4 * 512], BF16))
                witn_bf = _f(nc.sbuf_tensor("witn_bfE", [128, 4 * 512], BF16))
                br_sb = _f(nc.sbuf_tensor("br_sbE", [128, 4], F32))
                bi_sb = _f(nc.sbuf_tensor("bi_sbE", [128, 4], F32))
                psS0e = _f(nc.psum_tensor("psS0e", [128, 512], F32))
                psS1e = _f(nc.psum_tensor("psS1e", [128, 512], F32))
                psP0e = _f(nc.psum_tensor("psP0e", [128, 1536], F32))
                psP1e = _f(nc.psum_tensor("psP1e", [128, 1536], F32))
                blockE = stE.enter_context(nc.Block())
                v_tm = v1_tm
                q1, k1, s2town = q1_d, k1_d, s2t_d
                psS = [psS0e, psS1e]
                psPV = [psP0e, psP1e]
                kb_sb = [k0, k1b, k2b, k3b]
                ptb = [pt0, pt1, ptc]
                pt2 = [pt2a, pt2b]
                s2_t = [s2_t0, s2_t1]
                o_t = [o_t0, o_t1]
                msk = [m0_sb, m1_sb]
                q_sb = [q_sbE, q_sbE2]
                psS0, psS1 = psS0e, psS1e
                stat, rstd, nmr, rg, eps_sb, id_sb = statE, rstdE, nmrE, rgE, epsE, idE
                aex_sb = y2b[:, 0:512]
                bex_sb = y2b[:, 512:1024]
                prod_sb = o_t0[:, 0:512]
                wacc_j = [attn_q[:, 0:512], attn_q[:, 512:1024], s3f[:, 0:512], s3f[:, 512:1024]]
                sLd, sQt, sKt, sS2r = S["eLd"], S["eQt"], S["eKt"], S["eS2r"]
                sS, sP, sMk, sV2 = S["eS"], S["eP"], S["eMk"], S["eV2"]
                sE, sY3, sVr, sLn = S["eE"], S["eY3"], S["eVr"], S["eLn"]
                sNm, sTr, sTe, sCb = S["eNm"], S["eTr"], S["eTe"], S["eCb"]
                sFF, sFE, sY4, sO = S["eFF"], S["eFE"], S["eY4"], S["eO"]
                sDv, sAc, sLg = S["eDv"], S["eAc"], S["eLg"]
                sTT, sTD, sMs = S["eTT"], S["eTD"], S["eMs"]
                @blockE.sync
                def _(sy):
                    sy.wait_ge(S["dOK"], 256)
                    sy.wait_ge(S["dOQ"], 128)
                    sy.wait_ge(S["dVE"], 128)
                    sy.dma_start(id_sb[:, :], ident.ap()).then_inc(sLd, 16)
                    sy.dma_start(m0_sb[:, :], m0.ap()).then_inc(sLd, 16)
                    sy.dma_start(m1_sb[:, :], m1.ap()).then_inc(sLd, 16)
                    sy.dma_start(br_sb[:, :], brc.ap()).then_inc(sLd, 16)
                    sy.dma_start(bi_sb[:, :], bic.ap()).then_inc(sLd, 16)
                    for qt in range(NQT):
                        sy.wait_ge(sQt, 16 * qt)
                        if qt >= 2:
                            sy.wait_ge(sS, CKB[qt - 1])
                        sy.dma_start(
                            q_sb[qt % 2].ap().rearrange("p (c t) -> p c t", t=256),
                            q1.ap().rearrange("(c p) t -> p c t", p=128)[:, :, qt * 256:(qt + 1) * 256],
                        ).then_inc(sQt, 16)
                        if qt >= 1:
                            for qsub in range(2):
                                gi = (qt - 1) * 2 + qsub
                                sy.wait_ge(sNm, 2 * gi + 2)
                                sy.wait_ge(sO, 16 * gi)
                                sy.dma_start(
                                    o_nat.ap().rearrange("(g p) c -> g p c", p=128)[gi],
                                    o_t[gi % 2][:, :]).then_inc(sO, 16)
                    for qsub in range(2):
                        gi = (NQT - 1) * 2 + qsub
                        sy.wait_ge(sNm, 2 * gi + 2)
                        sy.wait_ge(sO, 16 * gi)
                        sy.dma_start(
                            o_nat.ap().rearrange("(g p) c -> g p c", p=128)[gi],
                            o_t[gi % 2][:, :]).then_inc(sO, 16)
                    sy.wait_ge(sO, 16 * 2 * NQT)

                @blockE.gpsimd
                def _(g):
                    g.wait_ge(S["dOK"], 256)
                    g.wait_ge(S["dOQ"], 128)
                    g.wait_ge(S["dVE"], 128)
                    g.dma_start(g1_sb[:, 0:128], g1r.ap()).then_inc(sLg, 16)
                    g.dma_start(g1_sb[:, 128:256], g1i.ap()).then_inc(sLg, 16)
                    g.dma_start(g2_sb[:, 0:64], g2r.ap()).then_inc(sLg, 16)
                    g.dma_start(g2_sb[:, 64:128], g2i.ap()).then_inc(sLg, 16)
                    g.dma_start(e1_sb[:, :], e1m.ap()).then_inc(sLg, 16)
                    g.dma_start(e2_sb[:, :], e2m.ap()).then_inc(sLg, 16)
                    g.dma_start(e3_sb[:, :], e3m.ap()).then_inc(sLg, 16)
                    g.dma_start(e4_sb[:, :], e4m.ap()).then_inc(sLg, 16)
                    nk = 0
                    for qt in range(NQT):
                        for kb in range(KBH[qt]):
                            e = CKB[qt] + kb
                            g.wait_ge(sKt, 16 * e)
                            if e >= 4:
                                g.wait_ge(sS, e - 3)
                            g.dma_start(
                                kb_sb[e % 4].ap().rearrange("p (c t) -> p c t", t=128),
                                k1.ap().rearrange("(c p) t -> p c t", p=128)[:, :, kb * 128:(kb + 1) * 128],
                            ).then_inc(sKt, 16)
                        for qsub in range(2):
                            gi = qt * 2 + qsub
                            g.wait_ge(sS2r, 16 * gi)
                            if gi >= 2:
                                g.wait_ge(sY3, gi - 1)
                            g.dma_start(s2_t[gi % 2][:, :],
                                        s2town.ap().rearrange("(g p) c -> g p c", p=128)[gi]
                                        ).then_inc(sS2r, 16)

                @blockE.tensor
                def _(t):
                    t.wait_ge(S["dOK"], 256)
                    t.wait_ge(S["dOQ"], 128)
                    t.wait_ge(S["dVE"], 128)
                    t.wait_ge(sLd, 80)
                    t.wait_ge(sLg, 128)
                    t.wait_ge(sMs, 1)
                    ntt = 0
                    for part in range(2):
                        for c in range(4):
                            if ntt >= 1:
                                t.wait_ge(sTD, 2 * ntt)
                            t.matmul(psS0[0:32, 0:512], g1_sb[:, part * 128 + c * 32: part * 128 + (c + 1) * 32],
                                     e1_sb[:, :], start=True, stop=True).then_inc(sTT, 1)
                            t.matmul(psS1[0:16, 0:512], g2_sb[:, part * 64 + c * 16: part * 64 + (c + 1) * 16],
                                     e3_sb[:, :], start=True, stop=True).then_inc(sTT, 1)
                            t.wait_ge(sTD, 2 * ntt + 2)
                            for j in range(4):
                                t.matmul(psS0[0:128, 0:512], e2_sb[:, j * 128:(j + 1) * 128],
                                         r1_sb[:, :], start=True, stop=True).then_inc(sTT, 1)
                                t.matmul(psS1[0:128, 0:512], e4_sb[:, j * 128:(j + 1) * 128],
                                         s1_sb[:, :], start=True, stop=True).then_inc(sTT, 1)
                                t.wait_ge(sTD, 2 * ntt + 2 + 2 * (j + 1))
                            ntt += 5
                    for qt in range(NQT):
                        t.wait_ge(sQt, 16 * (qt + 1))
                        if qt >= 1:
                            t.wait_ge(sTe, 32 * qt)
                        def pv_e(ep, kbp):
                            crossing = kbp >= KBH[qt] - 2
                            if crossing:
                                cidx = 2 * qt + (kbp - (KBH[qt] - 2))
                                t.wait_ge(sMk, cidx + 1)
                                src = pt2[cidx % 2]
                            else:
                                t.wait_ge(sP, ep + 1)
                                src = ptb[ep % 3]
                            for qsub in range(2):
                                if kbp == 0:
                                    gi = qt * 2 + qsub
                                    if gi >= 2:
                                        t.wait_ge(sFE, 8 * (gi - 1))
                                for n in range(2):
                                    t.matmul(psPV[qsub][0:128, n * 512:(n + 1) * 512],
                                             src[:, qsub * 128:(qsub + 1) * 128],
                                             v_tm[:, kbp * 1025 + 1 + n * 512: kbp * 1025 + 1 + (n + 1) * 512],
                                             start=(kbp == 0), stop=(kbp == KBH[qt] - 1))
                                m = t.matmul(psPV[qsub][0:128, 1024:1025],
                                             src[:, qsub * 128:(qsub + 1) * 128],
                                             v_tm[:, kbp * 1025: kbp * 1025 + 1],
                                             start=(kbp == 0), stop=(kbp == KBH[qt] - 1))
                                if qsub == 1:
                                    m.then_inc(sV2, 1)

                        for kb in range(KBH[qt]):
                            e = CKB[qt] + kb
                            t.wait_ge(sKt, 16 * (e + 1))
                            if e >= 2:
                                t.wait_ge(sP, e - 1)
                            for ic in range(8):
                                m = t.matmul(psS[e % 2][0:128, 0:256],
                                             kb_sb[e % 4][:, ic * 128:(ic + 1) * 128],
                                             q_sb[qt % 2][:, ic * 256:(ic + 1) * 256],
                                             start=(ic == 0), stop=(ic == 7))
                                if ic == 7:
                                    m.then_inc(sS, 1)
                            if kb >= 1:
                                pv_e(e - 1, kb - 1)
                        pv_e(CKB[qt] + KBH[qt] - 1, KBH[qt] - 1)
                        for qsub in range(2):
                            gi = qt * 2 + qsub
                            t.wait_ge(sNm, 2 * gi + 1)
                            for dc in range(8):
                                j = 32 * qt + qsub * 16 + dc
                                if j >= 2:
                                    t.wait_ge(sTe, j - 1)
                                t.transpose(psS[j % 2][0:128, 0:128],
                                            s3tm[:, dc * 128:(dc + 1) * 128],
                                            id_sb[:, :]).then_inc(sTr, 1)
                            t.wait_ge(sCb, gi + 1)
                            for og in range(8):
                                if og >= 2:
                                    t.wait_ge(sFE, 8 * gi + og - 1)
                                reg = psPV[qsub][0:128, (og % 2) * 512:(og % 2) * 512 + 128]
                                for ic in range(8):
                                    if og < 4:
                                        w = wrt_bf if ic < 4 else witn_bf
                                        oc = og
                                    else:
                                        w = wit_bf if ic < 4 else wrt_bf
                                        oc = og - 4
                                    m = t.matmul(reg, w[:, (ic % 4) * 512 + oc * 128: (ic % 4) * 512 + (oc + 1) * 128],
                                                 s3f_bf[:, ic * 128:(ic + 1) * 128],
                                                 start=(ic == 0), stop=(ic == 7))
                                    if ic == 7:
                                        m.then_inc(sFF, 1)
                            t.wait_ge(sY4, 8 * (gi + 1))
                            for dc in range(8):
                                j = 32 * qt + qsub * 16 + 8 + dc
                                if j >= 2:
                                    t.wait_ge(sTe, j - 1)
                                t.transpose(psS[j % 2][0:128, 0:128],
                                            y4[:, dc * 128:(dc + 1) * 128],
                                            id_sb[:, :]).then_inc(sTr, 1)

                @blockE.scalar
                def _(s):
                    s.wait_ge(S["dOK"], 256)
                    s.wait_ge(S["dVE"], 128)
                    for qt in range(NQT):
                        for kb in range(KBH[qt]):
                            e = CKB[qt] + kb
                            s.wait_ge(sS, e + 1)
                            if e >= 3:
                                s.wait_ge(sV2, e - 2)
                            s.activation(ptb[e % 3][:, :], psS[e % 2][0:128, 0:256],
                                         AF.Exp).then_inc(sP, 1)
                        for qsub in range(2):
                            gi = qt * 2 + qsub
                            s.wait_ge(sVr, 4 * gi + 1)
                            s.activation(stat[:, 6:7], stat[:, 4:5], AF.Ln,
                                         bias=eps_sb[:, 0:1]).then_inc(sAc, 1)
                            s.wait_ge(sAc, 2 * gi + 1)
                            s.activation(rstd[:, 0:1], stat[:, 6:7], AF.Exp, scale=-0.5).then_inc(sLn, 1)
                            s.wait_ge(sVr, 4 * gi + 3)
                            s.activation(stat[:, 7:8], stat[:, 5:6], AF.Ln,
                                         bias=eps_sb[:, 0:1]).then_inc(sAc, 1)
                            s.wait_ge(sAc, 2 * gi + 2)
                            s.activation(rstd[:, 1:2], stat[:, 7:8], AF.Exp, scale=-0.5).then_inc(sLn, 1)

                @blockE.vector
                def _(v):
                    v.wait_ge(S["dOK"], 256)
                    v.wait_ge(S["dVE"], 128)
                    v.memset(eps_sb[:, :], EPS).then_inc(sMs, 1)
                    v.wait_ge(sMs, 1)
                    v.wait_ge(sLd, 80)
                    v.wait_ge(sLg, 128)
                    nDv = 0

                    def dv(inst):
                        nonlocal nDv
                        inst.then_inc(sDv, 1)
                        nDv += 1
                        v.wait_ge(sDv, nDv)

                    ntt = 0
                    for part in range(2):
                        wdst = [wrt_bf, wit_bf][part]
                        for c in range(4):
                            v.wait_ge(sTT, 2 * ntt + 2)
                            v.tensor_copy(r1_sb[:, :], psS0[0:32, 0:512]).then_inc(sTD, 1)
                            v.tensor_copy(s1_sb[:, :], psS1[0:16, 0:512]).then_inc(sTD, 1)
                            for j in range(4):
                                v.wait_ge(sTT, 2 * ntt + 2 + 2 * (j + 1))
                                dv(v.tensor_copy(aex_sb, psS0[0:128, 0:512]))
                                dv(v.tensor_copy(bex_sb, psS1[0:128, 0:512]))
                                v.sem_inc(sTD, 2)
                                dv(v.tensor_tensor(prod_sb, aex_sb, bex_sb, ALU.mult))
                                if c == 0:
                                    dv(v.tensor_copy(wacc_j[j], prod_sb))
                                else:
                                    dv(v.tensor_tensor(wacc_j[j], wacc_j[j], prod_sb, ALU.add))
                                if c == 3:
                                    dv(v.tensor_copy(wdst[:, j * 512:(j + 1) * 512], wacc_j[j]))
                            ntt += 5
                    dv(v.tensor_scalar(witn_bf[:, :], wit_bf[:, :], -1.0, None, ALU.mult))

                    for qt in range(NQT):
                        for ci in range(2):
                            kb = KBH[qt] - 2 + ci
                            e = CKB[qt] + kb
                            cidx = 2 * qt + ci
                            v.wait_ge(sP, e + 1)
                            if cidx >= 2:
                                v.wait_ge(sV2, e - 1)
                            v.tensor_tensor(pt2[cidx % 2][:, :], ptb[e % 3][:, :],
                                            msk[ci][:, :], ALU.mult).then_inc(sMk, 1)
                        for qsub in range(2):
                            gi = qt * 2 + qsub
                            v.wait_ge(sV2, CKB[qt + 1])
                            dv(v.reciprocal(rg[:, 0:1], psPV[qsub][0:128, 1024:1025]))
                            if gi >= 1:
                                v.wait_ge(sNm, 2 * gi - 1)
                            v.tensor_scalar(attn_q[:, :], psPV[qsub][0:128, 0:1024],
                                            rg[:, 0:1], None, ALU.mult).then_inc(sE, 1)
                            v.wait_ge(sE, gi + 1)
                            v.wait_ge(sS2r, 16 * (gi + 1))
                            v.tensor_tensor(attn_q[:, :], attn_q[:, :], s2_t[gi % 2][:, :],
                                            ALU.add).then_inc(sY3, 1)
                            v.wait_ge(sY3, gi + 1)
                            dv(v.reduce_sum(stat[:, 0:1], attn_q[:, :], AX.X))
                            dv(v.tensor_tensor(y2b[:, :], attn_q[:, :], attn_q[:, :], ALU.mult))
                            dv(v.reduce_sum(stat[:, 1:2], y2b[:, :], AX.X))
                            dv(v.tensor_scalar(stat[:, 2:3], stat[:, 0:1], 1.0 / D, None, ALU.mult))
                            dv(v.tensor_tensor(stat[:, 3:4], stat[:, 2:3], stat[:, 2:3], ALU.mult))
                            dv(v.tensor_scalar(stat[:, 4:5], stat[:, 1:2], 1.0 / D, None, ALU.mult))
                            v.tensor_tensor(stat[:, 4:5], stat[:, 4:5], stat[:, 3:4],
                                            ALU.subtract).then_inc(sVr, 1)
                            v.wait_ge(sLn, 2 * gi + 1)
                            v.scalar_tensor_tensor(nmr[:, 0:1], stat[:, 2:3], -1.0, rstd[:, 0:1],
                                                   ALU.mult, ALU.mult).then_inc(sVr, 1)
                            v.wait_ge(sVr, 4 * gi + 2)
                            if gi >= 1:
                                v.wait_ge(sTr, 16 * gi - 8)
                            v.tensor_scalar(s3tm[:, :], attn_q[:, :], rstd[:, 0:1], nmr[:, 0:1],
                                            ALU.mult, ALU.add).then_inc(sNm, 1)
                            for dc in range(8):
                                j = 32 * qt + qsub * 16 + dc
                                v.wait_ge(sTr, j + 1)
                                v.tensor_copy(s3f[:, dc * 128:(dc + 1) * 128],
                                              psS[j % 2][0:128, 0:128]).then_inc(sTe, 1)
                                v.wait_ge(sTe, j + 1)
                            v.tensor_copy(s3f_bf[:, :], s3f[:, :]).then_inc(sCb, 1)
                            for og in range(8):
                                v.wait_ge(sFF, 8 * gi + og + 1)
                                b_t = br_sb if og < 4 else bi_sb
                                oc = og % 4
                                ocg = og
                                dv(v.tensor_scalar(tb4[:, :],
                                                   psPV[qsub][0:128, (og % 2) * 512:(og % 2) * 512 + 128],
                                                   b_t[:, oc:oc + 1], None, ALU.add))
                                m = v.tensor_tensor(y4[:, ocg * 128:(ocg + 1) * 128], tb4[:, :],
                                                    s3f[:, ocg * 128:(ocg + 1) * 128], ALU.add)
                                m.then_inc(sFE, 1)
                                v.wait_ge(sFE, 8 * gi + og + 1)
                                v.sem_inc(sY4, 1)
                            for dc in range(8):
                                j = 32 * qt + qsub * 16 + 8 + dc
                                v.wait_ge(sTr, j + 1)
                                v.tensor_copy(y4tm[:, dc * 128:(dc + 1) * 128],
                                              psS[j % 2][0:128, 0:128]).then_inc(sTe, 1)
                                v.wait_ge(sTe, j + 1)
                            dv(v.reduce_sum(stat[:, 0:1], y4tm[:, :], AX.X))
                            dv(v.tensor_tensor(y2b[:, :], y4tm[:, :], y4tm[:, :], ALU.mult))
                            dv(v.reduce_sum(stat[:, 1:2], y2b[:, :], AX.X))
                            dv(v.tensor_scalar(stat[:, 2:3], stat[:, 0:1], 1.0 / D, None, ALU.mult))
                            dv(v.tensor_tensor(stat[:, 3:4], stat[:, 2:3], stat[:, 2:3], ALU.mult))
                            dv(v.tensor_scalar(stat[:, 5:6], stat[:, 1:2], 1.0 / D, None, ALU.mult))
                            v.tensor_tensor(stat[:, 5:6], stat[:, 5:6], stat[:, 3:4],
                                            ALU.subtract).then_inc(sVr, 1)
                            v.wait_ge(sLn, 2 * gi + 2)
                            v.scalar_tensor_tensor(nmr[:, 1:2], stat[:, 2:3], -1.0, rstd[:, 1:2],
                                                   ALU.mult, ALU.mult).then_inc(sVr, 1)
                            v.wait_ge(sVr, 4 * gi + 4)
                            if gi >= 2:
                                v.wait_ge(sO, 16 * (gi - 1))
                            v.tensor_scalar(o_t[gi % 2][:, :], y4tm[:, :], rstd[:, 1:2], nmr[:, 1:2],
                                            ALU.mult, ALU.add).then_inc(sNm, 1)


    return nc


def kernel_3l(L0_attn_x, L0_attn_Wq, L0_attn_bq, L0_attn_Wk, L0_attn_bk, L0_attn_Wv,
           L0_attn_bv, L0_attn_gamma, L0_ffn_W_r, L0_ffn_W_k, L0_ffn_W_v,
           L0_ffn_mu_r, L0_ffn_mu_k, L1_attn_w_q, L1_attn_w_k, L1_attn_w_v,
           L1_ffn_G1_R, L1_ffn_G2_R, L1_ffn_G1_I, L1_ffn_G2_I,
           L1_ffn_bias_real, L1_ffn_bias_imag):
    f32 = np.float32
    asc = np.ascontiguousarray
    x = np.asarray(L0_attn_x, f32)
    ident = np.eye(128, dtype=f32)
    gam = np.full((128, 1), np.asarray(L0_attn_gamma, f32).reshape(-1)[0], f32)
    bv_bc = np.tile(np.asarray(L0_attn_bv, f32)[None, :], (128, 1))
    wqT = asc(np.asarray(L0_attn_Wq, f32).T)
    wkT = asc(np.asarray(L0_attn_Wk, f32).T)
    wvT = asc(np.asarray(L0_attn_Wv, f32).T)
    bq = asc(np.asarray(L0_attn_bq, f32)[:, None])
    bk = asc(np.asarray(L0_attn_bk, f32)[:, None])
    wrT = asc(np.asarray(L0_ffn_W_r, f32).T)
    wkT2 = asc(np.asarray(L0_ffn_W_k, f32).T)
    wvT2 = asc(np.asarray(L0_ffn_W_v, f32).T)
    mur = asc(np.asarray(L0_ffn_mu_r, f32).reshape(8, 128).T)
    muk = asc(np.asarray(L0_ffn_mu_k, f32).reshape(8, 128).T)
    wq1 = asc(np.asarray(L1_attn_w_q, f32))
    wk1 = asc(np.asarray(L1_attn_w_k, f32))
    wv1 = asc(np.asarray(L1_attn_w_v, f32))
    g1r = asc(np.asarray(L1_ffn_G1_R, f32)[0].reshape(32, 128))
    g1i = asc(np.asarray(L1_ffn_G1_I, f32)[0].reshape(32, 128))
    g2r = asc(np.asarray(L1_ffn_G2_R, f32)[:, :, 0, :].transpose(1, 0, 2).reshape(16, 64))
    g2i = asc(np.asarray(L1_ffn_G2_I, f32)[:, :, 0, :].transpose(1, 0, 2).reshape(16, 64))
    brc = asc(np.asarray(L1_ffn_bias_real, f32).reshape(4, 128).T)
    bic = asc(np.asarray(L1_ffn_bias_imag, f32).reshape(4, 128).T)
    E1, E2, E3, E4, M0, M1 = _tt_consts()

    in_maps = []
    for c in range(NC):
        b, h = c // 2, c % 2
        own = slice(h * OWN, (h + 1) * OWN)
        oth = slice((1 - h) * OWN, (2 - h) * OWN)
        x_rot = np.concatenate([x[b, oth].T, x[b, own].T], axis=1)
        gate64 = np.ones((128, 64), f32)
        if h == 0:
            gate64[:, 0:32] = 0.0
        gpr = np.full((128, 1), float(h), f32)
        in_maps.append(dict(
            x_fm=asc(x_rot), x_nat=asc(x[b, own]),
            wqT=wqT, wkT=wkT, wvT=wvT, bq=bq, bk=bk, gam=gam, bv_bc=bv_bc,
            ident=ident, wrT=wrT, wkT2=wkT2, wvT2=wvT2, mur=mur, muk=muk,
            wq1=wq1, wk1=wk1, wv1=wv1,
            g1r=g1r, g1i=g1i, g2r=g2r, g2i=g2i,
            e1m=E1, e2m=E2, e3m=E3, e4m=E4, brc=brc, bic=bic, m0=M0, m1=M1,
            gate64=gate64, gprev=gpr,
        ))
    nc = build_front()
    r1 = run_bass_kernel_spmd(nc, in_maps, core_ids=list(range(NC)))
    s1ds = [np.array(r1.results[c]["s1d"], f32) for c in range(NC)]
    for c in range(NC):
        h = c % 2
        if h == 1:
            s1ds[c][:, 127] = s1ds[c - 1][:, 128 + OWN - 1]
        else:
            s1ds[c][:, 127] = 0.0
    for c in range(NC):
        in_maps[c]["s1d"] = asc(s1ds[c])
    nc = build_mid()
    r2 = run_bass_kernel_spmd(nc, in_maps, core_ids=list(range(NC)))
    s2o = [np.asarray(r2.results[c]["s2own"], f32) for c in range(NC)]
    s2t = [np.asarray(r2.results[c]["s2t_d"], f32) for c in range(NC)]
    for c in range(NC):
        in_maps[c]["s2own"] = s2o[c]
        in_maps[c]["s2t_d"] = s2t[c]
        in_maps[c]["gbuf"] = s2o[c ^ 1]
    nc = build_back()
    r3 = run_bass_kernel_spmd(nc, in_maps, core_ids=list(range(NC)))
    out = np.empty((B, T, D), f32)
    for c in range(NC):
        b, h = c // 2, c % 2
        out[b, h * OWN:(h + 1) * OWN, :] = r3.results[c]["o_nat"]
    return out


def build_front():
    nc = _nc()
    R = T
    NKR = R // 128
    NQT = 16
    x_fm = nc.dram_tensor("x_fm", [D, T], F32, kind="ExternalInput")
    x_nat = nc.dram_tensor("x_nat", [OWN, D], F32, kind="ExternalInput")
    wqT = nc.dram_tensor("wqT", [D, 128], F32, kind="ExternalInput")
    wkT = nc.dram_tensor("wkT", [D, 128], F32, kind="ExternalInput")
    wvT = nc.dram_tensor("wvT", [D, D], F32, kind="ExternalInput")
    bq = nc.dram_tensor("bq", [128, 1], F32, kind="ExternalInput")
    bk = nc.dram_tensor("bk", [128, 1], F32, kind="ExternalInput")
    gam = nc.dram_tensor("gam", [128, 1], F32, kind="ExternalInput")
    bv_bc = nc.dram_tensor("bv_bc", [128, D], F32, kind="ExternalInput")
    ident = nc.dram_tensor("ident", [128, 128], F32, kind="ExternalInput")
    wrT = nc.dram_tensor("wrT", [D, D], F32, kind="ExternalInput")
    wkT2 = nc.dram_tensor("wkT2", [D, D], F32, kind="ExternalInput")
    wvT2 = nc.dram_tensor("wvT2", [D, D], F32, kind="ExternalInput")
    mur = nc.dram_tensor("mur", [128, 8], F32, kind="ExternalInput")
    muk = nc.dram_tensor("muk", [128, 8], F32, kind="ExternalInput")
    wq1 = nc.dram_tensor("wq1", [D, D], F32, kind="ExternalInput")
    wk1 = nc.dram_tensor("wk1", [D, D], F32, kind="ExternalInput")
    wv1 = nc.dram_tensor("wv1", [D, D], F32, kind="ExternalInput")
    g1r = nc.dram_tensor("g1r", [32, 4 * 32], F32, kind="ExternalInput")
    g1i = nc.dram_tensor("g1i", [32, 4 * 32], F32, kind="ExternalInput")
    g2r = nc.dram_tensor("g2r", [16, 4 * 16], F32, kind="ExternalInput")
    g2i = nc.dram_tensor("g2i", [16, 4 * 16], F32, kind="ExternalInput")
    e1m = nc.dram_tensor("e1m", [32, 512], F32, kind="ExternalInput")
    e2m = nc.dram_tensor("e2m", [32, 4 * 128], F32, kind="ExternalInput")
    e3m = nc.dram_tensor("e3m", [16, 512], F32, kind="ExternalInput")
    e4m = nc.dram_tensor("e4m", [16, 4 * 128], F32, kind="ExternalInput")
    brc = nc.dram_tensor("brc", [128, 4], F32, kind="ExternalInput")
    bic = nc.dram_tensor("bic", [128, 4], F32, kind="ExternalInput")
    m0 = nc.dram_tensor("m0", [128, 256], BF16, kind="ExternalInput")
    m1 = nc.dram_tensor("m1", [128, 256], BF16, kind="ExternalInput")
    gate64 = nc.dram_tensor("gate64", [128, 64], F32, kind="ExternalInput")
    gprev = nc.dram_tensor("gprev", [128, 1], F32, kind="ExternalInput")
    o_nat = nc.dram_tensor("o_nat", [OWN, D], F32, kind="ExternalOutput")
    s1d = nc.dram_tensor("s1d", [D, 128 + OWN], F32, kind="ExternalOutput")
    s1own_col = nc.dram_tensor("s1own_col", [D, 1], F32, kind="Internal")
    gbufP = nc.dram_tensor("gbufP", [2 * D, 1], F32, kind="Internal")
    s2own = nc.dram_tensor("s2own", [D, OWN], F32, kind="Internal")
    gbuf = nc.dram_tensor("gbuf", [2 * D, OWN], F32, kind="Internal")
    s2t_d = nc.dram_tensor("s2t_d", [OWN, D], F32, kind="Internal")
    q1_d = nc.dram_tensor("q1_d", [D, OWN], BF16, kind="Internal")
    k1_d = nc.dram_tensor("k1_d", [D, R], BF16, kind="Internal")

    KBH = [32 + 2 * qt + 2 for qt in range(NQT)]
    CKB = [0]
    for qt in range(NQT):
        CKB.append(CKB[-1] + KBH[qt])

    with contextlib.ExitStack() as outer:
        _e = outer.enter_context
        S = {}
        for name in ("aW aX aK aKE aV aVE aQp aQE aB4 aMs "
                     "bQp bQm bXt bS bT bL bP bV2 bE bYa bVr bLn bNm bTr bTe bO bDv bAc "
                     "cP1 cPrv cPg "
                     "cW cIn cMM cMx cDv cAc cSg cRl cOu cMu cSq cYc cLn cVr cNm cTe cOF cOT cMs "
                     "cCC "
                     "dX dK dKE dQg dQE dV dVE dOK dOQ "
                     "eQt eKt eS2r eS eP eMk eV2 eE eY3 eVr eLn eNm eTr eTe eCb eFF eFE eY4 eO eDv eAc eLg eTT eTD eLd eMs"
                     ).split():
            S[name] = _e(nc.semaphore("s_" + name))

        st_ab = outer.enter_context(contextlib.ExitStack())
        k_sb = st_ab.enter_context(nc.sbuf_tensor("k_sb", [128, T], BF16))
        v_tm = st_ab.enter_context(nc.sbuf_tensor("v_tm", [128, 64 * 1032], BF16))
        q_sb = st_ab.enter_context(nc.sbuf_tensor("q_sb", [128, OWN], BF16))

        with contextlib.ExitStack() as stA:
            _a = stA.enter_context
            wq_sb = _a(nc.sbuf_tensor("wq_sb", [128, 8 * 128], BF16))
            wk_sb = _a(nc.sbuf_tensor("wk_sb", [128, 8 * 128], BF16))
            wv_sb = _a(nc.sbuf_tensor("wv_sb", [128, 8 * 1024], BF16))
            bq_sb = _a(nc.sbuf_tensor("bq_sb", [128, 1], F32))
            bq4 = _a(nc.sbuf_tensor("bq4", [128, 1], F32))
            bk_sb = _a(nc.sbuf_tensor("bk_sb", [128, 1], F32))
            x0 = _a(nc.sbuf_tensor("x0a", [128, 8 * 512], BF16))
            x1 = _a(nc.sbuf_tensor("x1a", [128, 8 * 512], BF16))
            psK0 = _a(nc.psum_tensor("psK0a", [128, 512], F32))
            psK1 = _a(nc.psum_tensor("psK1a", [128, 512], F32))
            psV0 = _a(nc.psum_tensor("psV0a", [128, 1024], F32))
            psV1 = _a(nc.psum_tensor("psV1a", [128, 1024], F32))
            psQ = _a(nc.psum_tensor("psQa", [128, 512], F32))
            blockA = stA.enter_context(nc.Block())
            xb = [x0, x1]
            psK = [psK0, psK1]
            psV = [psV0, psV1]
            x_r = x_fm.ap().rearrange("(c p) t -> p c t", p=128)
            NT = 16

            @blockA.gpsimd
            def _(g):
                g.dma_start(wq_sb.ap().rearrange("p (c m) -> p c m", m=128),
                            wqT.ap().rearrange("(c p) m -> p c m", p=128)).then_inc(S["aW"], 16)
                g.dma_start(wk_sb.ap().rearrange("p (c m) -> p c m", m=128),
                            wkT.ap().rearrange("(c p) m -> p c m", p=128)).then_inc(S["aW"], 16)
                g.dma_start(wv_sb.ap().rearrange("p (c m) -> p c m", m=1024),
                            wvT.ap().rearrange("(c p) m -> p c m", p=128)).then_inc(S["aW"], 16)
                g.dma_start(bq_sb[:, :], bq.ap()).then_inc(S["aW"], 16)
                g.dma_start(bk_sb[:, :], bk.ap()).then_inc(S["aW"], 16)
                g.wait_ge(S["aW"], 80)
                for ti in range(NT):
                    if ti >= 2:
                        g.wait_ge(S["aV"], 4 * (ti - 1))
                    g.wait_ge(S["aX"], 16 * ti)
                    g.dma_start(
                        xb[ti % 2].ap().rearrange("p (c t) -> p c t", t=512),
                        x_r[:, :, ti * 512:(ti + 1) * 512],
                    ).then_inc(S["aX"], 16)

            @blockA.tensor
            def _(t):
                t.wait_ge(S["aW"], 80)
                for ti in range(NT):
                    t.wait_ge(S["aX"], 16 * (ti + 1))
                    x = xb[ti % 2]
                    if ti >= 2:
                        t.wait_ge(S["aKE"], ti - 1)
                    for dc in range(8):
                        m = t.matmul(psK[ti % 2][0:128, 0:512],
                                     wk_sb[:, dc * 128:(dc + 1) * 128],
                                     x[:, dc * 512:(dc + 1) * 512],
                                     start=(dc == 0), stop=(dc == 7))
                        if dc == 7:
                            m.then_inc(S["aK"], 1)
                    if ti >= 8:
                        qi = ti - 8
                        if qi >= 1:
                            t.wait_ge(S["aQE"], qi)
                        for dc in range(8):
                            m = t.matmul(psQ[0:128, 0:512],
                                         wq_sb[:, dc * 128:(dc + 1) * 128],
                                         x[:, dc * 512:(dc + 1) * 512],
                                         start=(dc == 0), stop=(dc == 7))
                            if dc == 7:
                                m.then_inc(S["aQp"], 1)
                    for ci in range(4):
                        vi = ti * 4 + ci
                        if vi >= 2:
                            t.wait_ge(S["aVE"], vi - 1)
                        for dc in range(8):
                            for n in range(2):
                                m = t.matmul(
                                    psV[vi % 2][0:128, n * 512:(n + 1) * 512],
                                    x[:, dc * 512 + ci * 128: dc * 512 + ci * 128 + 128],
                                    wv_sb[:, dc * 1024 + n * 512: dc * 1024 + (n + 1) * 512],
                                    start=(dc == 0), stop=(dc == 7))
                                if dc == 7 and n == 1:
                                    m.then_inc(S["aV"], 1)

            @blockA.scalar
            def _(s):
                s.wait_ge(S["aW"], 80)
                s.mul(bq4[:, :], bq_sb[:, :], 0.25).then_inc(S["aB4"], 1)

            @blockA.vector
            def _(v):
                v.memset(v_tm[:, 0:33024], 1.0)
                v.memset(v_tm[:, 33024:66048], 1.0).then_inc(S["aMs"], 1)
                v.wait_ge(S["aMs"], 1)
                v.wait_ge(S["aB4"], 1)
                for ti in range(NT):
                    v.wait_ge(S["aK"], ti + 1)
                    v.tensor_scalar(k_sb[:, ti * 512:(ti + 1) * 512], psK[ti % 2][0:128, 0:512],
                                    bk_sb[:, 0:1], None, ALU.add).then_inc(S["aKE"], 1)
                    if ti >= 8:
                        qi = ti - 8
                        v.wait_ge(S["aQp"], qi + 1)
                        v.tensor_scalar(q_sb[:, qi * 512:(qi + 1) * 512], psQ[0:128, 0:512],
                                        0.25, bq4[:, 0:1], ALU.mult, ALU.add).then_inc(S["aQE"], 1)
                    for ci in range(4):
                        vi = ti * 4 + ci
                        v.wait_ge(S["aV"], vi + 1)
                        v.tensor_copy(
                            v_tm.ap().rearrange("p (k h j) -> p k h j", h=8, j=129)[:, vi, :, 1:129],
                            psV[vi % 2].ap().rearrange("p (h j) -> p h j", j=128),
                        ).then_inc(S["aVE"], 1)

        with contextlib.ExitStack() as stB:
            _b = stB.enter_context
            qpad = _b(nc.sbuf_tensor("qpad", [128, 8 * 512], BF16))
            x_tm = _b(nc.sbuf_tensor("x_tm", [128, 2 * 1024], F32))
            attn = _b(nc.sbuf_tensor("attn", [128, 4 * 1024], F32))
            s1tm = _b(nc.sbuf_tensor("s1tm", [128, 1024], F32))
            s1f0 = _b(nc.sbuf_tensor("s1f0", [128, 8 * 128], F32))
            tb0 = _b(nc.sbuf_tensor("tb0", [128, 512], BF16))
            tb1 = _b(nc.sbuf_tensor("tb1", [128, 512], BF16))
            lk0 = _b(nc.sbuf_tensor("lk0", [128, 512], BF16))
            lk1 = _b(nc.sbuf_tensor("lk1", [128, 512], BF16))
            pt0 = _b(nc.sbuf_tensor("pt0", [128, 512], BF16))
            pt1 = _b(nc.sbuf_tensor("pt1", [128, 512], BF16))
            gam_sb = _b(nc.sbuf_tensor("gam_sb", [128, 1], F32))
            bv_sb = _b(nc.sbuf_tensor("bv_sb", [128, 1024], F32))
            id_sb = _b(nc.sbuf_tensor("id_sbB", [128, 128], F32))
            rg = _b(nc.sbuf_tensor("rgB", [128, 4], F32))
            stat = _b(nc.sbuf_tensor("statB", [128, 16], F32))
            rstd = _b(nc.sbuf_tensor("rstdB", [128, 4], F32))
            nmr = _b(nc.sbuf_tensor("nmrB", [128, 4], F32))
            y2 = _b(nc.sbuf_tensor("y2B", [128, 1024], BF16))
            eps_sb = _b(nc.sbuf_tensor("eps_sbB", [128, 1], F32))
            prevt = _b(nc.sbuf_tensor("prevt", [128, 8], F32))
            gprev_sb = _b(nc.sbuf_tensor("gprev_sb", [128, 1], F32))
            psS0 = _b(nc.psum_tensor("psS0b", [128, 512], F32))
            psS1 = _b(nc.psum_tensor("psS1b", [128, 512], F32))
            psP0 = _b(nc.psum_tensor("psP0b", [128, 512], F32))
            psP1 = _b(nc.psum_tensor("psP1b", [128, 512], F32))
            psP2 = _b(nc.psum_tensor("psP2b", [128, 512], F32))
            psP3 = _b(nc.psum_tensor("psP3b", [128, 512], F32))
            psT0 = _b(nc.psum_tensor("psT0b", [128, 512], F32))
            psT1 = _b(nc.psum_tensor("psT1b", [128, 512], F32))
            blockB = stB.enter_context(nc.Block())
            psS = [psS0, psS1]
            psPV = [psP0, psP1, psP2, psP3]
            psT = [psT0, psT1]
            tb = [tb0, tb1]
            lk = [lk0, lk1]
            pt = [pt0, pt1]
            s1f = [s1f0, s1f0]
            xr2 = x_nat.ap().rearrange("(g p) c -> g p c", p=128)
            sLd, sQp, sQm, sXt = S["cP1"], S["bQp"], S["bQm"], S["bXt"]
            sS, sT, sL, sP = S["bS"], S["bT"], S["bL"], S["bP"]
            sV2, sE, sYa, sVr = S["bV2"], S["bE"], S["bYa"], S["bVr"]
            sLn, sNm, sTr, sTe = S["bLn"], S["bNm"], S["bTr"], S["bTe"]
            sO, sMs, sDv, sAc = S["bO"], S["aMs"], S["bDv"], S["bAc"]

            def i_of(qs, h, kb):
                return (qs * 8 + h) * NKB + kb

            def pe_pv(t, i):
                qs, rem = divmod(i, 8 * NKB)
                h, kb = divmod(rem, NKB)
                t.wait_ge(sP, i + 1)
                if kb == 0:
                    hg = qs * 8 + h
                    if hg >= 1:
                        t.wait_ge(sE, 4 * hg)
                for u in range(4):
                    m = t.matmul(psPV[u][0:128, 0:129],
                                 pt[i % 2][:, u * 128:(u + 1) * 128],
                                 v_tm[:, kb * 1032 + h * 129: kb * 1032 + (h + 1) * 129],
                                 start=(kb == 0), stop=(kb == NKB - 1))
                    if u == 3:
                        m.then_inc(sV2, 1)

            @blockB.sync
            def _(sy):
                sy.dma_start(id_sb[:, :], ident.ap()).then_inc(sLd, 16)
                sy.dma_start(gam_sb[:, :], gam.ap()).then_inc(sLd, 16)
                sy.dma_start(bv_sb[:, :], bv_bc.ap()).then_inc(sLd, 16)
                sy.dma_start(gprev_sb[:, :], gprev.ap()).then_inc(sLd, 16)
                for qs in range(NQS):
                    sy.wait_ge(sQm, qs + 1)
                    sy.wait_ge(sQp, 128 * qs)
                    for h in range(8):
                        sy.dma_start(qpad[16 * h:16 * h + 16, h * 512:(h + 1) * 512],
                                     q_sb[16 * h:16 * h + 16, qs * 512:(qs + 1) * 512]
                                     ).then_inc(sQp, 16)
                    if qs >= 1:
                        for u in range(4):
                            g = (qs - 1) * 4 + u
                            sy.wait_ge(sTe, 8 * (g + 1))
                            sy.wait_ge(sO, 16 * g)
                            sy.dma_start(
                                s1d.ap().rearrange("(c p) t -> p c t", p=128)[
                                    :, :, 128 + (qs - 1) * 512 + u * 128: 128 + (qs - 1) * 512 + (u + 1) * 128],
                                s1f[g % 2].ap().rearrange("p (c t) -> p c t", t=128),
                            ).then_inc(sO, 16)
                for u in range(4):
                    g = (NQS - 1) * 4 + u
                    sy.wait_ge(sTe, 8 * (g + 1))
                    sy.wait_ge(sO, 16 * g)
                    sy.dma_start(
                        s1d.ap().rearrange("(c p) t -> p c t", p=128)[
                            :, :, 128 + (NQS - 1) * 512 + u * 128: 128 + (NQS - 1) * 512 + (u + 1) * 128],
                        s1f[g % 2].ap().rearrange("p (c t) -> p c t", t=128),
                    ).then_inc(sO, 16)
                sy.wait_ge(sO, 16 * 4 * NQS)

            @blockB.gpsimd
            def _(g):
                g.wait_ge(S["aVE"], 64)
                g.wait_ge(S["aKE"], 16)
                g.wait_ge(S["aQE"], 8)
                for gi in range(4 * NQS):
                    if gi >= 2:
                        g.wait_ge(sYa, gi - 1)
                    g.wait_ge(sXt, 16 * gi)
                    g.dma_start(x_tm[:, (gi % 2) * 1024:(gi % 2 + 1) * 1024],
                                xr2[gi]).then_inc(sXt, 16)
            @blockB.tensor
            def _(t):
                t.wait_ge(S["aVE"], 64)
                t.wait_ge(S["aKE"], 16)
                t.wait_ge(S["aQE"], 8)
                t.wait_ge(sLd, 64)
                for qs in range(NQS):
                    t.wait_ge(sQp, 128 * (qs + 1))
                    for h in range(8):
                        for kb in range(NKB):
                            i = i_of(qs, h, kb)
                            if i >= 2:
                                t.wait_ge(sL, i - 1)
                            t.matmul(psS[i % 2][0:128, 0:512],
                                     k_sb[:, kb * 128:(kb + 1) * 128],
                                     qpad[:, h * 512:(h + 1) * 512],
                                     start=True, stop=True).then_inc(sS, 1)
                            if i >= 1 and (i % (8 * NKB)) != 0:
                                pe_pv(t, i - 1)
                    pe_pv(t, i_of(qs, 7, NKB - 1))
                    for u in range(4):
                        gg = qs * 4 + u
                        t.wait_ge(sNm, gg + 1)
                        for dc in range(8):
                            j = gg * 8 + dc
                            if j >= 2:
                                t.wait_ge(sTe, j - 1)
                            t.transpose(psT[j % 2][0:128, 0:128],
                                        s1tm[:, dc * 128:(dc + 1) * 128],
                                        id_sb[:, :]).then_inc(sTr, 1)

            @blockB.scalar
            def _(s):
                for qs in range(NQS):
                    for h in range(8):
                        for kb in range(NKB):
                            i = i_of(qs, h, kb)
                            s.wait_ge(sS, i + 1)
                            if i >= 2:
                                s.wait_ge(sL, i - 1)
                            s.activation(tb[i % 2][:, :], psS[i % 2][0:128, 0:512],
                                         AF.Copy, scale=0.2).then_inc(sT, 1)
                            s.wait_ge(sL, i + 1)
                            if i >= 2:
                                s.wait_ge(sV2, i - 1)
                            s.activation(pt[i % 2][:, :], lk[i % 2][:, :],
                                         AF.Exp).then_inc(sP, 1)
                    s.wait_ge(sVr, 2 * qs + 1)
                    s.activation(stat[:, 12:16], stat[:, 8:12], AF.Ln,
                                 bias=eps_sb[:, 0:1]).then_inc(sAc, 1)
                    s.wait_ge(sAc, qs + 1)
                    s.activation(rstd[:, :], stat[:, 12:16], AF.Exp, scale=-0.5).then_inc(sLn, 1)

            @blockB.vector
            def _(v):
                v.memset(eps_sb[:, :], EPS).then_inc(S["aMs"], 2)
                v.wait_ge(S["aMs"], 3)
                v.wait_ge(S["aVE"], 64)
                v.wait_ge(S["aKE"], 16)
                v.wait_ge(S["aQE"], 8)
                v.wait_ge(sLd, 64)
                v.tensor_scalar(bv_sb[:, :], bv_sb[:, :], gam_sb[:, 0:1], None,
                                ALU.mult).then_inc(S["aMs"], 1)
                v.wait_ge(S["aMs"], 4)
                dv_n = [0]
                v.memset(qpad[:, :], 0.0).then_inc(sQm, 1)
                for qs in range(NQS):
                    for h in range(8):
                        for kb in range(NKB):
                            i = i_of(qs, h, kb)
                            v.wait_ge(sT, i + 1)
                            if i >= 2:
                                v.wait_ge(sP, i - 1)
                            v.tensor_tensor(lk[i % 2][:, :], psS[i % 2][0:128, 0:512],
                                            tb[i % 2][:, :], ALU.max).then_inc(sL, 1)
                        hg = qs * 8 + h
                        v.wait_ge(sV2, (hg + 1) * NKB)
                        if h == 0 and qs >= 1:
                            v.wait_ge(sNm, 4 * qs)
                        v.wait_ge(sE, 4 * hg)
                        nDv = dv_n[0]
                        for u in range(4):
                            v.reciprocal(rg[:, u:u + 1], psPV[u][0:128, 0:1]).then_inc(sDv, 1)
                            nDv += 1
                        v.wait_ge(sDv, nDv)
                        v.tensor_scalar(rg[:, :], rg[:, :], gam_sb[:, 0:1], None,
                                        ALU.mult).then_inc(sDv, 1)
                        nDv += 1
                        dv_n[0] = nDv
                        v.wait_ge(sDv, nDv)
                        for u in range(4):
                            v.scalar_tensor_tensor(
                                attn[:, u * 1024 + h * 128: u * 1024 + (h + 1) * 128],
                                psPV[u][0:128, 1:129], rg[:, u:u + 1],
                                bv_sb[:, h * 128:(h + 1) * 128],
                                ALU.mult, ALU.add).then_inc(sE, 1)
                    if qs + 1 < NQS:
                        v.wait_ge(sS, 512 * (qs + 1))
                        v.memset(qpad[:, :], 0.0).then_inc(sQm, 1)
                    v.wait_ge(sE, 32 * (qs + 1))
                    v.wait_ge(sVr, 2 * qs)
                    nDv = dv_n[0]
                    for u in range(4):
                        gi = qs * 4 + u
                        v.wait_ge(sXt, 16 * (gi + 1))
                        v.scalar_tensor_tensor(attn[:, u * 1024:(u + 1) * 1024],
                                               x_tm[:, (gi % 2) * 1024:(gi % 2 + 1) * 1024],
                                               2.0,
                                               attn[:, u * 1024:(u + 1) * 1024],
                                               ALU.mult, ALU.add).then_inc(sYa, 1)
                        v.wait_ge(sYa, gi + 1)
                        v.reduce_sum(stat[:, u:u + 1], attn[:, u * 1024:(u + 1) * 1024],
                                     AX.X).then_inc(sDv, 1)
                        nDv += 1
                        v.tensor_tensor(y2[:, :], attn[:, u * 1024:(u + 1) * 1024],
                                        attn[:, u * 1024:(u + 1) * 1024],
                                        ALU.mult).then_inc(sDv, 1)
                        nDv += 1
                        v.wait_ge(sDv, nDv)
                        v.reduce_sum(stat[:, 4 + u:5 + u], y2[:, :], AX.X).then_inc(sDv, 1)
                        nDv += 1
                    v.wait_ge(sDv, nDv)
                    v.tensor_scalar(stat[:, 0:4], stat[:, 0:4], 1.0 / D, None,
                                    ALU.mult).then_inc(sDv, 1)
                    nDv += 1
                    v.tensor_scalar(stat[:, 4:8], stat[:, 4:8], 1.0 / D, None,
                                    ALU.mult).then_inc(sDv, 1)
                    nDv += 1
                    v.wait_ge(sDv, nDv)
                    v.tensor_tensor(stat[:, 12:16], stat[:, 0:4], stat[:, 0:4],
                                    ALU.mult).then_inc(sDv, 1)
                    nDv += 1
                    v.wait_ge(sDv, nDv)
                    v.tensor_tensor(stat[:, 8:12], stat[:, 4:8], stat[:, 12:16],
                                    ALU.subtract).then_inc(sVr, 1)
                    dv_n[0] = nDv
                    v.wait_ge(sLn, qs + 1)
                    v.scalar_tensor_tensor(nmr[:, :], stat[:, 0:4], -1.0, rstd[:, :],
                                           ALU.mult, ALU.mult).then_inc(sVr, 1)
                    v.wait_ge(sVr, 2 * qs + 2)
                    for u in range(4):
                        gg = qs * 4 + u
                        if gg >= 1:
                            v.wait_ge(sTr, 8 * gg)
                        v.tensor_scalar(s1tm[:, :], attn[:, u * 1024:(u + 1) * 1024],
                                        rstd[:, u:u + 1], nmr[:, u:u + 1],
                                        ALU.mult, ALU.add).then_inc(sNm, 1)
                        for dc in range(8):
                            j = gg * 8 + dc
                            v.wait_ge(sTr, j + 1)
                            if dc == 0 and gg >= 1:
                                v.wait_ge(sO, 16 * gg)
                            v.tensor_copy(s1f[gg % 2][:, dc * 128:(dc + 1) * 128],
                                          psT[j % 2][0:128, 0:128]).then_inc(sTe, 1)

    return nc


def build_mid():
    nc = _nc()
    R = T
    NKR = R // 128
    NQT = 16
    x_fm = nc.dram_tensor("x_fm", [D, T], F32, kind="ExternalInput")
    x_nat = nc.dram_tensor("x_nat", [OWN, D], F32, kind="ExternalInput")
    wqT = nc.dram_tensor("wqT", [D, 128], F32, kind="ExternalInput")
    wkT = nc.dram_tensor("wkT", [D, 128], F32, kind="ExternalInput")
    wvT = nc.dram_tensor("wvT", [D, D], F32, kind="ExternalInput")
    bq = nc.dram_tensor("bq", [128, 1], F32, kind="ExternalInput")
    bk = nc.dram_tensor("bk", [128, 1], F32, kind="ExternalInput")
    gam = nc.dram_tensor("gam", [128, 1], F32, kind="ExternalInput")
    bv_bc = nc.dram_tensor("bv_bc", [128, D], F32, kind="ExternalInput")
    ident = nc.dram_tensor("ident", [128, 128], F32, kind="ExternalInput")
    wrT = nc.dram_tensor("wrT", [D, D], F32, kind="ExternalInput")
    wkT2 = nc.dram_tensor("wkT2", [D, D], F32, kind="ExternalInput")
    wvT2 = nc.dram_tensor("wvT2", [D, D], F32, kind="ExternalInput")
    mur = nc.dram_tensor("mur", [128, 8], F32, kind="ExternalInput")
    muk = nc.dram_tensor("muk", [128, 8], F32, kind="ExternalInput")
    wq1 = nc.dram_tensor("wq1", [D, D], F32, kind="ExternalInput")
    wk1 = nc.dram_tensor("wk1", [D, D], F32, kind="ExternalInput")
    wv1 = nc.dram_tensor("wv1", [D, D], F32, kind="ExternalInput")
    g1r = nc.dram_tensor("g1r", [32, 4 * 32], F32, kind="ExternalInput")
    g1i = nc.dram_tensor("g1i", [32, 4 * 32], F32, kind="ExternalInput")
    g2r = nc.dram_tensor("g2r", [16, 4 * 16], F32, kind="ExternalInput")
    g2i = nc.dram_tensor("g2i", [16, 4 * 16], F32, kind="ExternalInput")
    e1m = nc.dram_tensor("e1m", [32, 512], F32, kind="ExternalInput")
    e2m = nc.dram_tensor("e2m", [32, 4 * 128], F32, kind="ExternalInput")
    e3m = nc.dram_tensor("e3m", [16, 512], F32, kind="ExternalInput")
    e4m = nc.dram_tensor("e4m", [16, 4 * 128], F32, kind="ExternalInput")
    brc = nc.dram_tensor("brc", [128, 4], F32, kind="ExternalInput")
    bic = nc.dram_tensor("bic", [128, 4], F32, kind="ExternalInput")
    m0 = nc.dram_tensor("m0", [128, 256], BF16, kind="ExternalInput")
    m1 = nc.dram_tensor("m1", [128, 256], BF16, kind="ExternalInput")
    gate64 = nc.dram_tensor("gate64", [128, 64], F32, kind="ExternalInput")
    gprev = nc.dram_tensor("gprev", [128, 1], F32, kind="ExternalInput")
    o_nat = nc.dram_tensor("o_nat", [OWN, D], F32, kind="ExternalOutput")
    s1d = nc.dram_tensor("s1d", [D, 128 + OWN], F32, kind="ExternalInput")
    s1own_col = nc.dram_tensor("s1own_col", [D, 1], F32, kind="Internal")
    gbufP = nc.dram_tensor("gbufP", [2 * D, 1], F32, kind="Internal")
    s2own = nc.dram_tensor("s2own", [D, OWN], F32, kind="ExternalOutput")
    gbuf = nc.dram_tensor("gbuf", [2 * D, OWN], F32, kind="Internal")
    s2t_d = nc.dram_tensor("s2t_d", [OWN, D], F32, kind="ExternalOutput")
    q1_d = nc.dram_tensor("q1_d", [D, OWN], BF16, kind="Internal")
    k1_d = nc.dram_tensor("k1_d", [D, R], BF16, kind="Internal")

    KBH = [32 + 2 * qt + 2 for qt in range(NQT)]
    CKB = [0]
    for qt in range(NQT):
        CKB.append(CKB[-1] + KBH[qt])

    with contextlib.ExitStack() as outer:
        _e = outer.enter_context
        S = {}
        for name in ("aW aX aK aKE aV aVE aQp aQE aB4 aMs "
                     "bQp bQm bXt bS bT bL bP bV2 bE bYa bVr bLn bNm bTr bTe bO bDv bAc "
                     "cP1 cPrv cPg "
                     "cW cIn cMM cMx cDv cAc cSg cRl cOu cMu cSq cYc cLn cVr cNm cTe cOF cOT cMs "
                     "cCC "
                     "dX dK dKE dQg dQE dV dVE dOK dOQ "
                     "eQt eKt eS2r eS eP eMk eV2 eE eY3 eVr eLn eNm eTr eTe eCb eFF eFE eY4 eO eDv eAc eLg eTT eTD eLd eMs"
                     ).split():
            S[name] = _e(nc.semaphore("s_" + name))


        NTS = 8
        NGT = 60
        NBx = 4
        marks = []
        for ts in range(NTS):
            for oc in range(8):
                marks.append(("cSg", 8 * ts + oc + 1))
            for oc in range(8):
                marks.append(("cRl", 8 * ts + oc + 1))
            for oc in range(8):
                marks.append(("cOu", 8 * ts + oc + 1))
            marks.append(("cMu", ts + 1))
            marks.append(("cVr", 2 * ts + 1))
            marks.append(("cNm", 8 * (ts + 1)))
            marks.append(("cNm", 8 * (ts + 1)))
            for j in range(32):
                marks.append(("cTe", 32 * ts + j + 1))

        def gidx(ts, g):
            return ts * NGT + g

        with contextlib.ExitStack() as stC:
            _c = stC.enter_context
            wr_sb = _c(nc.sbuf_tensor("wr_sb", [128, 8 * 1024], BF16))
            wk2_sb = _c(nc.sbuf_tensor("wk2_sb", [128, 8 * 1024], BF16))
            wv2_sb = _c(nc.sbuf_tensor("wv2_sb", [128, 8 * 1024], BF16))
            mur_sb = _c(nc.sbuf_tensor("mur_sb", [128, 8], F32))
            muk_sb = _c(nc.sbuf_tensor("muk_sb", [128, 8], F32))
            omr_sb = _c(nc.sbuf_tensor("omr_sb", [128, 8], F32))
            omk_sb = _c(nc.sbuf_tensor("omk_sb", [128, 8], F32))
            idC_sb = _c(nc.sbuf_tensor("idC_sb", [128, 128], F32))
            xc0 = _c(nc.sbuf_tensor("xc0", [128, 8 * 513], F32))
            xc1 = _c(nc.sbuf_tensor("xc1", [128, 8 * 513], F32))
            xr_bf = _c(nc.sbuf_tensor("xr_bf", [128, 8 * 512], BF16))
            xk_bf = _c(nc.sbuf_tensor("xk_bf", [128, 8 * 512], BF16))
            t1 = _c(nc.sbuf_tensor("t1C", [128, 512], F32))
            r_bf = _c(nc.sbuf_tensor("r_bf", [128, 8 * 512], BF16))
            t2_bf = _c(nc.sbuf_tensor("t2_bf", [128, 512], BF16))
            ka_bf = _c(nc.sbuf_tensor("ka_bf", [128, 8 * 512], BF16))
            t3 = _c(nc.sbuf_tensor("t3C", [128, 512], F32))
            y_sb = _c(nc.sbuf_tensor("y_sbC", [128, 8 * 512], F32))
            y_bf = _c(nc.sbuf_tensor("y_bfC", [128, 8 * 512], BF16))
            y2_bf = _c(nc.sbuf_tensor("y2_bfC", [128, 8 * 512], BF16))
            ones_bf = _c(nc.sbuf_tensor("ones_bfC", [128, 1], BF16))
            ones_f = _c(nc.sbuf_tensor("ones_fC", [1, 128], F32))
            epsC = _c(nc.sbuf_tensor("epsC", [1, 1], F32))
            mu_row = _c(nc.sbuf_tensor("mu_rowC", [1, 512], F32))
            mu2_row = _c(nc.sbuf_tensor("mu2_rowC", [1, 512], F32))
            var_row = _c(nc.sbuf_tensor("var_rowC", [1, 512], F32))
            lnv_row = _c(nc.sbuf_tensor("lnv_rowC", [1, 512], F32))
            rstd_row = _c(nc.sbuf_tensor("rstd_rowC", [1, 512], F32))
            mr_row = _c(nc.sbuf_tensor("mr_rowC", [1, 512], F32))
            s2_sb = _c(nc.sbuf_tensor("s2_sbC", [128, 8 * 512], F32))
            s2t_sb0 = _c(nc.sbuf_tensor("s2t_sb0C", [128, 8 * 128], F32))
            s2t_sb1 = _c(nc.sbuf_tensor("s2t_sb1C", [128, 8 * 128], F32))
            psX0 = _c(nc.psum_tensor("psX0C", [128, 512], F32))
            psX1 = _c(nc.psum_tensor("psX1C", [128, 512], F32))
            psX2 = _c(nc.psum_tensor("psX2C", [128, 512], F32))
            psX3 = _c(nc.psum_tensor("psX3C", [128, 512], F32))
            blockC = stC.enter_context(nc.Block())
            xb = [xc0, xc1]
            psX = [psX0, psX1, psX2, psX3]
            s2t_sb = [s2t_sb0, s2t_sb1]
            semsC = dict(cSg=S["cSg"], cRl=S["cRl"], cOu=S["cOu"], cMu=S["cMu"],
                         cVr=S["cVr"], cNm=S["cNm"], cTe=S["cTe"])
            sW, sIn, sMM, sMx = S["cW"], S["cIn"], S["cMM"], S["cMx"]
            sDv, sAc, sSg, sRl = S["cDv"], S["cAc"], S["cSg"], S["cRl"]
            sOu, sMu, sSq2, sYc = S["cOu"], S["cMu"], S["cSq"], S["cYc"]
            sLn, sVr, sNm, sTe = S["cLn"], S["cVr"], S["cNm"], S["cTe"]
            sOF, sOT, sMs = S["cOF"], S["cOT"], S["cMs"]

            def bank_wait(t, g):
                if g >= NBx:
                    nm, val = marks[g - NBx]
                    t.wait_ge(semsC[nm], val)

            @blockC.gpsimd
            def _(g):
                g.dma_start(wr_sb.ap().rearrange("p (c m) -> p c m", m=1024),
                            wrT.ap().rearrange("(c p) m -> p c m", p=128)).then_inc(sW, 16)
                g.dma_start(wk2_sb.ap().rearrange("p (c m) -> p c m", m=1024),
                            wkT2.ap().rearrange("(c p) m -> p c m", p=128)).then_inc(sW, 16)
                g.dma_start(wv2_sb.ap().rearrange("p (c m) -> p c m", m=1024),
                            wvT2.ap().rearrange("(c p) m -> p c m", p=128)).then_inc(sW, 16)
                g.dma_start(mur_sb[:, :], mur.ap()).then_inc(sW, 16)
                g.dma_start(muk_sb[:, :], muk.ap()).then_inc(sW, 16)
                g.dma_start(idC_sb[:, :], ident.ap()).then_inc(sW, 16)
                for ts in range(NTS):
                    if ts >= 2:
                        g.wait_ge(sMx, 16 * (ts - 1))
                        g.wait_ge(sYc, 9 * (ts - 1))
                    g.wait_ge(sIn, 16 * ts)
                    g.dma_start(
                        xb[ts % 2].ap().rearrange("p (c t) -> p c t", t=513),
                        s1d.ap().rearrange("(c p) t -> p c t", p=128)[:, :, 127 + ts * 512: 127 + ts * 512 + 513],
                    ).then_inc(sIn, 16)
            @blockC.tensor
            def _(t):
                t.wait_ge(sW, 96)
                t.wait_ge(sMs, 1)
                for ts in range(NTS):
                    for stage, (wsb, rhs) in enumerate([(wr_sb, xr_bf), (wk2_sb, xk_bf)]):
                        t.wait_ge(sMx, 16 * ts + (stage + 1) * 8)
                        for oc in range(8):
                            g = gidx(ts, stage * 8 + oc)
                            bank_wait(t, g)
                            for ic in range(8):
                                m = t.matmul(psX[g % NBx][0:128, 0:512],
                                             wsb[:, ic * 1024 + oc * 128: ic * 1024 + (oc + 1) * 128],
                                             rhs[:, ic * 512:(ic + 1) * 512],
                                             start=(ic == 0), stop=(ic == 7))
                                if ic == 7:
                                    m.then_inc(sMM, 1)
                    t.wait_ge(sSq2, 8 * (ts + 1))
                    for oc in range(8):
                        g = gidx(ts, 16 + oc)
                        bank_wait(t, g)
                        for ic in range(8):
                            m = t.matmul(psX[g % NBx][0:128, 0:512],
                                         wv2_sb[:, ic * 1024 + oc * 128: ic * 1024 + (oc + 1) * 128],
                                         ka_bf[:, ic * 512:(ic + 1) * 512],
                                         start=(ic == 0), stop=(ic == 7))
                            if ic == 7:
                                m.then_inc(sMM, 1)
                    g = gidx(ts, 24)
                    t.wait_ge(sYc, 9 * (ts + 1))
                    bank_wait(t, g)
                    for dc in range(8):
                        m = t.matmul(psX[g % NBx][0:1, 0:512], ones_bf[:, 0:1],
                                     y_bf[:, dc * 512:(dc + 1) * 512],
                                     start=(dc == 0), stop=(dc == 7))
                        if dc == 7:
                            m.then_inc(sMM, 1)
                    g = gidx(ts, 25)
                    bank_wait(t, g)
                    for dc in range(8):
                        m = t.matmul(psX[g % NBx][0:1, 0:512], ones_bf[:, 0:1],
                                     y2_bf[:, dc * 512:(dc + 1) * 512],
                                     start=(dc == 0), stop=(dc == 7))
                        if dc == 7:
                            m.then_inc(sMM, 1)
                    g = gidx(ts, 26)
                    t.wait_ge(sLn, ts + 1)
                    bank_wait(t, g)
                    t.matmul(psX[g % NBx][0:128, 0:512], ones_f[0:1, :], rstd_row[0:1, :],
                             start=True, stop=True).then_inc(sMM, 1)
                    g = gidx(ts, 27)
                    t.wait_ge(sVr, 2 * ts + 2)
                    bank_wait(t, g)
                    t.matmul(psX[g % NBx][0:128, 0:512], ones_f[0:1, :], mr_row[0:1, :],
                             start=True, stop=True).then_inc(sMM, 1)
                    t.wait_ge(sNm, 8 * (ts + 1))
                    for u in range(4):
                        for dc in range(8):
                            g = gidx(ts, 28 + u * 8 + dc)
                            bank_wait(t, g)
                            t.transpose(psX[g % NBx][0:128, 0:128],
                                        s2_sb[:, dc * 512 + u * 128: dc * 512 + (u + 1) * 128],
                                        idC_sb[:, :]).then_inc(sMM, 1)

            @blockC.scalar
            def _(s):
                for ts in range(NTS):
                    for oc in range(8):
                        g = gidx(ts, oc)
                        s.wait_ge(sMM, g + 1)
                        s.activation(r_bf[:, oc * 512:(oc + 1) * 512], psX[g % NBx][0:128, 0:512],
                                     AF.Sigmoid).then_inc(sSg, 1)
                    for oc in range(8):
                        s.wait_ge(sRl, 8 * ts + oc + 1)
                        s.activation(ka_bf[:, oc * 512:(oc + 1) * 512], t2_bf[:, :],
                                     AF.Square).then_inc(sSq2, 1)
                    s.wait_ge(sVr, 2 * ts + 1)
                    s.activation(lnv_row[0:1, :], var_row[0:1, :], AF.Ln,
                                 bias=epsC[0:1, 0:1]).then_inc(sAc, 1)
                    s.wait_ge(sAc, ts + 1)
                    s.activation(rstd_row[0:1, :], lnv_row[0:1, :], AF.Exp,
                                 scale=-0.5).then_inc(sLn, 1)

            @blockC.vector
            def _(v):
                v.memset(ones_bf[:, :], 1.0 / D)
                v.memset(ones_f[:, :], 1.0)
                v.memset(epsC[:, :], EPS).then_inc(sMs, 1)
                v.wait_ge(sMs, 1)
                v.wait_ge(sW, 96)
                nDv = 0
                v.tensor_scalar(omr_sb[:, :], mur_sb[:, :], -1.0, 1.0, ALU.mult,
                                ALU.add).then_inc(sDv, 1)
                v.tensor_scalar(omk_sb[:, :], muk_sb[:, :], -1.0, 1.0, ALU.mult,
                                ALU.add).then_inc(sDv, 1)
                nDv += 2
                v.wait_ge(sDv, nDv)
                for ts in range(NTS):
                    v.wait_ge(sIn, 16 * (ts + 1))
                    if ts >= 1:
                        v.wait_ge(sMM, gidx(ts - 1, 16))
                    for stage, (mu_t, om_t, dst) in enumerate(
                            [(mur_sb, omr_sb, xr_bf), (muk_sb, omk_sb, xk_bf)]):
                        for dc in range(8):
                            x = xb[ts % 2]
                            v.wait_ge(sMx, 16 * ts + stage * 8 + dc)
                            v.tensor_scalar(t1[:, :], x[:, dc * 513: dc * 513 + 512],
                                            om_t[:, dc:dc + 1], None, ALU.mult).then_inc(sDv, 1)
                            nDv += 1
                            v.wait_ge(sDv, nDv)
                            v.scalar_tensor_tensor(dst[:, dc * 512:(dc + 1) * 512],
                                                   x[:, dc * 513 + 1: dc * 513 + 513],
                                                   mu_t[:, dc:dc + 1],
                                                   t1[:, :], ALU.mult,
                                                   ALU.add).then_inc(sMx, 1)
                    for oc in range(8):
                        g = gidx(ts, 8 + oc)
                        v.wait_ge(sMM, g + 1)
                        if oc >= 1:
                            v.wait_ge(sSq2, 8 * ts + oc)
                        v.tensor_scalar(t2_bf[:, :], psX[g % NBx][0:128, 0:512],
                                        0.0, None, ALU.max).then_inc(sRl, 1)
                    for oc in range(8):
                        g = gidx(ts, 16 + oc)
                        v.wait_ge(sMM, g + 1)
                        v.wait_ge(sSg, 8 * ts + oc + 1)
                        v.wait_ge(sDv, nDv)
                        v.tensor_tensor(t3[:, :], psX[g % NBx][0:128, 0:512],
                                        r_bf[:, oc * 512:(oc + 1) * 512],
                                        ALU.mult).then_inc(sOu, 1)
                        v.wait_ge(sOu, 8 * ts + oc + 1)
                        if ts >= 1 and oc == 0:
                            v.wait_ge(sMM, gidx(ts - 1, 26))
                        v.tensor_tensor(y_sb[:, oc * 512:(oc + 1) * 512], t3[:, :],
                                        xb[ts % 2][:, oc * 513 + 1: oc * 513 + 513],
                                        ALU.add).then_inc(sDv, 1)
                        nDv += 1
                        v.wait_ge(sDv, nDv)
                        v.tensor_copy(y_bf[:, oc * 512:(oc + 1) * 512],
                                      y_sb[:, oc * 512:(oc + 1) * 512]).then_inc(sYc, 1)
                    v.wait_ge(sYc, 9 * ts + 8)
                    v.tensor_tensor(y2_bf[:, :], y_bf[:, :], y_bf[:, :],
                                    ALU.mult).then_inc(sYc, 1)
                    g = gidx(ts, 24)
                    v.wait_ge(sMM, g + 1)
                    v.tensor_copy(mu_row[0:1, :], psX[g % NBx][0:1, 0:512]).then_inc(sMu, 1)
                    g = gidx(ts, 25)
                    v.wait_ge(sMM, g + 1)
                    v.wait_ge(sMu, ts + 1)
                    v.tensor_tensor(mu2_row[0:1, :], mu_row[0:1, :], mu_row[0:1, :],
                                    ALU.mult).then_inc(sDv, 1)
                    nDv += 1
                    v.wait_ge(sDv, nDv)
                    v.tensor_tensor(var_row[0:1, :], psX[g % NBx][0:1, 0:512],
                                    mu2_row[0:1, :], ALU.subtract).then_inc(sVr, 1)
                    v.wait_ge(sLn, ts + 1)
                    v.tensor_tensor(mr_row[0:1, :], mu_row[0:1, :], rstd_row[0:1, :],
                                    ALU.mult).then_inc(sVr, 1)
                    gR = gidx(ts, 26)
                    gM = gidx(ts, 27)
                    v.wait_ge(sMM, gM + 1)
                    if ts >= 1:
                        v.wait_ge(sMM, gidx(ts, 0))
                        v.wait_ge(sOF, 16 * ts)
                    for dc in range(8):
                        v.wait_ge(sNm, 8 * ts + dc)
                        v.tensor_tensor(t3[:, :], y_sb[:, dc * 512:(dc + 1) * 512],
                                        psX[gR % NBx][0:128, 0:512], ALU.mult).then_inc(sDv, 1)
                        nDv += 1
                        v.wait_ge(sDv, nDv)
                        v.tensor_tensor(s2_sb[:, dc * 512:(dc + 1) * 512], t3[:, :],
                                        psX[gM % NBx][0:128, 0:512],
                                        ALU.subtract).then_inc(sNm, 1)
                    for u in range(4):
                        for dc in range(8):
                            g = gidx(ts, 28 + u * 8 + dc)
                            gu = ts * 4 + u
                            v.wait_ge(sMM, g + 1)
                            if dc == 0 and gu >= 2:
                                v.wait_ge(sOT, 16 * (gu - 1))
                            v.tensor_copy(s2t_sb[gu % 2][:, dc * 128:(dc + 1) * 128],
                                          psX[g % NBx][0:128, 0:128]).then_inc(sTe, 1)

            @blockC.sync
            def _(sy):
                for ts in range(NTS):
                    sy.wait_ge(sNm, 8 * (ts + 1))
                    sy.wait_ge(sOF, 16 * ts)
                    sy.dma_start(
                        s2own.ap().rearrange("(c p) t -> p c t", p=128)[:, :, ts * 512:(ts + 1) * 512],
                        s2_sb.ap().rearrange("p (c t) -> p c t", t=512),
                    ).then_inc(sOF, 16)
                    for u in range(4):
                        gu = ts * 4 + u
                        sy.wait_ge(sTe, 8 * (gu + 1))
                        sy.wait_ge(sOT, 16 * gu)
                        sy.dma_start(
                            s2t_d.ap().rearrange("(g p) c -> g p c", p=128)[gu],
                            s2t_sb[gu % 2][:, :],
                        ).then_inc(sOT, 16)
                sy.wait_ge(S["cOF"], 16 * NTS)
                sy.wait_ge(S["cOT"], 16 * 4 * NTS)
    return nc


def build_back():
    nc = _nc()
    R = T
    NKR = R // 128
    NQT = 16
    x_fm = nc.dram_tensor("x_fm", [D, T], F32, kind="ExternalInput")
    x_nat = nc.dram_tensor("x_nat", [OWN, D], F32, kind="ExternalInput")
    wqT = nc.dram_tensor("wqT", [D, 128], F32, kind="ExternalInput")
    wkT = nc.dram_tensor("wkT", [D, 128], F32, kind="ExternalInput")
    wvT = nc.dram_tensor("wvT", [D, D], F32, kind="ExternalInput")
    bq = nc.dram_tensor("bq", [128, 1], F32, kind="ExternalInput")
    bk = nc.dram_tensor("bk", [128, 1], F32, kind="ExternalInput")
    gam = nc.dram_tensor("gam", [128, 1], F32, kind="ExternalInput")
    bv_bc = nc.dram_tensor("bv_bc", [128, D], F32, kind="ExternalInput")
    ident = nc.dram_tensor("ident", [128, 128], F32, kind="ExternalInput")
    wrT = nc.dram_tensor("wrT", [D, D], F32, kind="ExternalInput")
    wkT2 = nc.dram_tensor("wkT2", [D, D], F32, kind="ExternalInput")
    wvT2 = nc.dram_tensor("wvT2", [D, D], F32, kind="ExternalInput")
    mur = nc.dram_tensor("mur", [128, 8], F32, kind="ExternalInput")
    muk = nc.dram_tensor("muk", [128, 8], F32, kind="ExternalInput")
    wq1 = nc.dram_tensor("wq1", [D, D], F32, kind="ExternalInput")
    wk1 = nc.dram_tensor("wk1", [D, D], F32, kind="ExternalInput")
    wv1 = nc.dram_tensor("wv1", [D, D], F32, kind="ExternalInput")
    g1r = nc.dram_tensor("g1r", [32, 4 * 32], F32, kind="ExternalInput")
    g1i = nc.dram_tensor("g1i", [32, 4 * 32], F32, kind="ExternalInput")
    g2r = nc.dram_tensor("g2r", [16, 4 * 16], F32, kind="ExternalInput")
    g2i = nc.dram_tensor("g2i", [16, 4 * 16], F32, kind="ExternalInput")
    e1m = nc.dram_tensor("e1m", [32, 512], F32, kind="ExternalInput")
    e2m = nc.dram_tensor("e2m", [32, 4 * 128], F32, kind="ExternalInput")
    e3m = nc.dram_tensor("e3m", [16, 512], F32, kind="ExternalInput")
    e4m = nc.dram_tensor("e4m", [16, 4 * 128], F32, kind="ExternalInput")
    brc = nc.dram_tensor("brc", [128, 4], F32, kind="ExternalInput")
    bic = nc.dram_tensor("bic", [128, 4], F32, kind="ExternalInput")
    m0 = nc.dram_tensor("m0", [128, 256], BF16, kind="ExternalInput")
    m1 = nc.dram_tensor("m1", [128, 256], BF16, kind="ExternalInput")
    gate64 = nc.dram_tensor("gate64", [128, 64], F32, kind="ExternalInput")
    gprev = nc.dram_tensor("gprev", [128, 1], F32, kind="ExternalInput")
    o_nat = nc.dram_tensor("o_nat", [OWN, D], F32, kind="ExternalOutput")
    s1d = nc.dram_tensor("s1d", [D, 128 + OWN], F32, kind="Internal")
    s1own_col = nc.dram_tensor("s1own_col", [D, 1], F32, kind="Internal")
    gbufP = nc.dram_tensor("gbufP", [2 * D, 1], F32, kind="Internal")
    s2own = nc.dram_tensor("s2own", [D, OWN], F32, kind="ExternalInput")
    gbuf = nc.dram_tensor("gbuf", [D, OWN], F32, kind="ExternalInput")
    s2t_d = nc.dram_tensor("s2t_d", [OWN, D], F32, kind="ExternalInput")
    q1_d = nc.dram_tensor("q1_d", [D, OWN], BF16, kind="Internal")
    k1_d = nc.dram_tensor("k1_d", [D, R], BF16, kind="Internal")

    KBH = [32 + 2 * qt + 2 for qt in range(NQT)]
    CKB = [0]
    for qt in range(NQT):
        CKB.append(CKB[-1] + KBH[qt])

    with contextlib.ExitStack() as outer:
        _e = outer.enter_context
        S = {}
        for name in ("aW aX aK aKE aV aVE aQp aQE aB4 aMs "
                     "bQp bQm bXt bS bT bL bP bV2 bE bYa bVr bLn bNm bTr bTe bO bDv bAc "
                     "cP1 cPrv cPg "
                     "cW cIn cMM cMx cDv cAc cSg cRl cOu cMu cSq cYc cLn cVr cNm cTe cOF cOT cMs "
                     "cCC "
                     "dX dK dKE dQg dQE dV dVE dOK dOQ "
                     "eQt eKt eS2r eS eP eMk eV2 eE eY3 eVr eLn eNm eTr eTe eCb eFF eFE eY4 eO eDv eAc eLg eTT eTD eLd eMs"
                     ).split():
            S[name] = _e(nc.semaphore("s_" + name))

        with contextlib.ExitStack() as stDE:
            v1_tm = stDE.enter_context(nc.sbuf_tensor("v1_tm", [128, NKR * 1025], BF16))

            with contextlib.ExitStack() as stD:
                _d = stD.enter_context
                wqa = _d(nc.sbuf_tensor("wqa", [128, 8 * 1024], BF16))
                wka = _d(nc.sbuf_tensor("wka", [128, 8 * 1024], BF16))
                wva = _d(nc.sbuf_tensor("wva", [128, 8 * 1024], BF16))
                xd = _d(nc.sbuf_tensor("xd", [128, 8 * 512], BF16))
                k_st = _d(nc.sbuf_tensor("k_stD", [128, 8 * 512], BF16))
                q_st = _d(nc.sbuf_tensor("q_stD", [128, 8 * 512], BF16))
                gate_f = _d(nc.sbuf_tensor("gate_f", [128, 64], F32))
                gate_bf = _d(nc.sbuf_tensor("gate_bf", [128, 64], BF16))
                psKd0 = _d(nc.psum_tensor("psKd0", [128, 512], F32))
                psKd1 = _d(nc.psum_tensor("psKd1", [128, 512], F32))
                psQd0 = _d(nc.psum_tensor("psQd0", [128, 512], F32))
                psQd1 = _d(nc.psum_tensor("psQd1", [128, 512], F32))
                psVd0 = _d(nc.psum_tensor("psVd0", [128, 1024], F32))
                psVd1 = _d(nc.psum_tensor("psVd1", [128, 1024], F32))
                blockD = stD.enter_context(nc.Block())
                psKD = [psKd0, psKd1]
                psQD = [psQd0, psQd1]
                psVD = [psVd0, psVd1]
                NTR = 16
                sXd, sKd, sKEd, sQgd = S["dX"], S["dK"], S["dKE"], S["dQg"]
                sQEd, sVd, sVEd = S["dQE"], S["dV"], S["dVE"]
                sOKd, sOQd = S["dOK"], S["dOQ"]

                @blockD.gpsimd
                def _(g):
                    g.dma_start(wqa.ap().rearrange("p (c m) -> p c m", m=1024),
                                wq1.ap().rearrange("(c p) m -> p c m", p=128)).then_inc(S["dX"], 16)
                    g.dma_start(wka.ap().rearrange("p (c m) -> p c m", m=1024),
                                wk1.ap().rearrange("(c p) m -> p c m", p=128)).then_inc(S["dX"], 16)
                    g.dma_start(wva.ap().rearrange("p (c m) -> p c m", m=1024),
                                wv1.ap().rearrange("(c p) m -> p c m", p=128)).then_inc(S["dX"], 16)
                    g.dma_start(gate_f[:, :], gate64.ap()).then_inc(S["dX"], 16)
                    g.dma_start(gate_bf[:, :], gate64.ap()).then_inc(S["dX"], 16)
                    for ti in range(NTR):
                        if ti >= 1:
                            g.wait_ge(sVd, 4 * ti)
                        g.wait_ge(S["dX"], 80 + 16 * ti)
                        if ti < 8:
                            src = gbuf.ap()[0:D, :].rearrange("(c p) t -> p c t", p=128)[
                                :, :, ti * 512:(ti + 1) * 512]
                        else:
                            src = s2own.ap().rearrange("(c p) t -> p c t", p=128)[
                                :, :, (ti - 8) * 512:(ti - 8) * 512 + 512]
                        g.dma_start(xd.ap().rearrange("p (c t) -> p c t", t=512),
                                    src).then_inc(S["dX"], 16)

                @blockD.tensor
                def _(t):
                    for ti in range(NTR):
                        t.wait_ge(S["dX"], 80 + 16 * (ti + 1))
                        for oc in range(8):
                            kg = ti * 8 + oc
                            if kg >= 2:
                                t.wait_ge(sKEd, kg - 1)
                            for ic in range(8):
                                m = t.matmul(psKD[kg % 2][0:128, 0:512],
                                             wka[:, ic * 1024 + oc * 128: ic * 1024 + (oc + 1) * 128],
                                             xd[:, ic * 512:(ic + 1) * 512],
                                             start=(ic == 0), stop=(ic == 7))
                                if ic == 7:
                                    m.then_inc(sKd, 1)
                        if ti >= 8:
                            for oc in range(8):
                                qg = (ti - 8) * 8 + oc
                                if qg >= 2:
                                    t.wait_ge(sQEd, qg - 1)
                                for ic in range(8):
                                    m = t.matmul(psQD[qg % 2][0:128, 0:512],
                                                 wqa[:, ic * 1024 + oc * 128: ic * 1024 + (oc + 1) * 128],
                                                 xd[:, ic * 512:(ic + 1) * 512],
                                                 start=(ic == 0), stop=(ic == 7))
                                    if ic == 7:
                                        m.then_inc(sQgd, 1)
                        for ci in range(4):
                            vi = ti * 4 + ci
                            if vi >= 2:
                                t.wait_ge(sVEd, 2 * (vi - 1))
                            for ic in range(8):
                                for n in range(2):
                                    m = t.matmul(
                                        psVD[vi % 2][0:128, n * 512:(n + 1) * 512],
                                        xd[:, ic * 512 + ci * 128: ic * 512 + ci * 128 + 128],
                                        wva[:, ic * 1024 + n * 512: ic * 1024 + (n + 1) * 512],
                                        start=(ic == 0), stop=(ic == 7))
                                    if ic == 7 and n == 1:
                                        m.then_inc(sVd, 1)

                @blockD.scalar
                def _(s):
                    for ti in range(8, NTR):
                        for oc in range(8):
                            qg = (ti - 8) * 8 + oc
                            s.wait_ge(sQgd, qg + 1)
                            if qg >= 8:
                                s.wait_ge(sOQd, 16 * (ti - 8))
                            s.activation(q_st[:, oc * 512:(oc + 1) * 512],
                                         psQD[qg % 2][0:128, 0:512], AF.Copy,
                                         scale=float(1.0 / 32.0)).then_inc(sQEd, 1)

                @blockD.vector
                def _(v):
                    v.wait_ge(S["dX"], 80)
                    for ti in range(NTR):
                        for oc in range(8):
                            kg = ti * 8 + oc
                            v.wait_ge(sKd, kg + 1)
                            if kg >= 8:
                                v.wait_ge(sOKd, 16 * ti)
                            v.tensor_copy(k_st[:, oc * 512:(oc + 1) * 512],
                                          psKD[kg % 2][0:128, 0:512]).then_inc(sKEd, 1)
                        for ci in range(4):
                            vi = ti * 4 + ci
                            v.wait_ge(sVd, vi + 1)
                            v.tensor_scalar(v1_tm[:, vi * 1025 + 1:(vi + 1) * 1025],
                                            psVD[vi % 2][0:128, 0:1024],
                                            gate_f[:, vi:vi + 1], None,
                                            ALU.mult).then_inc(sVEd, 1)
                            v.tensor_copy(v1_tm[:, vi * 1025: vi * 1025 + 1],
                                          gate_bf[:, vi:vi + 1]).then_inc(sVEd, 1)

                @blockD.sync
                def _(sy):
                    for ti in range(NTR):
                        sy.wait_ge(sKEd, 8 * (ti + 1))
                        sy.wait_ge(sOKd, 16 * ti)
                        sy.dma_start(
                            k1_d.ap().rearrange("(c p) t -> p c t", p=128)[:, :, ti * 512:(ti + 1) * 512],
                            k_st.ap().rearrange("p (c t) -> p c t", t=512),
                        ).then_inc(sOKd, 16)
                        if ti >= 8:
                            tq = ti - 8
                            sy.wait_ge(sQEd, 8 * (tq + 1))
                            sy.wait_ge(sOQd, 16 * tq)
                            sy.dma_start(
                                q1_d.ap().rearrange("(c p) t -> p c t", p=128)[:, :, tq * 512:(tq + 1) * 512],
                                q_st.ap().rearrange("p (c t) -> p c t", t=512),
                            ).then_inc(sOQd, 16)

            with contextlib.ExitStack() as stE:
                _f = stE.enter_context
                q_sbE = _f(nc.sbuf_tensor("q_sbE", [128, 8 * 256], BF16))
                k0 = _f(nc.sbuf_tensor("k0E", [128, 8 * 128], BF16))
                k1b = _f(nc.sbuf_tensor("k1E", [128, 8 * 128], BF16))
                pt0 = _f(nc.sbuf_tensor("pt0E", [128, 256], BF16))
                pt1 = _f(nc.sbuf_tensor("pt1E", [128, 256], BF16))
                pt2a = _f(nc.sbuf_tensor("pt2aE", [128, 256], BF16))
                pt2b = _f(nc.sbuf_tensor("pt2bE", [128, 256], BF16))
                m0_sb = _f(nc.sbuf_tensor("m0_sbE", [128, 256], BF16))
                m1_sb = _f(nc.sbuf_tensor("m1_sbE", [128, 256], BF16))
                attn_q = _f(nc.sbuf_tensor("attn_qE", [128, 1024], F32))
                s2_t0 = _f(nc.sbuf_tensor("s2_t0E", [128, 1024], F32))
                s2_t1 = _f(nc.sbuf_tensor("s2_t1E", [128, 1024], F32))
                s3tm = _f(nc.sbuf_tensor("s3tmE", [128, 1024], F32))
                s3f = _f(nc.sbuf_tensor("s3fE", [128, 1024], F32))
                s3f_bf = _f(nc.sbuf_tensor("s3f_bfE", [128, 1024], BF16))
                y4 = _f(nc.sbuf_tensor("y4E", [128, 1024], F32))
                y4tm = _f(nc.sbuf_tensor("y4tmE", [128, 1024], F32))
                o_t0 = _f(nc.sbuf_tensor("o_t0E", [128, 1024], F32))
                o_t1 = _f(nc.sbuf_tensor("o_t1E", [128, 1024], F32))
                tb4 = _f(nc.sbuf_tensor("tb4E", [128, 128], F32))
                y2b = _f(nc.sbuf_tensor("y2bE", [128, 1024], BF16))
                statE = _f(nc.sbuf_tensor("statE", [128, 8], F32))
                rstdE = _f(nc.sbuf_tensor("rstdE", [128, 2], F32))
                nmrE = _f(nc.sbuf_tensor("nmrE", [128, 2], F32))
                rgE = _f(nc.sbuf_tensor("rgE", [128, 1], F32))
                epsE = _f(nc.sbuf_tensor("epsE", [128, 1], F32))
                idE = _f(nc.sbuf_tensor("idE", [128, 128], F32))
                g1_sb = _f(nc.sbuf_tensor("g1_sbE", [32, 2 * 4 * 32], BF16))
                g2_sb = _f(nc.sbuf_tensor("g2_sbE", [16, 2 * 4 * 16], BF16))
                e1_sb = _f(nc.sbuf_tensor("e1_sbE", [32, 512], BF16))
                e2_sb = _f(nc.sbuf_tensor("e2_sbE", [32, 4 * 128], BF16))
                e3_sb = _f(nc.sbuf_tensor("e3_sbE", [16, 512], BF16))
                e4_sb = _f(nc.sbuf_tensor("e4_sbE", [16, 4 * 128], BF16))
                r1_sb = _f(nc.sbuf_tensor("r1_sbE", [32, 512], BF16))
                s1_sb = _f(nc.sbuf_tensor("s1_sbE", [16, 512], BF16))
                wrt_bf = _f(nc.sbuf_tensor("wrt_bfE", [128, 4 * 512], BF16))
                wit_bf = _f(nc.sbuf_tensor("wit_bfE", [128, 4 * 512], BF16))
                witn_bf = _f(nc.sbuf_tensor("witn_bfE", [128, 4 * 512], BF16))
                br_sb = _f(nc.sbuf_tensor("br_sbE", [128, 4], F32))
                bi_sb = _f(nc.sbuf_tensor("bi_sbE", [128, 4], F32))
                psS0e = _f(nc.psum_tensor("psS0e", [128, 512], F32))
                psS1e = _f(nc.psum_tensor("psS1e", [128, 512], F32))
                psP0e = _f(nc.psum_tensor("psP0e", [128, 1536], F32))
                psP1e = _f(nc.psum_tensor("psP1e", [128, 1536], F32))
                blockE = stE.enter_context(nc.Block())
                v_tm = v1_tm
                q1, k1, s2town = q1_d, k1_d, s2t_d
                psS = [psS0e, psS1e]
                psPV = [psP0e, psP1e]
                kb_sb = [k0, k1b]
                ptb = [pt0, pt1]
                pt2 = [pt2a, pt2b]
                s2_t = [s2_t0, s2_t1]
                o_t = [o_t0, o_t1]
                msk = [m0_sb, m1_sb]
                q_sb = q_sbE
                psS0, psS1 = psS0e, psS1e
                stat, rstd, nmr, rg, eps_sb, id_sb = statE, rstdE, nmrE, rgE, epsE, idE
                aex_sb = y2b[:, 0:512]
                bex_sb = y2b[:, 512:1024]
                prod_sb = o_t0[:, 0:512]
                wacc_j = [attn_q[:, 0:512], attn_q[:, 512:1024], s3f[:, 0:512], s3f[:, 512:1024]]
                sLd, sQt, sKt, sS2r = S["eLd"], S["eQt"], S["eKt"], S["eS2r"]
                sS, sP, sMk, sV2 = S["eS"], S["eP"], S["eMk"], S["eV2"]
                sE, sY3, sVr, sLn = S["eE"], S["eY3"], S["eVr"], S["eLn"]
                sNm, sTr, sTe, sCb = S["eNm"], S["eTr"], S["eTe"], S["eCb"]
                sFF, sFE, sY4, sO = S["eFF"], S["eFE"], S["eY4"], S["eO"]
                sDv, sAc, sLg = S["eDv"], S["eAc"], S["eLg"]
                sTT, sTD, sMs = S["eTT"], S["eTD"], S["eMs"]
                @blockE.sync
                def _(sy):
                    sy.wait_ge(S["dOK"], 256)
                    sy.wait_ge(S["dOQ"], 128)
                    sy.wait_ge(S["dVE"], 128)
                    sy.dma_start(id_sb[:, :], ident.ap()).then_inc(sLd, 16)
                    sy.dma_start(m0_sb[:, :], m0.ap()).then_inc(sLd, 16)
                    sy.dma_start(m1_sb[:, :], m1.ap()).then_inc(sLd, 16)
                    sy.dma_start(br_sb[:, :], brc.ap()).then_inc(sLd, 16)
                    sy.dma_start(bi_sb[:, :], bic.ap()).then_inc(sLd, 16)
                    for qt in range(NQT):
                        sy.wait_ge(sQt, 16 * qt)
                        if qt >= 1:
                            sy.wait_ge(sS, CKB[qt])
                        sy.dma_start(
                            q_sb.ap().rearrange("p (c t) -> p c t", t=256),
                            q1.ap().rearrange("(c p) t -> p c t", p=128)[:, :, qt * 256:(qt + 1) * 256],
                        ).then_inc(sQt, 16)
                        if qt >= 1:
                            for qsub in range(2):
                                gi = (qt - 1) * 2 + qsub
                                sy.wait_ge(sNm, 2 * gi + 2)
                                sy.wait_ge(sO, 16 * gi)
                                sy.dma_start(
                                    o_nat.ap().rearrange("(g p) c -> g p c", p=128)[gi],
                                    o_t[gi % 2][:, :]).then_inc(sO, 16)
                    for qsub in range(2):
                        gi = (NQT - 1) * 2 + qsub
                        sy.wait_ge(sNm, 2 * gi + 2)
                        sy.wait_ge(sO, 16 * gi)
                        sy.dma_start(
                            o_nat.ap().rearrange("(g p) c -> g p c", p=128)[gi],
                            o_t[gi % 2][:, :]).then_inc(sO, 16)
                    sy.wait_ge(sO, 16 * 2 * NQT)

                @blockE.gpsimd
                def _(g):
                    g.wait_ge(S["dOK"], 256)
                    g.wait_ge(S["dOQ"], 128)
                    g.wait_ge(S["dVE"], 128)
                    g.dma_start(g1_sb[:, 0:128], g1r.ap()).then_inc(sLg, 16)
                    g.dma_start(g1_sb[:, 128:256], g1i.ap()).then_inc(sLg, 16)
                    g.dma_start(g2_sb[:, 0:64], g2r.ap()).then_inc(sLg, 16)
                    g.dma_start(g2_sb[:, 64:128], g2i.ap()).then_inc(sLg, 16)
                    g.dma_start(e1_sb[:, :], e1m.ap()).then_inc(sLg, 16)
                    g.dma_start(e2_sb[:, :], e2m.ap()).then_inc(sLg, 16)
                    g.dma_start(e3_sb[:, :], e3m.ap()).then_inc(sLg, 16)
                    g.dma_start(e4_sb[:, :], e4m.ap()).then_inc(sLg, 16)
                    nk = 0
                    for qt in range(NQT):
                        for kb in range(KBH[qt]):
                            e = CKB[qt] + kb
                            g.wait_ge(sKt, 16 * e)
                            if e >= 2:
                                g.wait_ge(sS, e - 1)
                            g.dma_start(
                                kb_sb[e % 2].ap().rearrange("p (c t) -> p c t", t=128),
                                k1.ap().rearrange("(c p) t -> p c t", p=128)[:, :, kb * 128:(kb + 1) * 128],
                            ).then_inc(sKt, 16)
                        for qsub in range(2):
                            gi = qt * 2 + qsub
                            g.wait_ge(sS2r, 16 * gi)
                            if gi >= 2:
                                g.wait_ge(sY3, gi - 1)
                            g.dma_start(s2_t[gi % 2][:, :],
                                        s2town.ap().rearrange("(g p) c -> g p c", p=128)[gi]
                                        ).then_inc(sS2r, 16)

                @blockE.tensor
                def _(t):
                    t.wait_ge(S["dOK"], 256)
                    t.wait_ge(S["dOQ"], 128)
                    t.wait_ge(S["dVE"], 128)
                    t.wait_ge(sLd, 80)
                    t.wait_ge(sLg, 128)
                    t.wait_ge(sMs, 1)
                    ntt = 0
                    for part in range(2):
                        for c in range(4):
                            if ntt >= 1:
                                t.wait_ge(sTD, 2 * ntt)
                            t.matmul(psS0[0:32, 0:512], g1_sb[:, part * 128 + c * 32: part * 128 + (c + 1) * 32],
                                     e1_sb[:, :], start=True, stop=True).then_inc(sTT, 1)
                            t.matmul(psS1[0:16, 0:512], g2_sb[:, part * 64 + c * 16: part * 64 + (c + 1) * 16],
                                     e3_sb[:, :], start=True, stop=True).then_inc(sTT, 1)
                            t.wait_ge(sTD, 2 * ntt + 2)
                            for j in range(4):
                                t.matmul(psS0[0:128, 0:512], e2_sb[:, j * 128:(j + 1) * 128],
                                         r1_sb[:, :], start=True, stop=True).then_inc(sTT, 1)
                                t.matmul(psS1[0:128, 0:512], e4_sb[:, j * 128:(j + 1) * 128],
                                         s1_sb[:, :], start=True, stop=True).then_inc(sTT, 1)
                                t.wait_ge(sTD, 2 * ntt + 2 + 2 * (j + 1))
                            ntt += 5
                    for qt in range(NQT):
                        t.wait_ge(sQt, 16 * (qt + 1))
                        if qt >= 1:
                            t.wait_ge(sTe, 32 * qt)
                        for kb in range(KBH[qt]):
                            e = CKB[qt] + kb
                            t.wait_ge(sKt, 16 * (e + 1))
                            if e >= 2:
                                t.wait_ge(sP, e - 1)
                            for ic in range(8):
                                m = t.matmul(psS[e % 2][0:128, 0:256],
                                             kb_sb[e % 2][:, ic * 128:(ic + 1) * 128],
                                             q_sb[:, ic * 256:(ic + 1) * 256],
                                             start=(ic == 0), stop=(ic == 7))
                                if ic == 7:
                                    m.then_inc(sS, 1)
                            crossing = kb >= KBH[qt] - 2
                            if crossing:
                                cidx = 2 * qt + (kb - (KBH[qt] - 2))
                                t.wait_ge(sMk, cidx + 1)
                                src = pt2[cidx % 2]
                            else:
                                t.wait_ge(sP, e + 1)
                                src = ptb[e % 2]
                            for qsub in range(2):
                                if kb == 0:
                                    gi = qt * 2 + qsub
                                    if gi >= 2:
                                        t.wait_ge(sFE, 8 * (gi - 1))
                                for n in range(2):
                                    t.matmul(psPV[qsub][0:128, n * 512:(n + 1) * 512],
                                             src[:, qsub * 128:(qsub + 1) * 128],
                                             v_tm[:, kb * 1025 + 1 + n * 512: kb * 1025 + 1 + (n + 1) * 512],
                                             start=(kb == 0), stop=(kb == KBH[qt] - 1))
                                m = t.matmul(psPV[qsub][0:128, 1024:1025],
                                             src[:, qsub * 128:(qsub + 1) * 128],
                                             v_tm[:, kb * 1025: kb * 1025 + 1],
                                             start=(kb == 0), stop=(kb == KBH[qt] - 1))
                                if qsub == 1:
                                    m.then_inc(sV2, 1)
                        for qsub in range(2):
                            gi = qt * 2 + qsub
                            t.wait_ge(sNm, 2 * gi + 1)
                            for dc in range(8):
                                j = 32 * qt + qsub * 16 + dc
                                if j >= 2:
                                    t.wait_ge(sTe, j - 1)
                                t.transpose(psS[j % 2][0:128, 0:128],
                                            s3tm[:, dc * 128:(dc + 1) * 128],
                                            id_sb[:, :]).then_inc(sTr, 1)
                            t.wait_ge(sCb, gi + 1)
                            for og in range(8):
                                if og >= 2:
                                    t.wait_ge(sFE, 8 * gi + og - 1)
                                reg = psPV[qsub][0:128, (og % 2) * 512:(og % 2) * 512 + 128]
                                for ic in range(8):
                                    if og < 4:
                                        w = wrt_bf if ic < 4 else witn_bf
                                        oc = og
                                    else:
                                        w = wit_bf if ic < 4 else wrt_bf
                                        oc = og - 4
                                    m = t.matmul(reg, w[:, (ic % 4) * 512 + oc * 128: (ic % 4) * 512 + (oc + 1) * 128],
                                                 s3f_bf[:, ic * 128:(ic + 1) * 128],
                                                 start=(ic == 0), stop=(ic == 7))
                                    if ic == 7:
                                        m.then_inc(sFF, 1)
                            t.wait_ge(sY4, 8 * (gi + 1))
                            for dc in range(8):
                                j = 32 * qt + qsub * 16 + 8 + dc
                                if j >= 2:
                                    t.wait_ge(sTe, j - 1)
                                t.transpose(psS[j % 2][0:128, 0:128],
                                            y4[:, dc * 128:(dc + 1) * 128],
                                            id_sb[:, :]).then_inc(sTr, 1)

                @blockE.scalar
                def _(s):
                    s.wait_ge(S["dOK"], 256)
                    s.wait_ge(S["dVE"], 128)
                    for qt in range(NQT):
                        for kb in range(KBH[qt]):
                            e = CKB[qt] + kb
                            s.wait_ge(sS, e + 1)
                            if e >= 2:
                                s.wait_ge(sV2, e - 1)
                            s.activation(ptb[e % 2][:, :], psS[e % 2][0:128, 0:256],
                                         AF.Exp).then_inc(sP, 1)
                        for qsub in range(2):
                            gi = qt * 2 + qsub
                            s.wait_ge(sVr, 4 * gi + 1)
                            s.activation(stat[:, 6:7], stat[:, 4:5], AF.Ln,
                                         bias=eps_sb[:, 0:1]).then_inc(sAc, 1)
                            s.wait_ge(sAc, 2 * gi + 1)
                            s.activation(rstd[:, 0:1], stat[:, 6:7], AF.Exp, scale=-0.5).then_inc(sLn, 1)
                            s.wait_ge(sVr, 4 * gi + 3)
                            s.activation(stat[:, 7:8], stat[:, 5:6], AF.Ln,
                                         bias=eps_sb[:, 0:1]).then_inc(sAc, 1)
                            s.wait_ge(sAc, 2 * gi + 2)
                            s.activation(rstd[:, 1:2], stat[:, 7:8], AF.Exp, scale=-0.5).then_inc(sLn, 1)

                @blockE.vector
                def _(v):
                    v.wait_ge(S["dOK"], 256)
                    v.wait_ge(S["dVE"], 128)
                    v.memset(eps_sb[:, :], EPS).then_inc(sMs, 1)
                    v.wait_ge(sMs, 1)
                    v.wait_ge(sLd, 80)
                    v.wait_ge(sLg, 128)
                    nDv = 0

                    def dv(inst):
                        nonlocal nDv
                        inst.then_inc(sDv, 1)
                        nDv += 1
                        v.wait_ge(sDv, nDv)

                    ntt = 0
                    for part in range(2):
                        wdst = [wrt_bf, wit_bf][part]
                        for c in range(4):
                            v.wait_ge(sTT, 2 * ntt + 2)
                            v.tensor_copy(r1_sb[:, :], psS0[0:32, 0:512]).then_inc(sTD, 1)
                            v.tensor_copy(s1_sb[:, :], psS1[0:16, 0:512]).then_inc(sTD, 1)
                            for j in range(4):
                                v.wait_ge(sTT, 2 * ntt + 2 + 2 * (j + 1))
                                dv(v.tensor_copy(aex_sb, psS0[0:128, 0:512]))
                                dv(v.tensor_copy(bex_sb, psS1[0:128, 0:512]))
                                v.sem_inc(sTD, 2)
                                dv(v.tensor_tensor(prod_sb, aex_sb, bex_sb, ALU.mult))
                                if c == 0:
                                    dv(v.tensor_copy(wacc_j[j], prod_sb))
                                else:
                                    dv(v.tensor_tensor(wacc_j[j], wacc_j[j], prod_sb, ALU.add))
                                if c == 3:
                                    dv(v.tensor_copy(wdst[:, j * 512:(j + 1) * 512], wacc_j[j]))
                            ntt += 5
                    dv(v.tensor_scalar(witn_bf[:, :], wit_bf[:, :], -1.0, None, ALU.mult))

                    for qt in range(NQT):
                        for ci in range(2):
                            kb = KBH[qt] - 2 + ci
                            e = CKB[qt] + kb
                            cidx = 2 * qt + ci
                            v.wait_ge(sP, e + 1)
                            if cidx >= 2:
                                v.wait_ge(sV2, e - 1)
                            v.tensor_tensor(pt2[cidx % 2][:, :], ptb[e % 2][:, :],
                                            msk[ci][:, :], ALU.mult).then_inc(sMk, 1)
                        for qsub in range(2):
                            gi = qt * 2 + qsub
                            v.wait_ge(sV2, CKB[qt + 1])
                            dv(v.reciprocal(rg[:, 0:1], psPV[qsub][0:128, 1024:1025]))
                            if gi >= 1:
                                v.wait_ge(sNm, 2 * gi - 1)
                            v.tensor_scalar(attn_q[:, :], psPV[qsub][0:128, 0:1024],
                                            rg[:, 0:1], None, ALU.mult).then_inc(sE, 1)
                            v.wait_ge(sE, gi + 1)
                            v.wait_ge(sS2r, 16 * (gi + 1))
                            v.tensor_tensor(attn_q[:, :], attn_q[:, :], s2_t[gi % 2][:, :],
                                            ALU.add).then_inc(sY3, 1)
                            v.wait_ge(sY3, gi + 1)
                            dv(v.reduce_sum(stat[:, 0:1], attn_q[:, :], AX.X))
                            dv(v.tensor_tensor(y2b[:, :], attn_q[:, :], attn_q[:, :], ALU.mult))
                            dv(v.reduce_sum(stat[:, 1:2], y2b[:, :], AX.X))
                            dv(v.tensor_scalar(stat[:, 2:3], stat[:, 0:1], 1.0 / D, None, ALU.mult))
                            dv(v.tensor_tensor(stat[:, 3:4], stat[:, 2:3], stat[:, 2:3], ALU.mult))
                            dv(v.tensor_scalar(stat[:, 4:5], stat[:, 1:2], 1.0 / D, None, ALU.mult))
                            v.tensor_tensor(stat[:, 4:5], stat[:, 4:5], stat[:, 3:4],
                                            ALU.subtract).then_inc(sVr, 1)
                            v.wait_ge(sLn, 2 * gi + 1)
                            v.scalar_tensor_tensor(nmr[:, 0:1], stat[:, 2:3], -1.0, rstd[:, 0:1],
                                                   ALU.mult, ALU.mult).then_inc(sVr, 1)
                            v.wait_ge(sVr, 4 * gi + 2)
                            if gi >= 1:
                                v.wait_ge(sTr, 16 * gi - 8)
                            v.tensor_scalar(s3tm[:, :], attn_q[:, :], rstd[:, 0:1], nmr[:, 0:1],
                                            ALU.mult, ALU.add).then_inc(sNm, 1)
                            for dc in range(8):
                                j = 32 * qt + qsub * 16 + dc
                                v.wait_ge(sTr, j + 1)
                                v.tensor_copy(s3f[:, dc * 128:(dc + 1) * 128],
                                              psS[j % 2][0:128, 0:128]).then_inc(sTe, 1)
                                v.wait_ge(sTe, j + 1)
                            v.tensor_copy(s3f_bf[:, :], s3f[:, :]).then_inc(sCb, 1)
                            for og in range(8):
                                v.wait_ge(sFF, 8 * gi + og + 1)
                                b_t = br_sb if og < 4 else bi_sb
                                oc = og % 4
                                ocg = og
                                dv(v.tensor_scalar(tb4[:, :],
                                                   psPV[qsub][0:128, (og % 2) * 512:(og % 2) * 512 + 128],
                                                   b_t[:, oc:oc + 1], None, ALU.add))
                                m = v.tensor_tensor(y4[:, ocg * 128:(ocg + 1) * 128], tb4[:, :],
                                                    s3f[:, ocg * 128:(ocg + 1) * 128], ALU.add)
                                m.then_inc(sFE, 1)
                                v.wait_ge(sFE, 8 * gi + og + 1)
                                v.sem_inc(sY4, 1)
                            for dc in range(8):
                                j = 32 * qt + qsub * 16 + 8 + dc
                                v.wait_ge(sTr, j + 1)
                                v.tensor_copy(y4tm[:, dc * 128:(dc + 1) * 128],
                                              psS[j % 2][0:128, 0:128]).then_inc(sTe, 1)
                                v.wait_ge(sTe, j + 1)
                            dv(v.reduce_sum(stat[:, 0:1], y4tm[:, :], AX.X))
                            dv(v.tensor_tensor(y2b[:, :], y4tm[:, :], y4tm[:, :], ALU.mult))
                            dv(v.reduce_sum(stat[:, 1:2], y2b[:, :], AX.X))
                            dv(v.tensor_scalar(stat[:, 2:3], stat[:, 0:1], 1.0 / D, None, ALU.mult))
                            dv(v.tensor_tensor(stat[:, 3:4], stat[:, 2:3], stat[:, 2:3], ALU.mult))
                            dv(v.tensor_scalar(stat[:, 5:6], stat[:, 1:2], 1.0 / D, None, ALU.mult))
                            v.tensor_tensor(stat[:, 5:6], stat[:, 5:6], stat[:, 3:4],
                                            ALU.subtract).then_inc(sVr, 1)
                            v.wait_ge(sLn, 2 * gi + 2)
                            v.scalar_tensor_tensor(nmr[:, 1:2], stat[:, 2:3], -1.0, rstd[:, 1:2],
                                                   ALU.mult, ALU.mult).then_inc(sVr, 1)
                            v.wait_ge(sVr, 4 * gi + 4)
                            if gi >= 2:
                                v.wait_ge(sO, 16 * (gi - 1))
                            v.tensor_scalar(o_t[gi % 2][:, :], y4tm[:, :], rstd[:, 1:2], nmr[:, 1:2],
                                            ALU.mult, ALU.add).then_inc(sNm, 1)


    return nc


def kernel(**inputs):
    import os
    if os.environ.get("MK_UNFUSED") == "3":
        return kernel_3l(**inputs)
    if os.environ.get("MK_UNFUSED") == "6":
        return kernel_unfused(**inputs)
    f32 = np.float32
    asc = np.ascontiguousarray
    x = np.asarray(inputs["L0_attn_x"], f32)
    ident = np.eye(128, dtype=f32)
    gam = np.full((128, 1), np.asarray(inputs["L0_attn_gamma"], f32).reshape(-1)[0], f32)
    bv_bc = asc(np.asarray(inputs["L0_attn_bv"], f32).reshape(8, 128).T)
    wqT = asc(np.asarray(inputs["L0_attn_Wq"], f32).T)
    wkT = asc(np.asarray(inputs["L0_attn_Wk"], f32).T)
    wvT = asc(np.asarray(inputs["L0_attn_Wv"], f32).T)
    bq = asc(np.asarray(inputs["L0_attn_bq"], f32)[:, None])
    bk = asc(np.asarray(inputs["L0_attn_bk"], f32)[:, None])
    wrT = asc(np.asarray(inputs["L0_ffn_W_r"], f32).T)
    wkT2 = asc(np.asarray(inputs["L0_ffn_W_k"], f32).T)
    wvT2 = asc(np.asarray(inputs["L0_ffn_W_v"], f32).T)
    mur = asc(np.asarray(inputs["L0_ffn_mu_r"], f32).reshape(8, 128).T)
    muk = asc(np.asarray(inputs["L0_ffn_mu_k"], f32).reshape(8, 128).T)
    wq1 = asc(np.asarray(inputs["L1_attn_w_q"], f32))
    wk1 = asc(np.asarray(inputs["L1_attn_w_k"], f32))
    wv1 = asc(np.asarray(inputs["L1_attn_w_v"], f32))
    g1r = asc(np.asarray(inputs["L1_ffn_G1_R"], f32)[0].reshape(32, 128))
    g1i = asc(np.asarray(inputs["L1_ffn_G1_I"], f32)[0].reshape(32, 128))
    g2r = asc(np.asarray(inputs["L1_ffn_G2_R"], f32)[:, :, 0, :].transpose(1, 0, 2).reshape(16, 64))
    g2i = asc(np.asarray(inputs["L1_ffn_G2_I"], f32)[:, :, 0, :].transpose(1, 0, 2).reshape(16, 64))
    brc = asc(np.asarray(inputs["L1_ffn_bias_real"], f32).reshape(4, 128).T)
    bic = asc(np.asarray(inputs["L1_ffn_bias_imag"], f32).reshape(4, 128).T)
    E1, E2, E3, E4, M0, M1 = _tt_consts()

    nc = build_fused()
    in_maps = []
    for c in range(NC):
        b, h = c // 2, c % 2
        own = slice(h * OWN, (h + 1) * OWN)
        oth = slice((1 - h) * OWN, (2 - h) * OWN)
        x_rot = np.concatenate([x[b, oth].T, x[b, own].T], axis=1)
        gate64 = np.ones((128, 64), f32)
        if h == 0:
            gate64[:, 0:32] = 0.0
        gpr = np.full((128, 1), float(h), f32)
        in_maps.append(dict(
            x_fm=asc(x_rot),
            wqT=wqT, wkT=wkT, wvT=wvT, bq=bq, bk=bk, gam=gam, bv_bc=bv_bc,
            ident=ident, wrT=wrT, wkT2=wkT2, wvT2=wvT2, mur=mur, muk=muk,
            wq1=wq1, wk1=wk1, wv1=wv1,
            g1r=g1r, g1i=g1i, g2r=g2r, g2i=g2i,
            e1m=E1, e2m=E2, e3m=E3, e4m=E4, brc=brc, bic=bic, m0=M0, m1=M1,
            gate64=gate64, gprev=gpr, hv=np.array([[h]], np.int32),
        ))
    res = run_bass_kernel_spmd(nc, in_maps, core_ids=list(range(NC)))
    out = np.empty((B, T, D), f32)
    for c in range(NC):
        b, h = c // 2, c % 2
        out[b, h * OWN:(h + 1) * OWN, :] = res.results[c]["o_nat"]
    return out
